# version 23 (final)
# baseline (speedup 1.0000x reference)
import numpy as np
import concourse.bass as bass
import concourse.mybir as mybir
from concourse.bass_utils import run_bass_kernel_spmd
from contextlib import ExitStack

F32 = mybir.dt.float32
BF16 = mybir.dt.bfloat16
ALU = mybir.AluOpType
ACT = mybir.ActivationFunctionType
AX = mybir.AxisListType.X

MAGIC = 12582912.0
NEGINF = -1.0e30
D = 1024
DH = 64
NH = 16
DFF = 4096
DQ = 2816
NSLAB_G = DQ // 128
INV127 = 1.0 / 127.0
EPS = 1e-5


class Ct:
    __slots__ = ("v",)

    def __init__(self):
        self.v = 0

    def inc(self, n=1):
        self.v += n
        return self.v


class DmaQ:

    def __init__(self, sems):
        self.sems = sems
        self.vals = [0] * len(sems)
        self.i = 0

    def dma(self, eng, out, in_):
        r = self.i % len(self.sems)
        self.i += 1
        if self.vals[r] > 0:
            eng.wait_ge(self.sems[r], self.vals[r])
        eng.dma_start(out=out, in_=in_).then_inc(self.sems[r], 16)
        self.vals[r] += 16
        return (self.sems[r], self.vals[r])


def tokwait(eng, tok):
    if tok is None:
        return
    if isinstance(tok, list):
        for tk in tok:
            tokwait(eng, tk)
        return
    eng.wait_ge(tok[0], tok[1])


def build_program(n_tiles=32, seq_tiles=16, cc_cores=8):
    nc = bass.Bass("TRN2", target_bir_lowering=False)
    NT = n_tiles
    NTOK = NT * 128

    x_in = nc.dram_tensor("x", [NTOK, D], F32, kind="ExternalInput").ap()
    wq = nc.dram_tensor("wq", [D, D], F32, kind="ExternalInput").ap()
    wk = nc.dram_tensor("wk", [D, D], F32, kind="ExternalInput").ap()
    wv = nc.dram_tensor("wv", [D, D], F32, kind="ExternalInput").ap()
    wo = nc.dram_tensor("wo", [D, D], F32, kind="ExternalInput").ap()
    wg = nc.dram_tensor("wg", [D, D], F32, kind="ExternalInput").ap()
    w1 = nc.dram_tensor("w1", [D, DFF], F32, kind="ExternalInput").ap()
    w2 = nc.dram_tensor("w2", [DFF, D], F32, kind="ExternalInput").ap()
    bwp = nc.dram_tensor("bwp", [D, DH], F32, kind="ExternalInput").ap()
    cwp = nc.dram_tensor("cwp", [DH, D], F32, kind="ExternalInput").ap()
    ln1g = nc.dram_tensor("ln1g", [128, 8], F32, kind="ExternalInput").ap()
    ln1b = nc.dram_tensor("ln1b", [128, 8], F32, kind="ExternalInput").ap()
    ln2g = nc.dram_tensor("ln2g", [128, 8], F32, kind="ExternalInput").ap()
    ln2b = nc.dram_tensor("ln2b", [128, 8], F32, kind="ExternalInput").ap()
    bqp = nc.dram_tensor("bqp", [128, 8], F32, kind="ExternalInput").ap()
    bkp = nc.dram_tensor("bkp", [128, 8], F32, kind="ExternalInput").ap()
    bvr = nc.dram_tensor("bvr", [1, D], F32, kind="ExternalInput").ap()
    bor = nc.dram_tensor("bor", [1, D], F32, kind="ExternalInput").ap()
    bgr = nc.dram_tensor("bgr", [1, D], F32, kind="ExternalInput").ap()
    b1r = nc.dram_tensor("b1r", [1, DFF], F32, kind="ExternalInput").ap()
    b2r = nc.dram_tensor("b2r", [1, D], F32, kind="ExternalInput").ap()
    a_rep = nc.dram_tensor("a_rep", [DH, 128], F32, kind="ExternalInput").ap()
    wgate = nc.dram_tensor("wgate", [DQ, D], F32, kind="ExternalInput").ap()
    wup = nc.dram_tensor("wup", [DQ, D], F32, kind="ExternalInput").ap()
    wdown = nc.dram_tensor("wdown", [D, DQ], F32, kind="ExternalInput").ap()
    clipx = nc.dram_tensor("clipx", [1, 1], F32, kind="ExternalInput").ap()
    clipd = nc.dram_tensor("clipd", [1, 1], F32, kind="ExternalInput").ap()
    identc = nc.dram_tensor("identc", [128, 128], F32, kind="ExternalInput").ap()
    identcb = nc.dram_tensor("identcb", [128, 128], BF16, kind="ExternalInput").ap()
    maskg = nc.dram_tensor("maskg", [128, 256], F32, kind="ExternalInput").ap()
    maskf = nc.dram_tensor("maskf", [128, 256], F32, kind="ExternalInput").ap()

    out = nc.dram_tensor("out", [NTOK, D], F32, kind="ExternalOutput").ap()

    sp_ao = nc.dram_tensor("sp_ao", [NT, 128, D], F32).ap()
    sp_g = nc.dram_tensor("sp_g", [NT, 128, D], F32).ap()
    sp_h0 = nc.dram_tensor("sp_h0", [NT, 128, D], F32).ap()
    sp_st = nc.dram_tensor("sp_st", [NT, DH, 128], F32).ap()
    sp_x1 = nc.dram_tensor("sp_x1", [NT, 128, D], F32).ap()
    sp_h2hi = nc.dram_tensor("sp_h2hi", [NT, 128, D], BF16).ap()
    sp_h2lo = nc.dram_tensor("sp_h2lo", [NT, 128, D], BF16).ap()
    sp_pt = nc.dram_tensor("sp_pt", [NT, 128, D], F32).ap()
    sp_xm = nc.dram_tensor("sp_xm", [NT, 128, D], F32).ap()
    sp_hp = nc.dram_tensor("sp_hp", [NT, 128, DQ], F32).ap()
    sp_w1h = nc.dram_tensor("sp_w1h", [16, 128, 2048], BF16).ap()
    sp_w1l = nc.dram_tensor("sp_w1l", [16, 128, 2048], BF16).ap()
    sp_w2h = nc.dram_tensor("sp_w2h", [32, 128, D], BF16).ap()
    sp_w2l = nc.dram_tensor("sp_w2l", [32, 128, D], BF16).ap()
    cc1_in = nc.dram_tensor("cc1_in", [1, 1], F32)
    cc1_out = nc.dram_tensor("cc1_out", [1, 1], F32)
    cc2_in = nc.dram_tensor("cc2_in", [1, 1], F32)
    cc2_out = nc.dram_tensor("cc2_out", [1, 1], F32)

    ctx = ExitStack()
    with ctx:
        E = ctx.enter_context

        ldsems = [E(nc.semaphore(f"s_ld{i}")) for i in range(8)]
        gpsems = [E(nc.semaphore(f"s_gq{i}")) for i in range(4)]
        stsems = [E(nc.semaphore(f"s_st{i}")) for i in range(6)]
        s_gp = E(nc.semaphore("s_gp"))
        s_rdy = E(nc.semaphore("s_rdy"))
        s_T = E(nc.semaphore("s_T"))
        s_V = E(nc.semaphore("s_V"))
        s_S = E(nc.semaphore("s_S"))
        s_C = E(nc.semaphore("s_C"))
        ldq, stq = DmaQ(ldsems), DmaQ(stsems)
        gpq = DmaQ(gpsems)
        c_gp = Ct()
        c_T, c_V, c_S, c_C = Ct(), Ct(), Ct(), Ct()
        c_rdy = Ct()

        ident = E(nc.sbuf_tensor("ident", [128, 128], F32))
        identb = E(nc.sbuf_tensor("identb", [128, 128], BF16))
        ones_r = E(nc.sbuf_tensor("ones_r", [1, 128], F32))
        ones_b = E(nc.sbuf_tensor("ones_b", [1, 128], BF16))
        arep_sb = E(nc.sbuf_tensor("arep_sb", [DH, 128], F32))
        lnp = E(nc.sbuf_tensor("lnp", [128, 32], F32))
        bqk = E(nc.sbuf_tensor("bqk", [128, 16], F32))
        clip_sb = E(nc.sbuf_tensor("clip_sb", [1, 2], F32))
        gmax1 = E(nc.sbuf_tensor("gmax1", [1, 1], F32))
        gmax2 = E(nc.sbuf_tensor("gmax2", [1, 1], F32))
        invsx = E(nc.sbuf_tensor("invsx", [128, 1], F32))
        invsd = E(nc.sbuf_tensor("invsd", [128, 1], F32))
        scbc1 = E(nc.sbuf_tensor("scbc1", [128, 1], F32))
        scbc2 = E(nc.sbuf_tensor("scbc2", [128, 1], F32))
        sxg = E(nc.sbuf_tensor("sxg", [1, 1], F32))
        sdg = E(nc.sbuf_tensor("sdg", [1, 1], F32))
        st_sum = E(nc.sbuf_tensor("st_sum", [128, 2], F32))
        st_nmu = E(nc.sbuf_tensor("st_nmu", [128, 2], F32))
        st_var = E(nc.sbuf_tensor("st_var", [128, 2], F32))
        st_rv = E(nc.sbuf_tensor("st_rv", [128, 2], F32))
        st_rstd = E(nc.sbuf_tensor("st_rstd", [128, 2], F32))
        st_nmr = E(nc.sbuf_tensor("st_nmr", [128, 2], F32))
        st_mx = E(nc.sbuf_tensor("st_mx", [128, 2], F32))
        nrs = E(nc.sbuf_tensor("nrs", [128, 1], F32))
        st_mn = E(nc.sbuf_tensor("st_mn", [128, 2], F32))
        rmax1 = E(nc.sbuf_tensor("rmax1", [128, 1], F32))
        rmax2 = E(nc.sbuf_tensor("rmax2", [128, 1], F32))
        at_mx = E(nc.sbuf_tensor("at_mx", [128, 2], F32))
        at_nb = E(nc.sbuf_tensor("at_nb", [128, 8], F32))
        at_sm = E(nc.sbuf_tensor("at_sm", [128, 8], F32))
        at_rs = E(nc.sbuf_tensor("at_rs", [128, 8], F32))
        carry = E(nc.sbuf_tensor("carry", [DH, 1], F32))
        tiny = E(nc.sbuf_tensor("tiny", [1, 8], F32))
        row128 = E(nc.sbuf_tensor("row128", [1, 128], F32))

        pb = [E(nc.psum_tensor(f"pb{i}", [128, 512], F32)) for i in range(8)]

        block = E(nc.Block())

        def barrier(tag):
            tC = c_C.v
            tgp = c_gp.v
            qtoks = [(s, v) for s, v in zip(ldq.sems, ldq.vals) if v > 0] + \
                    [(s, v) for s, v in zip(stq.sems, stq.vals) if v > 0]

            @block.vector
            def _(vector):
                vector.tensor_copy(tiny[0:1, 5:6],
                                   ones_r[0:1, 0:1]).then_inc(s_V, 1)
            c_V.inc()

            @block.scalar
            def _(scalar):
                scalar.copy(tiny[0:1, 6:7], ones_r[0:1, 0:1]).then_inc(s_S, 1)
            c_S.inc()

            bT, bV, bS = c_T.v, c_V.v, c_S.v

            def waits(eng, skip):
                if skip != "T":
                    eng.wait_ge(s_T, bT)
                if skip != "V":
                    eng.wait_ge(s_V, bV)
                if skip != "S":
                    eng.wait_ge(s_S, bS)
                if tC:
                    eng.wait_ge(s_C, tC)
                tokwait(eng, qtoks)
                if tgp:
                    eng.wait_ge(s_gp, tgp)

            @block.tensor
            def _(tensor):
                waits(tensor, "T")

            @block.vector
            def _(vector):
                waits(vector, "V")

            @block.scalar
            def _(scalar):
                waits(scalar, "S")

            @block.sync
            def _(sync):
                waits(sync, None)

            @block.gpsimd
            def _(gpsimd):
                waits(gpsimd, None)

        def ln_emit(src_sb, dst_sb, p):
            pp = slice(p, p + 1)
            I32 = mybir.dt.int32

            @block.vector
            def _(vector):
                vector.drain()
                vector.reduce_sum(st_sum[:, pp], src_sb, axis=AX)
                vector.scalar_tensor_tensor(dst_sb, src_sb, 1.0, src_sb,
                                            ALU.bypass, ALU.mult,
                                            accum_out=st_var[:, pp])
                vector.drain()
                vector.tensor_scalar(st_nmu[:, pp], st_sum[:, pp],
                                     -1.0 / D, None, ALU.mult)
                vector.tensor_scalar(st_rv[:, pp], st_var[:, pp], 1.0 / D, EPS,
                                     ALU.mult, ALU.add)
                vector.drain()
                vector.tensor_mul(nrs[:, 0:1], st_nmu[:, pp], st_nmu[:, pp])
                vector.drain()
                vector.tensor_sub(st_rv[:, pp], st_rv[:, pp], nrs[:, 0:1])
                vector.drain()
                rv_i = st_rv[:, pp].bitcast(I32)
                y_i = st_rstd[:, pp].bitcast(I32)
                vector.tensor_scalar(y_i, rv_i, 1, None, ALU.arith_shift_right)
                vector.drain()
                vector.tensor_scalar(y_i, y_i, 0, None, ALU.bitwise_not)
                vector.drain()
                vector.tensor_scalar(y_i, y_i, 1597463008, None, ALU.add)
                for _i in range(4):
                    vector.drain()
                    vector.tensor_mul(nrs[:, 0:1], st_rstd[:, pp], st_rstd[:, pp])
                    vector.drain()
                    vector.tensor_mul(nrs[:, 0:1], nrs[:, 0:1], st_rv[:, pp])
                    vector.drain()
                    vector.tensor_scalar(nrs[:, 0:1], nrs[:, 0:1], -0.5, 1.5,
                                         ALU.mult, ALU.add)
                    vector.drain()
                    vector.tensor_mul(st_rstd[:, pp], st_rstd[:, pp],
                                      nrs[:, 0:1])
                vector.drain()
                vector.tensor_mul(st_nmr[:, pp], st_nmu[:, pp], st_rstd[:, pp])
                vector.drain()
                vector.tensor_scalar(dst_sb, src_sb, st_rstd[:, pp],
                                     st_nmr[:, pp], ALU.mult,
                                     ALU.add).then_inc(s_V, 1)
            return c_V.inc()

        def tr_group(tensor, bank, srcs, idt=None):
            idt = ident if idt is None else idt
            for i, src in enumerate(srcs):
                tensor.matmul(pb[bank][:, i * 128:(i + 1) * 128],
                              src, idt[:], is_transpose=True,
                              start=(i == 0),
                              stop=(i == len(srcs) - 1)).then_inc(s_T, 1)
                c_T.inc()

        def load_w_chunks(sync, dst_sb, src_ap, nchunk, ck_cols, rows=128):
            return [ldq.dma(sync, dst_sb[:, k * ck_cols:(k + 1) * ck_cols],
                            src_ap[k * rows:(k + 1) * rows, :])
                    for k in range(nchunk)]

        def stage_ready(toks):

            @block.sync
            def _(sync):
                tokwait(sync, toks)
                sync.sem_inc(s_rdy, 1)
            n = c_rdy.inc()

            @block.tensor
            def _(tensor):
                tensor.wait_ge(s_rdy, n)

            @block.vector
            def _(vector):
                vector.wait_ge(s_rdy, n)

            @block.scalar
            def _(scalar):
                scalar.wait_ge(s_rdy, n)

        const_toks = []

        @block.sync
        def _(sync):
            for dst, srcp in ((ident[:], identc), (identb[:], identcb),
                              (arep_sb[:], a_rep), (lnp[:, 0:8], ln1g),
                              (lnp[:, 8:16], ln1b), (lnp[:, 16:24], ln2g),
                              (lnp[:, 24:32], ln2b), (bqk[:, 0:8], bqp),
                              (bqk[:, 8:16], bkp), (clip_sb[:, 0:1], clipx),
                              (clip_sb[:, 1:2], clipd)):
                const_toks.append(ldq.dma(sync, dst, srcp))

        @block.vector
        def _(vector):
            vector.memset(ones_r[:], 1.0).then_inc(s_V, 1)
            vector.memset(ones_b[:], 1.0).then_inc(s_V, 1)
            vector.memset(rmax1[:], 0.0).then_inc(s_V, 1)
            vector.memset(rmax2[:], 0.0).then_inc(s_V, 1)
        c_V.inc(4)
        lab_memset = c_V.v

        s1 = ExitStack()
        wqh = s1.enter_context(nc.sbuf_tensor("wqh", [128, 8 * D], BF16))
        wql = s1.enter_context(nc.sbuf_tensor("wql", [128, 8 * D], BF16))
        wkh = s1.enter_context(nc.sbuf_tensor("wkh", [128, 8 * D], BF16))
        wkl = s1.enter_context(nc.sbuf_tensor("wkl", [128, 8 * D], BF16))
        wvh = s1.enter_context(nc.sbuf_tensor("wvh", [128, 8 * D], BF16))
        wvl = s1.enter_context(nc.sbuf_tensor("wvl", [128, 8 * D], BF16))
        wgh = s1.enter_context(nc.sbuf_tensor("wgh", [128, 8 * D], BF16))
        wgl = s1.enter_context(nc.sbuf_tensor("wgl", [128, 8 * D], BF16))
        bw_sb = s1.enter_context(nc.sbuf_tensor("bw_sb", [128, 8 * DH], F32))
        hth1 = s1.enter_context(nc.sbuf_tensor("h1hi", [128, D], BF16))
        htl1 = s1.enter_context(nc.sbuf_tensor("h1lo", [128, D], BF16))
        brow1h = s1.enter_context(nc.sbuf_tensor("brow1h", [1, D], BF16))
        brow1l = s1.enter_context(nc.sbuf_tensor("brow1l", [1, D], BF16))
        mg_sb = s1.enter_context(nc.sbuf_tensor("mg_sb", [128, 256], F32))
        mf_sb = s1.enter_context(nc.sbuf_tensor("mf_sb", [128, 256], F32))
        x_t = s1.enter_context(nc.sbuf_tensor("x_t", [128, D], F32))
        h0_t = s1.enter_context(nc.sbuf_tensor("h0_t", [128, 2 * D], F32))
        ht_t = s1.enter_context(nc.sbuf_tensor("ht_t", [128, D], F32))
        qt_t = s1.enter_context(nc.sbuf_tensor("qt_t", [128, D], F32))
        kt_t = s1.enter_context(nc.sbuf_tensor("kt_t", [128, 2 * D], F32))
        v_t = s1.enter_context(nc.sbuf_tensor("v_t", [128, 2 * D], F32))
        g_t = s1.enter_context(nc.sbuf_tensor("g_t", [128, D], F32))
        ao_t = s1.enter_context(nc.sbuf_tensor("ao_t", [128, D], F32))
        u_sb = s1.enter_context(nc.sbuf_tensor("u_sb", [DH, 2 * 128], F32))
        stt_sb = s1.enter_context(nc.sbuf_tensor("stt_sb", [DH, 2 * 128], F32))
        pp4 = s1.enter_context(nc.sbuf_tensor("pp4", [128, 1024], F32))
        ptt4 = s1.enter_context(nc.sbuf_tensor("ptt4", [128, 1024], F32))
        qk_tok = s1.enter_context(nc.sbuf_tensor("qk_tok", [128, D], F32))
        hc_t = qk_tok
        bvbc = s1.enter_context(nc.sbuf_tensor("bvbc", [128, D], F32))

        s1_toks = list(const_toks)

        @block.sync
        def _(sync):
            for dst, srcp in ((mg_sb[:], maskg), (mf_sb[:], maskf),
                              (qk_tok[0:1, 0:D], bvr), (kt_t[0:1, 0:D], bgr)):
                s1_toks.append(ldq.dma(sync, dst, srcp))
            s1_toks.extend(load_w_chunks(sync, bw_sb, bwp, 8, DH))
        stage_ready(s1_toks)

        @block.vector
        def _(vector):
            vector.tensor_copy(brow1h[:], kt_t[0:1, 0:D])
            vector.drain()
            vector.tensor_sub(brow1l[:], kt_t[0:1, 0:D],
                              brow1h[:]).then_inc(s_V, 1)
        lab_bg = c_V.inc()

        lab_x_ld = [0] * NT
        lab_h0 = [0] * NT
        lab_hc = [0] * NT
        lab_ht = [0] * NT
        lab_hts = [0] * NT
        lab_tr = [0] * NT
        lab_qkevac = [0] * NT
        lab_qtr = [0] * NT
        lab_ktr = [0] * NT
        lab_vevac = [0] * NT
        lab_gevac = [0] * NT
        lab_gfin = [0] * NT
        NCK = NH // 4
        lab_scrd = [[0] * NCK for _ in range(NT)]
        lab_nb = [[0] * NCK for _ in range(NT)]
        lab_att_av = [[0] * NCK for _ in range(NT)]
        lab_scan = [0] * NT
        lab_qkv_mm = [0] * NT
        lab_att_sc = [[0] * NCK for _ in range(NT)]
        lab_att_pt = [[0] * NCK for _ in range(NT)]
        lab_ptevac = [[0] * NCK for _ in range(NT)]
        lab_exp = [[0] * NCK for _ in range(NT)]
        lab_aoev = [[0] * NCK for _ in range(NT)]
        lab_sp1 = [0] * NT


        def emit_xln(tt):
            pq = tt % 2
            dq = slice(pq * D, (pq + 1) * D)

            @block.sync
            def _(sync, tt=tt):
                if tt >= 1:
                    sync.wait_ge(s_V, lab_h0[tt - 1])
                lab_x_ld[tt] = ldq.dma(sync, x_t[:],
                                       x_in[tt * 128:(tt + 1) * 128, :])

            @block.vector
            def _(vector, tt=tt):
                tokwait(vector, lab_x_ld[tt])
                if tt >= 2:
                    tokwait(vector, lab_sp1[tt - 2])
            lab_h0[tt] = ln_emit(x_t[:], h0_t[:, dq], pq)

            @block.vector
            def _(vector, tt=tt):
                if tt >= 1:
                    vector.wait_ge(s_T, lab_ktr[tt - 1])
            lab_hc[tt] = ln_emit(h0_t[:, dq], hc_t[:], pq)

        def emit_tilehead(tt):
            @block.tensor
            def _(tensor, tt=tt):
                tensor.wait_ge(s_V, lab_hc[tt])
                if tt == 0:
                    tensor.wait_ge(s_V, lab_bvbc)
                if tt >= 1:
                    tensor.wait_ge(s_V, lab_ht[tt - 1])
                    tensor.wait_ge(s_V, lab_scrd[tt - 1][2])
                tr_group(tensor, 0, [hc_t[:, i * 128:(i + 1) * 128]
                                     for i in range(4)])
                tr_group(tensor, 1, [hc_t[:, i * 128:(i + 1) * 128]
                                     for i in range(4, 8)])
            lab_tr[tt] = c_T.v

            @block.vector
            def _(vector, tt=tt):
                vector.wait_ge(s_T, lab_tr[tt])
                if tt >= 1:
                    vector.wait_ge(s_T, lab_qkv_mm[tt - 1])
                for c in range(8):
                    bank, off = (0, c * 128) if c < 4 else (1, (c - 4) * 128)
                    vector.tensor_scalar(ht_t[:, c * 128:(c + 1) * 128],
                                         pb[bank][:, off:off + 128],
                                         lnp[:, c:c + 1], lnp[:, 8 + c:9 + c],
                                         ALU.mult, ALU.add).then_inc(s_V, 1)
                    c_V.inc()
            lab_ht[tt] = c_V.v

            @block.vector
            def _(vector, tt=tt):
                vector.drain()
                vector.tensor_copy(hth1[:], ht_t[:])
                vector.drain()
                vector.tensor_sub(htl1[:], ht_t[:], hth1[:]).then_inc(s_V, 1)
            lab_hts[tt] = c_V.inc()

        @block.tensor
        def _(tensor):
            tensor.wait_ge(s_V, lab_memset)
            for half in range(2):
                tensor.matmul(pb[half][:], ones_r[:],
                              qk_tok[0:1, half * 512:half * 512 + 512],
                              start=True, stop=True).then_inc(s_T, 1)
                c_T.inc()
        lab_bvmm = c_T.v

        @block.vector
        def _(vector):
            vector.wait_ge(s_T, lab_bvmm)
            vector.tensor_copy(bvbc[:, 0:512], pb[0][:])
            vector.tensor_copy(bvbc[:, 512:D], pb[1][:]).then_inc(s_V, 1)
        lab_bvbc = c_V.inc()

        emit_xln(0)
        emit_tilehead(0)

        l_wld1 = [None] * 32
        l_wsp1 = [0] * 32
        for k in range(32):
            wi, kk = divmod(k, 8)
            whbm, hsb, lsb = ((wq, wqh, wql), (wk, wkh, wkl),
                              (wv, wvh, wvl), (wg, wgh, wgl))[wi]
            ssl = slice((k % 2) * D, (k % 2) * D + D)

            @block.sync
            def _(sync, k=k, ssl=ssl, whbm=whbm, kk=kk):
                if k == 0:
                    sync.wait_ge(s_V, lab_bg)
                if k >= 2:
                    sync.wait_ge(s_V, l_wsp1[k - 2])
                l_wld1[k] = ldq.dma(sync, kt_t[:, ssl],
                                    whbm[kk * 128:(kk + 1) * 128, :])

            @block.scalar
            def _(scalar, k=k, ssl=ssl, hsb=hsb, kk=kk):
                tokwait(scalar, [l_wld1[k]])
                scalar.activation(hsb[:, kk * D:(kk + 1) * D], kt_t[:, ssl],
                                  ACT.Copy).then_inc(s_S, 1)
            lab_whi1 = c_S.inc()

            @block.vector
            def _(vector, ssl=ssl, hsb=hsb, lsb=lsb, kk=kk, lab_whi1=lab_whi1):
                vector.wait_ge(s_S, lab_whi1)
                vector.tensor_sub(lsb[:, kk * D:(kk + 1) * D], kt_t[:, ssl],
                                  hsb[:, kk * D:(kk + 1) * D]).then_inc(s_V, 1)
            l_wsp1[k] = c_V.inc()

        for t in range(NT):
            p = t % 2
            first = (t % seq_tiles) == 0
            dslc = slice(p * D, (p + 1) * D)
            uslc = slice(p * 128, (p + 1) * 128)

            if t >= 1:
                emit_tilehead(t)

            @block.tensor
            def _(tensor, t=t, p=p):
                tensor.wait_ge(s_V, lab_hts[t])
                if t == 0:
                    tensor.wait_ge(s_V, l_wsp1[7])
                if t >= 1:
                    tensor.wait_ge(s_V, lab_aoev[t - 1][2])
                for half in range(2):
                    oj = half * 512
                    n = 0
                    for kc in range(8):
                        ksl = slice(kc * 128, (kc + 1) * 128)
                        wsl = slice(kc * D + oj, kc * D + oj + 512)
                        for a_sb, w_sb in ((hth1, wqh), (hth1, wql),
                                           (htl1, wqh)):
                            n += 1
                            mm = tensor.matmul(pb[2 + half][:], a_sb[:, ksl],
                                               w_sb[:, wsl], start=(n == 1),
                                               stop=(n == 24))
                    mm.then_inc(s_T, 1)
                    c_T.inc()
            lab_q_mm = c_T.v

            @block.vector
            def _(vector, t=t, lab_q_mm=lab_q_mm):
                vector.wait_ge(s_T, lab_q_mm)
                if t == 0:
                    vector.wait_ge(s_V, lab_bvbc)
                if t >= 1:
                    vector.wait_ge(s_T, lab_ktr[t - 1])
                vector.tensor_copy(qk_tok[:, 0:512], pb[2][:])
                vector.tensor_copy(qk_tok[:, 512:D], pb[3][:]).then_inc(s_V, 1)
            lab_qev = c_V.inc()

            @block.tensor
            def _(tensor, t=t, lab_qev=lab_qev):
                tensor.wait_ge(s_V, lab_qev)
                tr_group(tensor, 2, [qk_tok[:, i * 128:(i + 1) * 128]
                                     for i in range(4)])
                tr_group(tensor, 3, [qk_tok[:, i * 128:(i + 1) * 128]
                                     for i in range(4, 8)])
            lab_qtr[t] = c_T.v

            @block.vector
            def _(vector, t=t):
                vector.wait_ge(s_T, lab_qtr[t])
                if t >= 1:
                    vector.wait_ge(s_T, lab_att_sc[t - 1][NCK - 1])
                for c in range(8):
                    bank, off = (2, c * 128) if c < 4 else (3, (c - 4) * 128)
                    vector.tensor_scalar(qt_t[:, c * 128:(c + 1) * 128],
                                         pb[bank][:, off:off + 128],
                                         bqk[:, c:c + 1], None,
                                         ALU.add).then_inc(s_V, 1)
                    c_V.inc()
            lab_qtev = c_V.v

            @block.tensor
            def _(tensor, t=t):
                if t == 0:
                    tensor.wait_ge(s_V, l_wsp1[15])
                if t >= 1:
                    tensor.wait_ge(s_V, lab_scrd[t - 1][3])
                    tensor.wait_ge(s_V, lab_ptevac[t - 1][3])
                for half in range(2):
                    oj = half * 512
                    n = 0
                    for kc in range(8):
                        ksl = slice(kc * 128, (kc + 1) * 128)
                        wsl = slice(kc * D + oj, kc * D + oj + 512)
                        for a_sb, w_sb in ((hth1, wkh), (hth1, wkl),
                                           (htl1, wkh)):
                            n += 1
                            mm = tensor.matmul(pb[4 + half][:], a_sb[:, ksl],
                                               w_sb[:, wsl], start=(n == 1),
                                               stop=(n == 24))
                    mm.then_inc(s_T, 1)
                    c_T.inc()
            lab_k_mm = c_T.v

            @block.vector
            def _(vector, t=t, lab_k_mm=lab_k_mm):
                vector.wait_ge(s_T, lab_k_mm)
                vector.wait_ge(s_T, lab_qtr[t])
                vector.tensor_copy(qk_tok[:, 0:512], pb[4][:])
                vector.tensor_copy(qk_tok[:, 512:D],
                                   pb[5][:]).then_inc(s_V, 1)
            lab_kev = c_V.inc()

            @block.tensor
            def _(tensor, t=t, lab_kev=lab_kev):
                tensor.wait_ge(s_V, lab_kev)
                tr_group(tensor, 4, [qk_tok[:, i * 128:(i + 1) * 128]
                                     for i in range(4)])
                tr_group(tensor, 5, [qk_tok[:, i * 128:(i + 1) * 128]
                                     for i in range(4, 8)])
            lab_ktr[t] = c_T.v

            @block.vector
            def _(vector, t=t, p=p):
                vector.wait_ge(s_T, lab_ktr[t])
                for c in range(8):
                    bank, off = (4, c * 128) if c < 4 else (5, (c - 4) * 128)
                    vector.tensor_scalar(kt_t[:, p * D + c * 128: p * D + (c + 1) * 128],
                                         pb[bank][:, off:off + 128],
                                         bqk[:, 8 + c:9 + c], None,
                                         ALU.add).then_inc(s_V, 1)
                    c_V.inc()
            lab_qkevac[t] = c_V.v

            @block.tensor
            def _(tensor, t=t):
                if t == 0:
                    tensor.wait_ge(s_V, l_wsp1[23])
                if t >= 1:
                    tensor.wait_ge(s_V, lab_scrd[t - 1][3])
                    tensor.wait_ge(s_V, lab_aoev[t - 1][3])
                for half in range(2):
                    oj = half * 512
                    n = 0
                    for kc in range(8):
                        ksl = slice(kc * 128, (kc + 1) * 128)
                        wsl = slice(kc * D + oj, kc * D + oj + 512)
                        for a_sb, w_sb in ((hth1, wvh), (hth1, wvl),
                                           (htl1, wvh)):
                            n += 1
                            mm = tensor.matmul(pb[6 + half][:], a_sb[:, ksl],
                                               w_sb[:, wsl], start=(n == 1),
                                               stop=(n == 24))
                    mm.then_inc(s_T, 1)
                    c_T.inc()
            lab_v_mm = c_T.v

            @block.vector
            def _(vector, t=t, p=p, lab_v_mm=lab_v_mm):
                vector.wait_ge(s_T, lab_v_mm)
                if t >= 1:
                    vector.wait_ge(s_T, lab_att_av[t - 1][NCK - 1])
                vector.tensor_tensor(v_t[:, p * D:p * D + 512], pb[6][:],
                                     bvbc[:, 0:512], ALU.add)
                vector.tensor_tensor(v_t[:, p * D + 512:(p + 1) * D], pb[7][:],
                                     bvbc[:, 512:D], ALU.add).then_inc(s_V, 1)
            lab_vevac[t] = c_V.inc()

            @block.tensor
            def _(tensor, t=t, lab_qtev=lab_qtev):
                if t == 0:
                    tensor.wait_ge(s_V, l_wsp1[31])
                tensor.wait_ge(s_V, lab_qtev)
                tensor.wait_ge(s_V, lab_qkevac[t])
                for half in range(2):
                    oj = half * 512
                    for kc in range(8):
                        ksl = slice(kc * 128, (kc + 1) * 128)
                        wsl = slice(kc * D + oj, kc * D + oj + 512)
                        for a_sb, w_sb in ((hth1, wgh), (hth1, wgl),
                                           (htl1, wgh)):
                            tensor.matmul(pb[2 + half][:], a_sb[:, ksl],
                                          w_sb[:, wsl], start=(kc == 0 and
                                                               a_sb is hth1 and
                                                               w_sb is wgh),
                                          stop=False)
                    tensor.matmul(pb[2 + half][:], ones_b[:],
                                  brow1h[0:1, oj:oj + 512],
                                  start=False, stop=False)
                    tensor.matmul(pb[2 + half][:], ones_b[:],
                                  brow1l[0:1, oj:oj + 512],
                                  start=False, stop=True).then_inc(s_T, 1)
                    c_T.inc()
                for kc in range(8):
                    mm = tensor.matmul(pb[4][0:DH, 0:128],
                                       bw_sb[:, kc * DH:(kc + 1) * DH],
                                       ht_t[:, kc * 128:(kc + 1) * 128],
                                       start=(kc == 0), stop=(kc == 7))
                mm.then_inc(s_T, 1)
                c_T.inc()
            lab_g_mm = c_T.v - 1
            lab_u_mm = c_T.v
            lab_qkv_mm[t] = c_T.v

            @block.scalar
            def _(scalar, t=t, lab_g_mm=lab_g_mm):
                scalar.wait_ge(s_T, lab_g_mm)
                if t >= 1:
                    tokwait(scalar, lab_sp1[t - 1])
                scalar.activation(g_t[:, 0:512], pb[2][:],
                                  ACT.Exp, scale=-1.0).then_inc(s_S, 1)
                c_S.inc()
                scalar.activation(g_t[:, 512:D], pb[3][:],
                                  ACT.Exp, scale=-1.0).then_inc(s_S, 1)
                c_S.inc()
            lab_gevac[t] = c_S.v

            @block.vector
            def _(vector, t=t, uslc=uslc, first=first, lab_u_mm=lab_u_mm):
                vector.wait_ge(s_T, lab_u_mm)
                vector.wait_ge(s_S, lab_gevac[t])
                if t >= 2:
                    tokwait(vector, lab_sp1[t - 2])
                vector.drain()
                vector.tensor_scalar(g_t[:], g_t[:], 1.0, None, ALU.add)
                vector.drain()
                vector.reciprocal(g_t[:], g_t[:])
                vector.tensor_copy(u_sb[:, uslc], pb[4][0:DH, 0:128]).then_inc(s_V, 1)
                c_V.inc()
                vector.drain()
                if first:
                    vector.tensor_tensor_scan(
                        stt_sb[:, uslc], arep_sb[:], u_sb[:, uslc], 0.0,
                        ALU.mult, ALU.add)
                else:
                    vector.tensor_tensor_scan(
                        stt_sb[:, uslc], arep_sb[:], u_sb[:, uslc], carry[:, 0:1],
                        ALU.mult, ALU.add)
                vector.drain()
                vector.tensor_copy(carry[:, 0:1],
                                   stt_sb[:, uslc.start + 127:uslc.start + 128]
                                   ).then_inc(s_V, 1)
            lab_scan[t] = c_V.inc()

            def att_sc(c):
                cp = c % 2
                bA, bB = (0, 1) if cp == 0 else (5, 6)

                @block.tensor
                def _(tensor, t=t, p=p, c=c, first=first, bA=bA, bB=bB):
                    tensor.wait_ge(s_V, lab_qkevac[t])
                    if c <= 1:
                        tensor.wait_ge(s_V, lab_vevac[t])
                    if c >= 2:
                        tensor.wait_ge(s_V, lab_scrd[t][c - 2])
                    elif t >= 1:
                        tensor.wait_ge(s_V, lab_scrd[t - 1][c + 2])
                    pk_prev = (t - 1) % 2
                    po = (c % 2) * DH
                    for j in range(4):
                        h = (c // 2) * 8 + 2 * j + (c % 2)
                        hc2 = h // 2
                        bank = bA if j < 2 else bB
                        off = (j % 2) * 256
                        q_ap = qt_t[po:po + DH, hc2 * 128:(hc2 + 1) * 128]
                        kcur = kt_t[po:po + DH,
                                    p * D + hc2 * 128: p * D + (hc2 + 1) * 128]
                        kprev = kcur if first else \
                            kt_t[po:po + DH, pk_prev * D + hc2 * 128:
                                 pk_prev * D + (hc2 + 1) * 128]
                        tensor.matmul(pb[bank][:, off:off + 128], q_ap, kprev,
                                      start=(j % 2 == 0), stop=False
                                      ).then_inc(s_T, 1)
                        c_T.inc()
                        tensor.matmul(pb[bank][:, off + 128:off + 256], q_ap,
                                      kcur, start=False, stop=(j % 2 == 1)
                                      ).then_inc(s_T, 1)
                        c_T.inc()
                lab_att_sc[t][c] = c_T.v

            def att_maskred(c):
                cp = c % 2
                bA, bB = (0, 1) if cp == 0 else (5, 6)

                @block.vector
                def _(vector, t=t, c=c, first=first, bA=bA, bB=bB):
                    vector.wait_ge(s_T, lab_att_sc[t][c])
                    if c >= 1:
                        vector.wait_ge(s_T, lab_att_pt[t][c - 1])
                    elif t >= 1:
                        vector.wait_ge(s_T, lab_att_pt[t - 1][NCK - 1])
                    mask = mf_sb if first else mg_sb
                    for j in range(4):
                        bank = bA if j < 2 else bB
                        off = (j % 2) * 256
                        vector.scalar_tensor_tensor(
                            pp4[:, j * 256:(j + 1) * 256],
                            pb[bank][:, off:off + 256], 0.125, mask[:],
                            ALU.mult, ALU.add).then_inc(s_V, 1)
                        c_V.inc()
                    lab_scrd[t][c] = c_V.v
                    vector.drain()
                    for j in range(4):
                        jj = (c % 2) * 4 + j
                        vector.reduce_max(at_nb[:, jj:jj + 1],
                                          pp4[:, j * 256:(j + 1) * 256],
                                          axis=AX, negate=True).then_inc(s_V, 1)
                        c_V.inc()
                    lab_nb[t][c] = c_V.v

            def att_exp(c):
                @block.scalar
                def _(scalar, t=t, c=c):
                    scalar.wait_ge(s_V, lab_nb[t][c])
                    for j in range(4):
                        jj = (c % 2) * 4 + j
                        scalar.activation(pp4[:, j * 256:(j + 1) * 256],
                                          pp4[:, j * 256:(j + 1) * 256],
                                          ACT.Exp, bias=at_nb[:, jj:jj + 1],
                                          scale=1.0,
                                          accum_out=at_sm[:, jj:jj + 1]
                                          ).then_inc(s_S, 1)
                        c_S.inc()
                lab_exp[t][c] = c_S.v

            def att_pt(c):
                cp = c % 2
                bP0, bP1 = (2, 3) if cp == 0 else (7, 4)

                @block.tensor
                def _(tensor, t=t, c=c, bP0=bP0, bP1=bP1):
                    tensor.wait_ge(s_S, lab_exp[t][c])
                    if c >= 2:
                        tensor.wait_ge(s_V, lab_aoev[t][c - 2])
                    elif t >= 1:
                        tensor.wait_ge(s_V, lab_aoev[t - 1][c + 2])
                    tr_group(tensor, bP0, [pp4[:, i * 128:(i + 1) * 128]
                                           for i in range(4)])
                    tr_group(tensor, bP1, [pp4[:, i * 128:(i + 1) * 128]
                                           for i in range(4, 8)])
                lab_att_pt[t][c] = c_T.v

            def att_ptevac(c):
                cp = c % 2
                bP0, bP1 = (2, 3) if cp == 0 else (7, 4)

                @block.vector
                def _(vector, t=t, c=c, cp=cp, bP0=bP0, bP1=bP1):
                    vector.wait_ge(s_T, lab_att_pt[t][c])
                    vector.tensor_copy(ptt4[:, 0:512], pb[bP0][:])
                    vector.tensor_copy(ptt4[:, 512:1024], pb[bP1][:])
                    vector.drain()
                    for j in range(4):
                        jj = cp * 4 + j
                        vector.reciprocal(at_rs[:, jj:jj + 1],
                                          at_sm[:, jj:jj + 1]).then_inc(s_V, 1)
                        c_V.inc()
                    lab_ptevac[t][c] = c_V.v

            def att_av(c):
                cp = c % 2
                bP0 = 2 if cp == 0 else 7

                @block.tensor
                def _(tensor, t=t, p=p, c=c, first=first, bP0=bP0):
                    tensor.wait_ge(s_V, lab_ptevac[t][c])
                    pk_prev = (t - 1) % 2
                    for j in range(4):
                        h = (c // 2) * 8 + 2 * j + (c % 2)
                        vcur = v_t[:, p * D + h * DH: p * D + (h + 1) * DH]
                        vprev = vcur if first else \
                            v_t[:, pk_prev * D + h * DH:
                                pk_prev * D + (h + 1) * DH]
                        tensor.matmul(pb[bP0][:, j * DH:(j + 1) * DH],
                                      ptt4[:, j * 256:j * 256 + 128], vprev,
                                      start=(j == 0), stop=False)
                        tensor.matmul(pb[bP0][:, j * DH:(j + 1) * DH],
                                      ptt4[:, j * 256 + 128:(j + 1) * 256],
                                      vcur, start=False,
                                      stop=(j == 3)).then_inc(s_T, 1)
                        c_T.inc()
                lab_att_av[t][c] = c_T.v

            def att_aoev(c):
                cp = c % 2
                bP0 = 2 if cp == 0 else 7

                @block.vector
                def _(vector, t=t, c=c, cp=cp, bP0=bP0):
                    vector.wait_ge(s_T, lab_att_av[t][c])
                    if c == 0 and t >= 1:
                        tokwait(vector, lab_sp1[t - 1])
                    for j in range(4):
                        h = (c // 2) * 8 + 2 * j + (c % 2)
                        jj = cp * 4 + j
                        vector.tensor_scalar(ao_t[:, h * DH:(h + 1) * DH],
                                             pb[bP0][:, j * DH:(j + 1) * DH],
                                             at_rs[:, jj:jj + 1], None,
                                             ALU.mult).then_inc(s_V, 1)
                        c_V.inc()
                    lab_aoev[t][c] = c_V.v

            for c in range(NCK):
                att_sc(c)
                if c >= 2:
                    att_av(c - 2)
                if c >= 1:
                    att_pt(c - 1)
                att_maskred(c)
                if c >= 1:
                    att_ptevac(c - 1)
                if c >= 2:
                    att_aoev(c - 2)
                att_exp(c)
                if c == 2 and t + 1 < NT:
                    emit_xln(t + 1)
            att_av(NCK - 2)
            att_pt(NCK - 1)
            att_ptevac(NCK - 1)
            att_aoev(NCK - 2)
            att_av(NCK - 1)
            att_aoev(NCK - 1)

            @block.scalar
            def _(scalar, t=t, p=p, dslc=dslc, uslc=uslc):
                scalar.wait_ge(s_V, lab_aoev[t][NCK - 1])
                scalar.wait_ge(s_V, lab_scan[t])
                scalar.wait_ge(s_S, lab_gevac[t])
                lab_sp1[t] = [
                    stq.dma(scalar, sp_h0[t], h0_t[:, dslc]),
                    stq.dma(scalar, sp_g[t], g_t[:]),
                    stq.dma(scalar, sp_ao[t], ao_t[:]),
                    stq.dma(scalar, sp_st[t], stt_sb[:, uslc]),
                ]

        barrier("S1end")
        s1.close()

        s2 = ExitStack()
        woh = s2.enter_context(nc.sbuf_tensor("woh", [128, 8 * D], BF16))
        wol = s2.enter_context(nc.sbuf_tensor("wol", [128, 8 * D], BF16))
        cw_sb = s2.enter_context(nc.sbuf_tensor("cw_sb", [DH, D], F32))
        brow2 = s2.enter_context(nc.sbuf_tensor("brow2", [1, D], F32))
        ao2_t = s2.enter_context(nc.sbuf_tensor("ao2_t", [128, 2 * D], F32))
        g2_t = s2.enter_context(nc.sbuf_tensor("g2_t", [128, 2 * D], F32))
        h02_t = s2.enter_context(nc.sbuf_tensor("h02_t", [128, 2 * D], F32))
        st2_t = s2.enter_context(nc.sbuf_tensor("st2_t", [DH, 2 * 128], F32))
        aot_t = s2.enter_context(nc.sbuf_tensor("aot_t", [128, D], F32))
        aoth = s2.enter_context(nc.sbuf_tensor("aoth", [128, D], BF16))
        aotl = s2.enter_context(nc.sbuf_tensor("aotl", [128, D], BF16))
        at2_t = s2.enter_context(nc.sbuf_tensor("at2_t", [128, 2 * D], F32))
        ssm2_t = s2.enter_context(nc.sbuf_tensor("ssm2_t", [128, 2 * D], F32))
        x1_t = s2.enter_context(nc.sbuf_tensor("x1_t", [128, 2 * D], F32))
        bobc = s2.enter_context(nc.sbuf_tensor("bobc", [128, D], F32))
        h2c_t = s2.enter_context(nc.sbuf_tensor("h2c_t", [128, D], F32))
        h2t_t = s2.enter_context(nc.sbuf_tensor("h2t_t", [128, 2 * D], F32))
        h2hi2 = s2.enter_context(nc.sbuf_tensor("h2hi2", [128, 2 * D], BF16))
        h2lo2 = s2.enter_context(nc.sbuf_tensor("h2lo2", [128, 2 * D], BF16))

        s2_toks = []

        @block.sync
        def _(sync):
            s2_toks.append(ldq.dma(sync, cw_sb[:], cwp))
            s2_toks.append(ldq.dma(sync, brow2[:], bor))
        stage_ready(s2_toks)

        l_wld2 = [None] * 8
        l_wsp2 = [0] * 8
        for k in range(8):
            ssl = slice((k % 2) * D, (k % 2) * D + D)

            @block.sync
            def _(sync, k=k, ssl=ssl):
                if k >= 2:
                    sync.wait_ge(s_V, l_wsp2[k - 2])
                l_wld2[k] = ldq.dma(sync, at2_t[:, ssl],
                                    wo[k * 128:(k + 1) * 128, :])

            @block.scalar
            def _(scalar, k=k, ssl=ssl):
                tokwait(scalar, [l_wld2[k]])
                scalar.activation(woh[:, k * D:(k + 1) * D], at2_t[:, ssl],
                                  ACT.Copy).then_inc(s_S, 1)
            lab_whi2 = c_S.inc()

            @block.vector
            def _(vector, k=k, ssl=ssl, lab_whi2=lab_whi2):
                vector.wait_ge(s_S, lab_whi2)
                vector.tensor_sub(wol[:, k * D:(k + 1) * D], at2_t[:, ssl],
                                  woh[:, k * D:(k + 1) * D]).then_inc(s_V, 1)
            l_wsp2[k] = c_V.inc()

        @block.tensor
        def _(tensor):
            for half in range(2):
                tensor.matmul(pb[half][:], ones_r[:],
                              brow2[0:1, half * 512:half * 512 + 512],
                              start=True, stop=True).then_inc(s_T, 1)
                c_T.inc()
        lab_bomm = c_T.v

        @block.vector
        def _(vector):
            vector.wait_ge(s_T, lab_bomm)
            vector.tensor_copy(bobc[:, 0:512], pb[0][:])
            vector.tensor_copy(bobc[:, 512:D], pb[1][:]).then_inc(s_V, 1)
        lab_bobc = c_V.inc()

        w_stg = s2.enter_context(nc.sbuf_tensor("w_stg", [128, 2 * 2048], F32))
        whi_stg = s2.enter_context(nc.sbuf_tensor("whi_stg", [128, 2 * 2048], BF16))
        wlo_stg = s2.enter_context(nc.sbuf_tensor("wlo_stg", [128, 2 * 2048], BF16))
        l3ld = [None] * 48
        l3hi = [0] * 48
        l3lo = [0] * 48
        l3st = [None] * 48

        def emit_wslab(k):
            kp = k % 2
            if k < 16:
                halfw, kk = divmod(k, 8)
                wsrc = w1[kk * 128:(kk + 1) * 128,
                          halfw * 2048:(halfw + 1) * 2048]
                cols = 2048
                dsth, dstl = sp_w1h[k], sp_w1l[k]
            else:
                halfw, kk = divmod(k - 16, 16)
                wsrc = w2[halfw * 2048 + kk * 128:
                          halfw * 2048 + (kk + 1) * 128, :]
                cols = 1024
                dsth, dstl = sp_w2h[k - 16], sp_w2l[k - 16]
            ssl = slice(kp * 2048, kp * 2048 + cols)

            @block.gpsimd
            def _(gpsimd, k=k, ssl=ssl, wsrc=wsrc):
                if k >= 2:
                    gpsimd.wait_ge(s_V, l3lo[k - 2])
                l3ld[k] = gpq.dma(gpsimd, w_stg[:, ssl], wsrc)

            @block.scalar
            def _(scalar, k=k, ssl=ssl):
                tokwait(scalar, [l3ld[k]])
                if k >= 2:
                    tokwait(scalar, l3st[k - 2])
                scalar.activation(whi_stg[:, ssl], w_stg[:, ssl],
                                  ACT.Copy).then_inc(s_S, 1)
            l3hi[k] = c_S.inc()

            @block.vector
            def _(vector, k=k, ssl=ssl):
                vector.wait_ge(s_S, l3hi[k])
                vector.tensor_sub(wlo_stg[:, ssl], w_stg[:, ssl],
                                  whi_stg[:, ssl]).then_inc(s_V, 1)
            l3lo[k] = c_V.inc()

            @block.scalar
            def _(scalar, k=k, ssl=ssl, dsth=dsth, dstl=dstl):
                scalar.wait_ge(s_V, l3lo[k])
                l3st[k] = [stq.dma(scalar, dsth, whi_stg[:, ssl]),
                           stq.dma(scalar, dstl, wlo_stg[:, ssl])]

        lab2_ld = [0] * NT
        lab2_aot = [0] * NT
        lab2_aos = [0] * NT
        lab2_tr = [0] * NT
        lab2_mm = [0] * NT
        lab2_att = [0] * NT
        lab2_x1 = [0] * NT
        lab2_h2c = [0] * NT
        lab2_h2tr = [0] * NT
        lab2_h2t = [0] * NT
        lab2_h2s = [0] * NT
        lab2_sp = [0] * NT

        for t in range(NT):
            p = t % 2
            dslc = slice(p * D, (p + 1) * D)
            uslc = slice(p * 128, (p + 1) * 128)

            @block.sync
            def _(sync, t=t, dslc=dslc, uslc=uslc):
                if t >= 2:
                    sync.wait_ge(s_T, lab2_tr[t - 2])
                    sync.wait_ge(s_V, lab2_x1[t - 2])
                lab2_ld[t] = [
                    ldq.dma(sync, ao2_t[:, dslc], sp_ao[t]),
                    ldq.dma(sync, g2_t[:, dslc], sp_g[t]),
                    ldq.dma(sync, h02_t[:, dslc], sp_h0[t]),
                    ldq.dma(sync, st2_t[:, uslc], sp_st[t]),
                ]

            for kslab in (2 * t, 2 * t + 1):
                if kslab < 48:
                    emit_wslab(kslab)

            @block.tensor
            def _(tensor, t=t, p=p):
                tokwait(tensor, lab2_ld[t])
                if t == 0:
                    tensor.wait_ge(s_V, lab_bobc)
                if t >= 1:
                    tensor.wait_ge(s_V, lab2_aos[t - 1])
                tr_group(tensor, 0, [ao2_t[:, p * D + i * 128: p * D + (i + 1) * 128]
                                     for i in range(4)])
                tr_group(tensor, 1, [ao2_t[:, p * D + i * 128: p * D + (i + 1) * 128]
                                     for i in range(4, 8)])
            lab2_tr[t] = c_T.v

            @block.scalar
            def _(scalar, t=t):
                scalar.wait_ge(s_T, lab2_tr[t])
                if t >= 1:
                    scalar.wait_ge(s_T, lab2_mm[t - 1])
                for c in range(8):
                    bank, off = (0, c * 128) if c < 4 else (1, (c - 4) * 128)
                    scalar.activation(aot_t[:, c * 128:(c + 1) * 128],
                                      pb[bank][:, off:off + 128],
                                      ACT.Copy).then_inc(s_S, 1)
                    c_S.inc()
            lab2_aot[t] = c_S.v

            @block.vector
            def _(vector, t=t):
                vector.wait_ge(s_S, lab2_aot[t])
                vector.tensor_copy(aoth[:], aot_t[:])
                vector.drain()
                vector.tensor_sub(aotl[:], aot_t[:],
                                  aoth[:]).then_inc(s_V, 1)
            lab2_aos[t] = c_V.inc()

            @block.tensor
            def _(tensor, t=t, uslc=uslc):
                tensor.wait_ge(s_V, lab2_aos[t])
                if t >= 1:
                    tensor.wait_ge(s_V, lab2_att[t - 1])
                for half in range(2):
                    oj = half * 512
                    n = 0
                    for kc in range(8):
                        ksl = slice(kc * 128, (kc + 1) * 128)
                        wsl = slice(kc * D + oj, kc * D + oj + 512)
                        for a_sb, w_sb in ((aoth, woh), (aoth, wol),
                                           (aotl, woh)):
                            n += 1
                            mm = tensor.matmul(pb[2 + half][:], a_sb[:, ksl],
                                               w_sb[:, wsl], start=(n == 1),
                                               stop=(n == 24))
                    mm.then_inc(s_T, 1)
                    c_T.inc()
                for half in range(2):
                    tensor.matmul(pb[4 + half][:], st2_t[:, uslc],
                                  cw_sb[:, half * 512:(half + 1) * 512],
                                  start=True, stop=True).then_inc(s_T, 1)
                    c_T.inc()
            lab2_mm[t] = c_T.v

            @block.vector
            def _(vector, t=t, p=p):
                vector.wait_ge(s_T, lab2_mm[t])
                if t >= 2:
                    tokwait(vector, lab2_sp[t - 2])
                vector.tensor_tensor(at2_t[:, p * D:p * D + 512], pb[2][:],
                                     bobc[:, 0:512], ALU.add)
                vector.tensor_tensor(at2_t[:, p * D + 512:(p + 1) * D], pb[3][:],
                                     bobc[:, 512:D], ALU.add)
                vector.tensor_copy(ssm2_t[:, p * D:p * D + 512], pb[4][:])
                vector.tensor_copy(ssm2_t[:, p * D + 512:(p + 1) * D],
                                   pb[5][:]).then_inc(s_V, 1)
            lab2_att[t] = c_V.inc()

            @block.vector
            def _(vector, t=t, dslc=dslc):
                vector.drain()
                vector.tensor_mul(at2_t[:, dslc], g2_t[:, dslc], at2_t[:, dslc])
                vector.drain()
                vector.tensor_add(x1_t[:, dslc], h02_t[:, dslc], at2_t[:, dslc])
                vector.drain()
                vector.tensor_add(x1_t[:, dslc], x1_t[:, dslc], ssm2_t[:, dslc])
                vector.drain()
                vector.tensor_mul(ssm2_t[:, dslc], g2_t[:, dslc], ssm2_t[:, dslc])
                vector.drain()
                vector.tensor_sub(x1_t[:, dslc], x1_t[:, dslc],
                                  ssm2_t[:, dslc]).then_inc(s_V, 1)
            lab2_x1[t] = c_V.inc()

            @block.vector
            def _(vector, t=t):
                if t >= 1:
                    vector.wait_ge(s_T, lab2_h2tr[t - 1])
            lab2_h2c[t] = ln_emit(x1_t[:, dslc], h2c_t[:], p)

            @block.tensor
            def _(tensor, t=t):
                tensor.wait_ge(s_V, lab2_h2c[t])
                if t >= 1:
                    tensor.wait_ge(s_V, lab2_h2t[t - 1])
                tr_group(tensor, 6, [h2c_t[:, i * 128:(i + 1) * 128]
                                     for i in range(4)])
                tr_group(tensor, 7, [h2c_t[:, i * 128:(i + 1) * 128]
                                     for i in range(4, 8)])
            lab2_h2tr[t] = c_T.v

            @block.vector
            def _(vector, t=t, p=p):
                vector.wait_ge(s_T, lab2_h2tr[t])
                if t >= 2:
                    tokwait(vector, lab2_sp[t - 2])
                for c in range(8):
                    bank, off = (6, c * 128) if c < 4 else (7, (c - 4) * 128)
                    vector.tensor_scalar(h2t_t[:, p * D + c * 128: p * D + (c + 1) * 128],
                                         pb[bank][:, off:off + 128],
                                         lnp[:, 16 + c:17 + c],
                                         lnp[:, 24 + c:25 + c],
                                         ALU.mult, ALU.add).then_inc(s_V, 1)
                    c_V.inc()
            lab2_h2t[t] = c_V.v

            @block.vector
            def _(vector, t=t, dslc=dslc):
                vector.drain()
                vector.tensor_copy(h2hi2[:, dslc], h2t_t[:, dslc])
                vector.drain()
                vector.tensor_sub(h2lo2[:, dslc], h2t_t[:, dslc],
                                  h2hi2[:, dslc]).then_inc(s_V, 1)
            lab2_h2s[t] = c_V.inc()

            @block.scalar
            def _(scalar, t=t, dslc=dslc):
                scalar.wait_ge(s_V, lab2_h2s[t])
                lab2_sp[t] = [
                    stq.dma(scalar, sp_x1[t], x1_t[:, dslc]),
                    stq.dma(scalar, sp_h2hi[t], h2hi2[:, dslc]),
                    stq.dma(scalar, sp_h2lo[t], h2lo2[:, dslc]),
                ]

        for kslab in range(2 * NT, 48):
            emit_wslab(kslab)

        barrier("S2end")
        s2.close()

        def mlp_half(half_idx):
            hx = str(half_idx)
            jbase = half_idx * 2048
            s3 = ExitStack()
            w1h = s3.enter_context(nc.sbuf_tensor("w1h" + hx, [128, 8 * 2048], BF16))
            w1l = s3.enter_context(nc.sbuf_tensor("w1l" + hx, [128, 8 * 2048], BF16))
            w2h = s3.enter_context(nc.sbuf_tensor("w2h" + hx, [128, 16 * D], BF16))
            w2l = s3.enter_context(nc.sbuf_tensor("w2l" + hx, [128, 16 * D], BF16))
            b1bc = s3.enter_context(nc.sbuf_tensor("b1bc" + hx, [128, 2048], F32))
            h2h3 = s3.enter_context(nc.sbuf_tensor("h2h3" + hx, [128, 2 * D], BF16))
            h2l3 = s3.enter_context(nc.sbuf_tensor("h2l3" + hx, [128, 2 * D], BF16))
            hid_t = s3.enter_context(nc.sbuf_tensor("hid_t" + hx, [128, 2048], F32))
            hth = s3.enter_context(nc.sbuf_tensor("hth" + hx, [128, 2048], BF16))
            htl = s3.enter_context(nc.sbuf_tensor("htl" + hx, [128, 2048], BF16))
            pt3_t = s3.enter_context(nc.sbuf_tensor("pt3_t" + hx, [128, 2 * D], F32))
            if half_idx == 1:
                b2bc = s3.enter_context(nc.sbuf_tensor("b2bc", [128, D], F32))
                x1b_t = s3.enter_context(nc.sbuf_tensor("x1b_t", [128, 2 * D], F32))
                xb_t = s3.enter_context(nc.sbuf_tensor("xb_t", [128, 2 * D], F32))
                xm_t = s3.enter_context(nc.sbuf_tensor("xm_t", [128, D], F32))

            tok_b = []

            @block.sync
            def _(sync):
                tok_b.append(ldq.dma(sync, pt3_t[0:1, 0:2048],
                                     b1r[0:1, jbase:jbase + 2048]))
                if half_idx == 1:
                    tok_b.append(ldq.dma(sync, hid_t[0:1, 0:D], b2r))

            @block.tensor
            def _(tensor):
                tokwait(tensor, tok_b)
                for j in range(4):
                    tensor.matmul(pb[j][:], ones_r[:],
                                  pt3_t[0:1, j * 512:(j + 1) * 512],
                                  start=True, stop=True).then_inc(s_T, 1)
                    c_T.inc()
                if half_idx == 1:
                    for half in range(2):
                        tensor.matmul(pb[4 + half][:], ones_r[:],
                                      hid_t[0:1, half * 512:half * 512 + 512],
                                      start=True, stop=True).then_inc(s_T, 1)
                        c_T.inc()
            bmm = c_T.v

            @block.vector
            def _(vector, bmm=bmm):
                vector.wait_ge(s_T, bmm)
                for j in range(4):
                    vector.tensor_copy(b1bc[:, j * 512:(j + 1) * 512], pb[j][:])
                if half_idx == 1:
                    vector.tensor_copy(b2bc[:, 0:512], pb[4][:])
                    vector.tensor_copy(b2bc[:, 512:D], pb[5][:])
                vector.engine_nop().then_inc(s_V, 1)
            lab_bbc = c_V.inc()

            w1toks = []
            w2toks = []

            @block.sync
            def _(sync):
                for k in range(8):
                    w1toks.append(ldq.dma(sync, w1h[:, k * 2048:(k + 1) * 2048],
                                          sp_w1h[half_idx * 8 + k]))
                    w1toks.append(ldq.dma(sync, w1l[:, k * 2048:(k + 1) * 2048],
                                          sp_w1l[half_idx * 8 + k]))
                for k in range(16):
                    w2toks.append(ldq.dma(sync, w2h[:, k * D:(k + 1) * D],
                                          sp_w2h[half_idx * 16 + k]))
                    w2toks.append(ldq.dma(sync, w2l[:, k * D:(k + 1) * D],
                                          sp_w2l[half_idx * 16 + k]))

            l_ld = [0] * NT
            l_hid_mm = [0] * NT
            l_pre = [0] * NT
            l_gelu = [0] * NT
            l_htr = [0] * NT
            l_hte = [0] * NT
            l_w2 = [0] * NT
            l_ev = [0] * NT
            l_sp = [0] * NT
            l_stats = [0] * NT

            def emit_ld3(tt):
                pq = tt % 2
                dq = slice(pq * D, (pq + 1) * D)

                @block.sync
                def _(sync, tt=tt, dq=dq):
                    if tt >= 2:
                        sync.wait_ge(s_T, l_hid_mm[tt - 2])
                        if half_idx == 1:
                            sync.wait_ge(s_V, l_ev[tt - 2])
                            sync.wait_ge(s_V, l_stats[tt - 2])
                    toks = [ldq.dma(sync, h2h3[:, dq], sp_h2hi[tt]),
                            ldq.dma(sync, h2l3[:, dq], sp_h2lo[tt])]
                    if half_idx == 1:
                        toks.append(ldq.dma(sync, x1b_t[:, dq], sp_x1[tt]))
                        toks.append(ldq.dma(sync, xb_t[:, dq],
                                            x_in[tt * 128:(tt + 1) * 128, :]))
                        if tt <= 1:
                            sync.wait_ge(s_T, bmm)
                        toks.append(ldq.dma(sync, pt3_t[:, dq], sp_pt[tt]))
                    l_ld[tt] = toks

            def hid_mms(tensor, tt, js):
                pq = tt % 2
                for j in js:
                    n = 0
                    for kc in range(8):
                        asl = slice(pq * D + kc * 128, pq * D + (kc + 1) * 128)
                        wsl = slice(kc * 2048 + j * 512,
                                    kc * 2048 + (j + 1) * 512)
                        for a_sb, w_sb in ((h2h3, w1h), (h2h3, w1l),
                                           (h2l3, w1h)):
                            n += 1
                            mm = tensor.matmul(pb[j][:], a_sb[:, asl],
                                               w_sb[:, wsl],
                                               start=(n == 1),
                                               stop=(n == 24))
                    mm.then_inc(s_T, 1)
                    c_T.inc()

            def emit_hid01(tt):
                @block.tensor
                def _(tensor, tt=tt):
                    tokwait(tensor, l_ld[tt])
                    if tt == 0:
                        tensor.wait_ge(s_V, lab_bbc)
                        tokwait(tensor, w1toks)
                    else:
                        tensor.wait_ge(s_V, l_pre[tt - 1])
                    hid_mms(tensor, tt, (0, 1))

            emit_ld3(0)
            emit_hid01(0)

            for t in range(NT):
                p = t % 2
                dslc = slice(p * D, (p + 1) * D)

                @block.tensor
                def _(tensor, t=t):
                    if t >= 1:
                        tensor.wait_ge(s_V, l_ev[t - 1])
                    hid_mms(tensor, t, (2, 3))
                l_hid_mm[t] = c_T.v

                @block.vector
                def _(vector, t=t):
                    vector.wait_ge(s_T, l_hid_mm[t])
                    if t >= 1:
                        vector.wait_ge(s_T, l_htr[t - 1])
                    for j in range(4):
                        mm = vector.tensor_tensor(
                            hid_t[:, j * 512:(j + 1) * 512], pb[j][:],
                            b1bc[:, j * 512:(j + 1) * 512], ALU.add)
                    mm.then_inc(s_V, 1)
                l_pre[t] = c_V.inc()

                @block.scalar
                def _(scalar, t=t):
                    scalar.wait_ge(s_V, l_pre[t])
                    for j in range(4):
                        scalar.activation(hid_t[:, j * 512:(j + 1) * 512],
                                          hid_t[:, j * 512:(j + 1) * 512],
                                          ACT.Gelu).then_inc(s_S, 1)
                        c_S.inc()
                l_gelu[t] = c_S.v

                if t + 1 < NT:
                    emit_ld3(t + 1)
                    emit_hid01(t + 1)

                @block.tensor
                def _(tensor, t=t):
                    tensor.wait_ge(s_S, l_gelu[t])
                    if t >= 1:
                        tensor.wait_ge(s_V, l_hte[t - 1])
                    for b in range(4):
                        tr_group(tensor, 4 + b,
                                 [hid_t[:, (b * 4 + i) * 128:(b * 4 + i + 1) * 128]
                                  for i in range(4)])
                l_htr[t] = c_T.v

                @block.vector
                def _(vector, t=t):
                    vector.wait_ge(s_T, l_htr[t])
                    if t >= 1:
                        vector.wait_ge(s_T, l_w2[t - 1])
                    for b in range(4):
                        vector.tensor_copy(hth[:, b * 512:(b + 1) * 512],
                                           pb[4 + b][:])
                    vector.drain()
                    for b in range(4):
                        vector.tensor_sub(htl[:, b * 512:(b + 1) * 512],
                                          pb[4 + b][:],
                                          hth[:, b * 512:(b + 1) * 512])
                    vector.engine_nop().then_inc(s_V, 1)
                l_hte[t] = c_V.inc()

                @block.tensor
                def _(tensor, t=t):
                    tensor.wait_ge(s_V, l_hte[t])
                    if t == 0:
                        tokwait(tensor, w2toks)
                    for half in range(2):
                        oj = half * 512
                        n = 0
                        for ic in range(16):
                            tsl = slice(ic * 128, (ic + 1) * 128)
                            wsl = slice(ic * D + oj, ic * D + oj + 512)
                            for t_sb, w_sb in ((hth, w2h), (hth, w2l),
                                               (htl, w2h)):
                                n += 1
                                mm = tensor.matmul(pb[2 + half][:],
                                                   t_sb[:, tsl],
                                                   w_sb[:, wsl],
                                                   start=(n == 1),
                                                   stop=(n == 48))
                        mm.then_inc(s_T, 1)
                        c_T.inc()
                l_w2[t] = c_T.v

                if half_idx == 0:
                    @block.vector
                    def _(vector, t=t, p=p, dslc=dslc):
                        vector.wait_ge(s_T, l_w2[t])
                        if t >= 2:
                            tokwait(vector, l_sp[t - 2])
                        vector.tensor_copy(pt3_t[:, p * D:p * D + 512], pb[2][:])
                        vector.tensor_copy(pt3_t[:, p * D + 512:(p + 1) * D],
                                           pb[3][:]).then_inc(s_V, 1)
                    l_ev[t] = c_V.inc()

                    @block.scalar
                    def _(scalar, t=t, dslc=dslc):
                        scalar.wait_ge(s_V, l_ev[t])
                        l_sp[t] = stq.dma(scalar, sp_pt[t], pt3_t[:, dslc])
                else:
                    @block.vector
                    def _(vector, t=t, p=p, dslc=dslc):
                        vector.wait_ge(s_T, l_w2[t])
                        if t >= 1:
                            tokwait(vector, l_sp[t - 1])
                        for half in range(2):
                            oj = half * 512
                            osl = slice(oj, oj + 512)
                            vector.tensor_add(xm_t[:, osl], pb[2 + half][:],
                                              b2bc[:, osl])
                            vector.drain()
                            vector.tensor_add(xm_t[:, osl], xm_t[:, osl],
                                              pt3_t[:, p * D + oj:p * D + oj + 512])
                            vector.drain()
                            vector.tensor_add(xm_t[:, osl], xm_t[:, osl],
                                              x1b_t[:, p * D + oj:p * D + oj + 512])
                            vector.drain()
                            vector.tensor_add(xm_t[:, osl], xm_t[:, osl],
                                              xb_t[:, p * D + oj:p * D + oj + 512])
                            vector.drain()
                        vector.engine_nop().then_inc(s_V, 1)
                    l_ev[t] = c_V.inc()

                    @block.vector
                    def _(vector, t=t, p=p, dslc=dslc):
                        pp = slice(p, p + 1)
                        vector.drain()
                        vector.reduce_sum(st_sum[:, pp], xm_t[:], axis=AX)
                        vector.scalar_tensor_tensor(x1b_t[:, dslc], xm_t[:],
                                                    1.0, xm_t[:], ALU.bypass,
                                                    ALU.mult,
                                                    accum_out=st_var[:, pp])
                        vector.reduce_max(st_mx[:, pp], xm_t[:], axis=AX)
                        vector.tensor_reduce(st_mn[:, pp], xm_t[:],
                                             axis=AX, op=ALU.min)
                        vector.drain()
                        vector.tensor_scalar(st_nmu[:, pp], st_sum[:, pp],
                                             -1.0 / D, None, ALU.mult)
                        vector.tensor_scalar(st_rv[:, pp], st_var[:, pp],
                                             1.0 / D, EPS, ALU.mult, ALU.add)
                        vector.drain()
                        vector.tensor_mul(nrs[:, 0:1], st_nmu[:, pp],
                                          st_nmu[:, pp])
                        vector.drain()
                        vector.tensor_sub(st_rv[:, pp], st_rv[:, pp],
                                          nrs[:, 0:1])
                        vector.tensor_add(st_mx[:, pp], st_mx[:, pp],
                                          st_nmu[:, pp])
                        vector.tensor_add(st_mn[:, pp], st_mn[:, pp],
                                          st_nmu[:, pp])
                        vector.drain()
                        vector.reciprocal(st_rv[:, pp], st_rv[:, pp])
                        vector.tensor_scalar(st_mn[:, pp], st_mn[:, pp], -1.0,
                                             None, ALU.mult)
                        vector.drain()
                        vector.tensor_max(st_mx[:, pp], st_mx[:, pp],
                                          st_mn[:, pp])
                        vector.drain()
                        vector.tensor_mul(st_mx[:, pp], st_mx[:, pp],
                                          st_mx[:, pp])
                        vector.drain()
                        vector.tensor_mul(st_mx[:, pp], st_mx[:, pp],
                                          st_rv[:, pp])
                        vector.drain()
                        vector.tensor_max(rmax1[:, 0:1], rmax1[:, 0:1],
                                          st_mx[:, pp]).then_inc(s_V, 1)
                    l_stats[t] = c_V.inc()

                    @block.scalar
                    def _(scalar, t=t):
                        scalar.wait_ge(s_V, l_ev[t])
                        l_sp[t] = stq.dma(scalar, sp_xm[t], xm_t[:])

            barrier(f"S3{half_idx}end")
            s3.close()

        mlp_half(0)
        mlp_half(1)

        def gs_start(rmax_sb, cc_in_t, cc_out_t, gmax_sb, bank):
            @block.tensor
            def _(tensor):
                tensor.transpose(pb[bank][0:1, 0:128], rmax_sb[:, 0:1],
                                 ident[:]).then_inc(s_T, 1)
            tt = c_T.inc()

            @block.vector
            def _(vector):
                vector.wait_ge(s_T, tt)
                vector.reduce_max(tiny[0:1, 0:1], pb[bank][0:1, 0:128],
                                  axis=AX).then_inc(s_V, 1)
            tv = c_V.inc()

            g0 = c_gp.v
            cc0 = c_C.v

            @block.gpsimd
            def _(gpsimd):
                gpsimd.wait_ge(s_V, tv)
                if g0:
                    gpsimd.wait_ge(s_gp, g0)
                gpsimd.dma_start(out=cc_in_t.ap(),
                                 in_=tiny[0:1, 0:1]).then_inc(s_gp, 16)
                gpsimd.wait_ge(s_gp, g0 + 16)
                gpsimd.collective_compute(
                    "AllReduce", ALU.max,
                    replica_groups=[list(range(cc_cores))],
                    ins=[cc_in_t.ap()],
                    outs=[cc_out_t.ap()],
                ).then_inc(s_C, 1)
                gpsimd.wait_ge(s_C, cc0 + 1)
                gpsimd.dma_start(out=gmax_sb[:],
                                 in_=cc_out_t.ap()).then_inc(s_gp, 16)
            c_gp.inc(32)
            c_C.inc(1)
            return c_gp.v

        def gs_finish(g_done, gmax_sb, clip_idx, scale_g, inv_bc, sc_bc,
                      bank, sqrt_first=False):
            @block.scalar
            def _(scalar):
                scalar.wait_ge(s_gp, g_done)
                if sqrt_first:
                    scalar.activation(gmax_sb[:], gmax_sb[:], ACT.Ln)
                    scalar.drain()
                    scalar.activation(gmax_sb[:], gmax_sb[:], ACT.Exp, scale=0.5)
                    scalar.drain()
                scalar.activation(tiny[0:1, 1:2],
                                  clip_sb[0:1, clip_idx:clip_idx + 1],
                                  ACT.Sigmoid).then_inc(s_S, 1)
            ts_ = c_S.inc()

            @block.vector
            def _(vector):
                vector.wait_ge(s_S, ts_)
                vector.drain()
                vector.tensor_mul(tiny[0:1, 2:3], tiny[0:1, 1:2], gmax_sb[:])
                vector.drain()
                vector.tensor_scalar(scale_g[:], tiny[0:1, 2:3], INV127, None,
                                     ALU.mult)
                vector.drain()
                vector.reciprocal(tiny[0:1, 3:4], scale_g[:]).then_inc(s_V, 1)
            tv2 = c_V.inc()

            @block.tensor
            def _(tensor):
                tensor.wait_ge(s_V, tv2)
                tensor.matmul(pb[bank][:, 0:1], ones_r[:], tiny[0:1, 3:4],
                              start=True, stop=True).then_inc(s_T, 1)
            tt2 = c_T.inc()

            @block.vector
            def _(vector):
                vector.wait_ge(s_T, tt2)
                vector.tensor_copy(inv_bc[:], pb[bank][:, 0:1]).then_inc(s_V, 1)
            tv3 = c_V.inc()

            @block.tensor
            def _(tensor):
                tensor.wait_ge(s_V, tv3)
                tensor.matmul(pb[bank][:, 0:1], ones_r[:], scale_g[0:1, 0:1],
                              start=True, stop=True).then_inc(s_T, 1)
            tt3 = c_T.inc()

            @block.vector
            def _(vector):
                vector.wait_ge(s_T, tt3)
                vector.tensor_copy(sc_bc[:], pb[bank][:, 0:1]).then_inc(s_V, 1)
            return c_V.inc()

        gs1_done = gs_start(rmax1, cc1_in, cc1_out, gmax1, bank=2)

        s4 = ExitStack()
        qwg_sb = s4.enter_context(nc.sbuf_tensor("qwg_sb", [128, 8 * DQ], BF16))
        qwu_sb = s4.enter_context(nc.sbuf_tensor("qwu_sb", [128, 8 * DQ], BF16))
        wsl_t = s4.enter_context(nc.sbuf_tensor("wsl_t", [128, 2 * D], F32))
        wql_t = s4.enter_context(nc.sbuf_tensor("wql_t", [128, 2 * D], F32))
        sgbc = s4.enter_context(nc.sbuf_tensor("sgbc", [128, DQ], F32))
        subc = s4.enter_context(nc.sbuf_tensor("subc", [128, DQ], F32))
        xm4_t = s4.enter_context(nc.sbuf_tensor("xm4_t", [128, 2 * D], F32))
        h14_t = s4.enter_context(nc.sbuf_tensor("h14_t", [128, 2 * D], F32))
        xq_t = s4.enter_context(nc.sbuf_tensor("xq_t", [128, D], F32))
        xqt_t = s4.enter_context(nc.sbuf_tensor("xqt_t", [128, 2 * D], BF16))
        gr_t = s4.enter_context(nc.sbuf_tensor("gr_t", [128, 512], F32))
        sg_t = s4.enter_context(nc.sbuf_tensor("sg_t", [128, 512], F32))
        up_t = s4.enter_context(nc.sbuf_tensor("up_t", [128, 512], F32))
        hp_t = s4.enter_context(nc.sbuf_tensor("hp_t", [128, 2 * DQ], F32))
        rmw = s4.enter_context(nc.sbuf_tensor("rmw", [128, 4], F32))

        l4_wld = [[0] * NSLAB_G for _ in range(2)]
        l4_q = [[0] * NSLAB_G for _ in range(2)]
        l4_tr = [[0] * NSLAB_G for _ in range(2)]
        l4_ev = [[0] * NSLAB_G for _ in range(2)]
        l4_row = [[0] * NSLAB_G for _ in range(2)]
        l4_bcv = [[0] * NSLAB_G for _ in range(2)]

        for wi, (wsrc, qdst, sdst) in enumerate(
                ((wgate, qwg_sb, sgbc), (wup, qwu_sb, subc))):
            for s in range(NSLAB_G):
                sp4 = s % 2
                dslc = slice(sp4 * D, (sp4 + 1) * D)
                idx = wi * NSLAB_G + s

                @block.sync
                def _(sync, wi=wi, s=s, dslc=dslc, wsrc=wsrc, idx=idx):
                    if idx == 0:
                        sync.wait_ge(s_rdy, c_rdy.v)
                    if idx >= 2:
                        pi, ps = divmod(idx - 2, NSLAB_G)
                        sync.wait_ge(s_V, l4_q[pi][ps])
                    l4_wld[wi][s] = ldq.dma(sync, wsl_t[:, dslc],
                                            wsrc[s * 128:(s + 1) * 128, :])

                @block.vector
                def _(vector, wi=wi, s=s, dslc=dslc, idx=idx):
                    tokwait(vector, l4_wld[wi][s])
                    if idx >= 2:
                        pi, ps = divmod(idx - 2, NSLAB_G)
                        vector.wait_ge(s_T, l4_tr[pi][ps])
                    vector.reduce_max(rmw[:, 0:1], wsl_t[:, dslc], axis=AX,
                                      apply_absolute_value=True)
                    vector.drain()
                    vector.tensor_scalar(rmw[:, 1:2], rmw[:, 0:1], INV127,
                                         None, ALU.mult)
                    vector.drain()
                    vector.reciprocal(rmw[:, 2:3], rmw[:, 1:2])
                    vector.drain()
                    vector.tensor_scalar(wql_t[:, dslc], wsl_t[:, dslc],
                                         rmw[:, 2:3], MAGIC, ALU.mult, ALU.add)
                    vector.drain()
                    vector.tensor_scalar(wql_t[:, dslc], wql_t[:, dslc],
                                         MAGIC, None,
                                         ALU.subtract).then_inc(s_V, 1)
                l4_q[wi][s] = c_V.inc()

                @block.tensor
                def _(tensor, wi=wi, s=s, sp4=sp4, idx=idx):
                    tensor.wait_ge(s_V, l4_q[wi][s])
                    if idx >= 1:
                        pi, ps = divmod(idx - 1, NSLAB_G)
                        tensor.wait_ge(s_S, l4_ev[pi][ps])
                        tensor.wait_ge(s_V, l4_bcv[pi][ps])
                    tensor.transpose(pb[7][0:1, 0:128], rmw[:, 1:2],
                                     ident[:]).then_inc(s_T, 1)
                    c_T.inc()
                    tr_group(tensor, 0, [wql_t[:, sp4 * D + i * 128:
                                                sp4 * D + (i + 1) * 128]
                                         for i in range(4)])
                    tr_group(tensor, 1, [wql_t[:, sp4 * D + i * 128:
                                                sp4 * D + (i + 1) * 128]
                                         for i in range(4, 8)])
                l4_tr[wi][s] = c_T.v

                @block.vector
                def _(vector, wi=wi, s=s):
                    vector.wait_ge(s_T, l4_tr[wi][s])
                    vector.tensor_copy(row128[0:1, :],
                                       pb[7][0:1, 0:128]).then_inc(s_V, 1)
                l4_row[wi][s] = c_V.inc()

                @block.tensor
                def _(tensor, wi=wi, s=s):
                    tensor.wait_ge(s_V, l4_row[wi][s])
                    tensor.matmul(pb[6][:, 0:128], ones_r[:], row128[0:1, :],
                                  start=True, stop=True).then_inc(s_T, 1)
                bcm = c_T.inc()

                @block.vector
                def _(vector, wi=wi, s=s, sdst=sdst, bcm=bcm):
                    vector.wait_ge(s_T, bcm)
                    vector.tensor_copy(sdst[:, s * 128:(s + 1) * 128],
                                       pb[6][:, 0:128]).then_inc(s_V, 1)
                l4_bcv[wi][s] = c_V.inc()

                @block.scalar
                def _(scalar, wi=wi, s=s, qdst=qdst):
                    scalar.wait_ge(s_T, l4_tr[wi][s])
                    for c in range(8):
                        bank, off = (0, c * 128) if c < 4 else (1, (c - 4) * 128)
                        scalar.activation(
                            qdst[:, c * DQ + s * 128: c * DQ + (s + 1) * 128],
                            pb[bank][:, off:off + 128],
                            ACT.Copy).then_inc(s_S, 1)
                        c_S.inc()
                l4_ev[wi][s] = c_S.v

        fin1 = gs_finish(gs1_done, gmax1, 0, sxg, invsx, scbc1, bank=6,
                         sqrt_first=True)

        @block.vector
        def _(vector):
            vector.wait_ge(s_V, fin1)
            vector.wait_ge(s_V, l4_bcv[1][NSLAB_G - 1])
            vector.drain()
            vector.tensor_scalar(sgbc[:], sgbc[:], scbc1[:, 0:1], None,
                                 ALU.mult)
            vector.drain()
            vector.tensor_scalar(subc[:], subc[:], scbc1[:, 0:1], None,
                                 ALU.mult).then_inc(s_V, 1)
        l4_scaled = c_V.inc()

        l4_ld = [0] * NT
        l4_h1 = [0] * NT
        l4_xq = [0] * NT
        l4_xtr = [0] * NT
        l4_xte = [0] * NT
        l4_mm = [[0] * 6 for _ in range(NT)]
        l4_sig = [[0] * 6 for _ in range(NT)]
        l4_sil = [[0] * 6 for _ in range(NT)]
        l4_sp = [0] * NT

        for t in range(NT):
            p = t % 2
            dslc = slice(p * D, (p + 1) * D)

            @block.sync
            def _(sync, t=t, dslc=dslc):
                if t == 0:
                    sync.wait_ge(s_V, l4_scaled)
                    sync.wait_ge(s_S, l4_ev[1][NSLAB_G - 1])
                if t >= 2:
                    sync.wait_ge(s_V, l4_h1[t - 2])
                lab = ldq.dma(sync, xm4_t[:, dslc], sp_xm[t])
                l4_ld[t] = lab

            @block.vector
            def _(vector, t=t):
                tokwait(vector, l4_ld[t])
                if t >= 2:
                    vector.wait_ge(s_V, l4_xq[t - 2])
            l4_h1[t] = ln_emit(xm4_t[:, dslc], h14_t[:, dslc], p)

            @block.vector
            def _(vector, t=t, dslc=dslc):
                if t >= 1:
                    vector.wait_ge(s_T, l4_xtr[t - 1])
                vector.drain()
                vector.tensor_scalar(xq_t[:], h14_t[:, dslc],
                                     invsx[:, 0:1], MAGIC, ALU.mult, ALU.add)
                vector.drain()
                vector.tensor_scalar(xq_t[:], xq_t[:],
                                     MAGIC - 128.0, MAGIC + 127.0,
                                     ALU.max, ALU.min)
                vector.drain()
                vector.tensor_scalar(xq_t[:], xq_t[:], MAGIC, None,
                                     ALU.subtract).then_inc(s_V, 1)
            l4_xq[t] = c_V.inc()

            @block.tensor
            def _(tensor, t=t):
                tensor.wait_ge(s_V, l4_xq[t])
                if t >= 1:
                    tensor.wait_ge(s_S, l4_xte[t - 1])
                tr_group(tensor, 6, [xq_t[:, i * 128:(i + 1) * 128]
                                     for i in range(4)])
                tr_group(tensor, 7, [xq_t[:, i * 128:(i + 1) * 128]
                                     for i in range(4, 8)])
            l4_xtr[t] = c_T.v

            @block.scalar
            def _(scalar, t=t, p=p):
                scalar.wait_ge(s_T, l4_xtr[t])
                if t >= 1:
                    scalar.wait_ge(s_T, l4_mm[t - 1][5])
                for c in range(8):
                    bank, off = (6, c * 128) if c < 4 else (7, (c - 4) * 128)
                    scalar.activation(
                        xqt_t[:, p * D + c * 128: p * D + (c + 1) * 128],
                        pb[bank][:, off:off + 128], ACT.Copy).then_inc(s_S, 1)
                    c_S.inc()
            l4_xte[t] = c_S.v

            for j in range(6):
                w = min(512, DQ - j * 512)
                gb = (j % 2) * 2
                ub = gb + 1

                @block.tensor
                def _(tensor, t=t, p=p, j=j, w=w, gb=gb, ub=ub):
                    tensor.wait_ge(s_S, l4_xte[t])
                    if j >= 2:
                        tensor.wait_ge(s_V, l4_sil[t][j - 2])
                    elif t >= 1:
                        tensor.wait_ge(s_V, l4_sil[t - 1][4 + j])
                    for (qsb, bank) in ((qwg_sb, gb), (qwu_sb, ub)):
                        for kc in range(8):
                            mm = tensor.matmul(
                                pb[bank][:, 0:w],
                                xqt_t[:, p * D + kc * 128: p * D + (kc + 1) * 128],
                                qsb[:, kc * DQ + j * 512: kc * DQ + j * 512 + w],
                                start=(kc == 0), stop=(kc == 7))
                        mm.then_inc(s_T, 1)
                        c_T.inc()
                l4_mm[t][j] = c_T.v

                @block.vector
                def _(vector, t=t, j=j, w=w, gb=gb, ub=ub):
                    jsl = slice(j * 512, j * 512 + w)
                    vector.wait_ge(s_T, l4_mm[t][j])
                    if t >= 2 and j == 0:
                        tokwait(vector, l4_sp[t - 2])
                    vector.tensor_mul(gr_t[:, 0:w], pb[gb][:, 0:w], sgbc[:, jsl])
                    vector.tensor_mul(up_t[:, 0:w], pb[ub][:, 0:w],
                                      subc[:, jsl]).then_inc(s_V, 1)
                vsc = c_V.inc()

                @block.scalar
                def _(scalar, t=t, j=j, w=w, vsc=vsc):
                    scalar.wait_ge(s_V, vsc)
                    scalar.activation(sg_t[:, 0:w], gr_t[:, 0:w],
                                      ACT.Silu).then_inc(s_S, 1)
                l4_sig[t][j] = c_S.inc()

                @block.vector
                def _(vector, t=t, p=p, j=j, w=w):
                    hsl = slice(p * DQ + j * 512, p * DQ + j * 512 + w)
                    vector.wait_ge(s_S, l4_sig[t][j])
                    vector.drain()
                    vector.tensor_mul(hp_t[:, hsl], sg_t[:, 0:w], up_t[:, 0:w])
                    vector.drain()
                    vector.reduce_max(rmw[:, 3:4], hp_t[:, hsl], axis=AX,
                                      apply_absolute_value=True)
                    vector.drain()
                    vector.tensor_max(rmax2[:, 0:1], rmax2[:, 0:1],
                                      rmw[:, 3:4]).then_inc(s_V, 1)
                l4_sil[t][j] = c_V.inc()

            @block.scalar
            def _(scalar, t=t, p=p):
                scalar.wait_ge(s_V, l4_sil[t][5])
                l4_sp[t] = stq.dma(scalar, sp_hp[t],
                                   hp_t[:, p * DQ:(p + 1) * DQ])

        barrier("S4end")
        s4.close()

        gs2_done = gs_start(rmax2, cc2_in, cc2_out, gmax2, bank=2)

        s5 = ExitStack()
        qwd_sb = s5.enter_context(nc.sbuf_tensor("qwd_sb", [128, NSLAB_G * D], BF16))
        wsl5 = s5.enter_context(nc.sbuf_tensor("wsl5", [128, 2 * DQ], F32))
        wql5 = s5.enter_context(nc.sbuf_tensor("wql5", [128, 2 * DQ], F32))
        sdbc = s5.enter_context(nc.sbuf_tensor("sdbc", [128, D], F32))
        hp5_t = s5.enter_context(nc.sbuf_tensor("hp5_t", [128, 2 * DQ], F32))
        hq5_t = s5.enter_context(nc.sbuf_tensor("hq5_t", [128, 2 * DQ], F32))
        hqt_t = s5.enter_context(nc.sbuf_tensor("hqt_t", [128, 2 * DQ], BF16))
        xm5_t = s5.enter_context(nc.sbuf_tensor("xm5_t", [128, 2 * D], F32))
        o5_t = s5.enter_context(nc.sbuf_tensor("o5_t", [128, 2 * D], F32))
        rmw5 = s5.enter_context(nc.sbuf_tensor("rmw5", [128, 4], F32))

        l5_wld = [0] * 8
        l5_q = [0] * 8
        l5_tr = [0] * 8
        l5_ev = [0] * 8
        l5_row = [0] * 8
        l5_bcv = [0] * 8
        for s in range(8):
            sp5 = s % 2
            qslc = slice(sp5 * DQ, (sp5 + 1) * DQ)

            @block.sync
            def _(sync, s=s, qslc=qslc):
                if s == 0:
                    sync.wait_ge(s_rdy, c_rdy.v)
                if s >= 2:
                    sync.wait_ge(s_V, l5_q[s - 2])
                l5_wld[s] = ldq.dma(sync, wsl5[:, qslc],
                                    wdown[s * 128:(s + 1) * 128, :])

            @block.vector
            def _(vector, s=s, qslc=qslc):
                tokwait(vector, l5_wld[s])
                if s >= 2:
                    vector.wait_ge(s_T, l5_tr[s - 2])
                vector.reduce_max(rmw5[:, 0:1], wsl5[:, qslc], axis=AX,
                                  apply_absolute_value=True)
                vector.drain()
                vector.tensor_scalar(rmw5[:, 1:2], rmw5[:, 0:1], INV127,
                                     None, ALU.mult)
                vector.drain()
                vector.reciprocal(rmw5[:, 2:3], rmw5[:, 1:2])
                vector.drain()
                vector.tensor_scalar(wql5[:, qslc], wsl5[:, qslc],
                                     rmw5[:, 2:3], MAGIC, ALU.mult, ALU.add)
                vector.drain()
                vector.tensor_scalar(wql5[:, qslc], wql5[:, qslc], MAGIC,
                                     None, ALU.subtract).then_inc(s_V, 1)
            l5_q[s] = c_V.inc()

            @block.tensor
            def _(tensor, s=s, sp5=sp5):
                tensor.wait_ge(s_V, l5_q[s])
                if s >= 1:
                    tensor.wait_ge(s_S, l5_ev[s - 1])
                    tensor.wait_ge(s_V, l5_bcv[s - 1])
                tensor.transpose(pb[7][0:1, 0:128], rmw5[:, 1:2],
                                 ident[:]).then_inc(s_T, 1)
                c_T.inc()
                done = 0
                bank_iter = 0
                while done < NSLAB_G:
                    nn_ = min(4, NSLAB_G - done)
                    tr_group(tensor, bank_iter % 6,
                             [wql5[:, sp5 * DQ + (done + i) * 128:
                                   sp5 * DQ + (done + i + 1) * 128]
                              for i in range(nn_)])
                    done += nn_
                    bank_iter += 1
            l5_tr[s] = c_T.v

            @block.vector
            def _(vector, s=s):
                vector.wait_ge(s_T, l5_tr[s])
                vector.tensor_copy(row128[0:1, :],
                                   pb[7][0:1, 0:128]).then_inc(s_V, 1)
            l5_row[s] = c_V.inc()

            @block.tensor
            def _(tensor, s=s):
                tensor.wait_ge(s_V, l5_row[s])
                tensor.matmul(pb[6][:, 0:128], ones_r[:], row128[0:1, :],
                              start=True, stop=True).then_inc(s_T, 1)
            bcm = c_T.inc()

            @block.vector
            def _(vector, s=s, bcm=bcm):
                vector.wait_ge(s_T, bcm)
                vector.tensor_copy(sdbc[:, s * 128:(s + 1) * 128],
                                   pb[6][:, 0:128]).then_inc(s_V, 1)
            l5_bcv[s] = c_V.inc()

            @block.scalar
            def _(scalar, s=s):
                scalar.wait_ge(s_T, l5_tr[s])
                done = 0
                bank_iter = 0
                while done < NSLAB_G:
                    nn_ = min(4, NSLAB_G - done)
                    for i in range(nn_):
                        ic = done + i
                        scalar.activation(
                            qwd_sb[:, ic * D + s * 128: ic * D + (s + 1) * 128],
                            pb[bank_iter % 6][:, i * 128:(i + 1) * 128],
                            ACT.Copy).then_inc(s_S, 1)
                        c_S.inc()
                    done += nn_
                    bank_iter += 1
            l5_ev[s] = c_S.v

        fin2 = gs_finish(gs2_done, gmax2, 1, sdg, invsd, scbc2, bank=6)

        @block.vector
        def _(vector):
            vector.wait_ge(s_V, fin2)
            vector.wait_ge(s_V, l5_bcv[7])
            vector.drain()
            vector.tensor_scalar(sdbc[:], sdbc[:], scbc2[:, 0:1], None,
                                 ALU.mult).then_inc(s_V, 1)
        l5_scaled = c_V.inc()

        l5_ld = [0] * NT
        l5_hq = [0] * NT
        l5_xtr = [0] * NT
        l5_xte = [0] * NT
        l5_mm = [0] * NT
        l5_out = [0] * NT
        l5_sp = [0] * NT

        for t in range(NT):
            p = t % 2
            dslc = slice(p * D, (p + 1) * D)
            qslc = slice(p * DQ, (p + 1) * DQ)

            def emit_ldq5(tt):
                pq = tt % 2
                dq = slice(pq * D, (pq + 1) * D)
                qq = slice(pq * DQ, (pq + 1) * DQ)

                @block.sync
                def _(sync, tt=tt, dq=dq, qq=qq):
                    if tt == 0:
                        sync.wait_ge(s_V, l5_scaled)
                        sync.wait_ge(s_S, l5_ev[7])
                    if tt >= 2:
                        sync.wait_ge(s_V, l5_hq[tt - 2])
                        sync.wait_ge(s_V, l5_out[tt - 2])
                    l5_ld[tt] = [
                        ldq.dma(sync, hp5_t[:, qq], sp_hp[tt]),
                        ldq.dma(sync, xm5_t[:, dq], sp_xm[tt]),
                    ]

                @block.vector
                def _(vector, tt=tt, qq=qq):
                    tokwait(vector, l5_ld[tt])
                    if tt >= 2:
                        vector.wait_ge(s_T, l5_xtr[tt - 2])
                    vector.tensor_scalar(hq5_t[:, qq], hp5_t[:, qq],
                                         invsd[:, 0:1], MAGIC,
                                         ALU.mult, ALU.add)
                    vector.drain()
                    vector.tensor_scalar(hq5_t[:, qq], hq5_t[:, qq],
                                         MAGIC - 128.0, MAGIC + 127.0,
                                         ALU.max, ALU.min)
                    vector.drain()
                    vector.tensor_scalar(hq5_t[:, qq], hq5_t[:, qq], MAGIC,
                                         None, ALU.subtract).then_inc(s_V, 1)
                l5_hq[tt] = c_V.inc()

            if t == 0:
                emit_ldq5(0)

            @block.tensor
            def _(tensor, t=t, p=p):
                tensor.wait_ge(s_V, l5_hq[t])
                if t >= 1:
                    tensor.wait_ge(s_S, l5_xte[t - 1])
                done = 0
                bank_iter = 0
                while done < NSLAB_G:
                    nn_ = min(4, NSLAB_G - done)
                    tr_group(tensor, bank_iter % 6,
                             [hq5_t[:, p * DQ + (done + i) * 128:
                                    p * DQ + (done + i + 1) * 128]
                              for i in range(nn_)])
                    done += nn_
                    bank_iter += 1
            l5_xtr[t] = c_T.v

            if t + 1 < NT:
                emit_ldq5(t + 1)

            @block.scalar
            def _(scalar, t=t, p=p):
                scalar.wait_ge(s_T, l5_xtr[t])
                if t >= 1:
                    scalar.wait_ge(s_T, l5_mm[t - 1])
                done = 0
                bank_iter = 0
                while done < NSLAB_G:
                    nn_ = min(4, NSLAB_G - done)
                    for i in range(nn_):
                        ic = done + i
                        scalar.activation(
                            hqt_t[:, p * DQ + ic * 128: p * DQ + (ic + 1) * 128],
                            pb[bank_iter % 6][:, i * 128:(i + 1) * 128],
                            ACT.Copy).then_inc(s_S, 1)
                        c_S.inc()
                    done += nn_
                    bank_iter += 1
            l5_xte[t] = c_S.v

            @block.tensor
            def _(tensor, t=t, p=p):
                tensor.wait_ge(s_S, l5_xte[t])
                if t >= 1:
                    tensor.wait_ge(s_V, l5_out[t - 1])
                for half in range(2):
                    oj = half * 512
                    for ic in range(NSLAB_G):
                        mm = tensor.matmul(
                            pb[6 + half][:],
                            hqt_t[:, p * DQ + ic * 128: p * DQ + (ic + 1) * 128],
                            qwd_sb[:, ic * D + oj: ic * D + oj + 512],
                            start=(ic == 0), stop=(ic == NSLAB_G - 1))
                    mm.then_inc(s_T, 1)
                    c_T.inc()
            l5_mm[t] = c_T.v

            @block.vector
            def _(vector, t=t, p=p):
                vector.wait_ge(s_T, l5_mm[t])
                if t >= 2:
                    tokwait(vector, l5_sp[t - 2])
                for half in range(2):
                    oj = half * 512
                    sl = slice(p * D + oj, p * D + oj + 512)
                    vector.tensor_mul(o5_t[:, sl], pb[6 + half][:],
                                      sdbc[:, oj:oj + 512])
                    vector.drain()
                    vector.tensor_add(o5_t[:, sl], o5_t[:, sl], xm5_t[:, sl])
                    vector.drain()
                vector.engine_nop().then_inc(s_V, 1)
            l5_out[t] = c_V.inc()

            @block.scalar
            def _(scalar, t=t, dslc=dslc):
                scalar.wait_ge(s_V, l5_out[t])
                l5_sp[t] = stq.dma(scalar, out[t * 128:(t + 1) * 128, :],
                                   o5_t[:, dslc])

        fin_toks = [(s, v) for s, v in zip(stq.sems, stq.vals) if v > 0]

        @block.vector
        def _(vector):
            tokwait(vector, fin_toks)

        s5.close()

    return nc


def _chunk_vec(v):
    return np.ascontiguousarray(v.reshape(8, 128).T)


def _eye_bf16():
    import ml_dtypes
    return np.eye(128).astype(ml_dtypes.bfloat16)


def _make_masks():
    q = np.arange(128)[:, None]
    k = np.arange(256)[None, :]
    allow_gen = np.where(k < 128, k >= q + 1, (k - 128) <= q)
    allow_first = np.where(k < 128, False, (k - 128) <= q)
    mg = np.where(allow_gen, 0.0, NEGINF).astype(np.float32)
    mf = np.where(allow_first, 0.0, NEGINF).astype(np.float32)
    return mg, mf


def kernel(L0_attn_x, L0_attn_ln1_g, L0_attn_ln1_b, L0_attn_WQ, L0_attn_bQ,
           L0_attn_WK, L0_attn_bK, L0_attn_WV, L0_attn_bV, L0_attn_WO,
           L0_attn_bO, L0_attn_Wg, L0_attn_bg, L0_attn_A, L0_attn_Bw,
           L0_attn_Cw, L0_attn_ln2_g, L0_attn_ln2_b, L0_attn_W1, L0_attn_b1,
           L0_attn_W2, L0_attn_b2, L0_ffn_clip_x, L0_ffn_clip_down,
           L0_ffn_w_gate, L0_ffn_w_up, L0_ffn_w_down):
    x = np.ascontiguousarray(np.asarray(L0_attn_x, dtype=np.float32))
    B, T, d = x.shape
    n_cores = 8
    bs = B // n_cores

    mg, mf = _make_masks()
    common = {
        "wq": np.ascontiguousarray(L0_attn_WQ, np.float32),
        "wk": np.ascontiguousarray(L0_attn_WK, np.float32),
        "wv": np.ascontiguousarray(L0_attn_WV, np.float32),
        "wo": np.ascontiguousarray(L0_attn_WO, np.float32),
        "wg": np.ascontiguousarray(L0_attn_Wg, np.float32),
        "w1": np.ascontiguousarray(L0_attn_W1, np.float32),
        "w2": np.ascontiguousarray(L0_attn_W2, np.float32),
        "bwp": np.ascontiguousarray(L0_attn_Bw, np.float32),
        "cwp": np.ascontiguousarray(L0_attn_Cw, np.float32),
        "ln1g": _chunk_vec(np.asarray(L0_attn_ln1_g, np.float32)),
        "ln1b": _chunk_vec(np.asarray(L0_attn_ln1_b, np.float32)),
        "ln2g": _chunk_vec(np.asarray(L0_attn_ln2_g, np.float32)),
        "ln2b": _chunk_vec(np.asarray(L0_attn_ln2_b, np.float32)),
        "bqp": _chunk_vec(np.asarray(L0_attn_bQ, np.float32)),
        "bkp": _chunk_vec(np.asarray(L0_attn_bK, np.float32)),
        "bvr": np.asarray(L0_attn_bV, np.float32).reshape(1, -1),
        "bor": np.asarray(L0_attn_bO, np.float32).reshape(1, -1),
        "bgr": np.asarray(L0_attn_bg, np.float32).reshape(1, -1),
        "b1r": np.asarray(L0_attn_b1, np.float32).reshape(1, -1),
        "b2r": np.asarray(L0_attn_b2, np.float32).reshape(1, -1),
        "a_rep": np.ascontiguousarray(
            np.broadcast_to(np.asarray(L0_attn_A, np.float32)[:, None],
                            (DH, 128))),
        "wgate": np.ascontiguousarray(L0_ffn_w_gate, np.float32),
        "wup": np.ascontiguousarray(L0_ffn_w_up, np.float32),
        "wdown": np.ascontiguousarray(L0_ffn_w_down, np.float32),
        "clipx": np.asarray(L0_ffn_clip_x, np.float32).reshape(1, 1),
        "clipd": np.asarray(L0_ffn_clip_down, np.float32).reshape(1, 1),
        "identc": np.eye(128, dtype=np.float32),
        "identcb": _eye_bf16(),
        "maskg": mg,
        "maskf": mf,
    }

    nc = build_program(n_tiles=(bs * T) // 128, seq_tiles=T // 128,
                       cc_cores=n_cores)
    in_maps = []
    for c in range(n_cores):
        m = dict(common)
        m["x"] = np.ascontiguousarray(
            x[c * bs:(c + 1) * bs].reshape(bs * T, d))
        in_maps.append(m)
    res = run_bass_kernel_spmd(nc, in_maps, core_ids=list(range(n_cores)))
    outs = [res.results[c]["out"].reshape(bs, T, d) for c in range(n_cores)]
    return np.concatenate(outs, axis=0)


# version 24
# speedup vs baseline: 1.0018x; 1.0018x over previous
import numpy as np
import concourse.bass as bass
import concourse.mybir as mybir
from concourse.bass_utils import run_bass_kernel_spmd
from contextlib import ExitStack

F32 = mybir.dt.float32
BF16 = mybir.dt.bfloat16
ALU = mybir.AluOpType
ACT = mybir.ActivationFunctionType
AX = mybir.AxisListType.X

MAGIC = 12582912.0
NEGINF = -1.0e30
D = 1024
DH = 64
NH = 16
DFF = 4096
DQ = 2816
NSLAB_G = DQ // 128
INV127 = 1.0 / 127.0
EPS = 1e-5


class Ct:
    __slots__ = ("v",)

    def __init__(self):
        self.v = 0

    def inc(self, n=1):
        self.v += n
        return self.v


class DmaQ:

    def __init__(self, sems):
        self.sems = sems
        self.vals = [0] * len(sems)
        self.i = 0

    def dma(self, eng, out, in_):
        r = self.i % len(self.sems)
        self.i += 1
        if self.vals[r] > 0:
            eng.wait_ge(self.sems[r], self.vals[r])
        eng.dma_start(out=out, in_=in_).then_inc(self.sems[r], 16)
        self.vals[r] += 16
        return (self.sems[r], self.vals[r])


def tokwait(eng, tok):
    if tok is None:
        return
    if isinstance(tok, list):
        for tk in tok:
            tokwait(eng, tk)
        return
    eng.wait_ge(tok[0], tok[1])


def build_program(n_tiles=32, seq_tiles=16, cc_cores=8):
    nc = bass.Bass("TRN2", target_bir_lowering=False)
    NT = n_tiles
    NTOK = NT * 128

    x_in = nc.dram_tensor("x", [NTOK, D], F32, kind="ExternalInput").ap()
    wq = nc.dram_tensor("wq", [D, D], F32, kind="ExternalInput").ap()
    wk = nc.dram_tensor("wk", [D, D], F32, kind="ExternalInput").ap()
    wv = nc.dram_tensor("wv", [D, D], F32, kind="ExternalInput").ap()
    wo = nc.dram_tensor("wo", [D, D], F32, kind="ExternalInput").ap()
    wg = nc.dram_tensor("wg", [D, D], F32, kind="ExternalInput").ap()
    w1 = nc.dram_tensor("w1", [D, DFF], F32, kind="ExternalInput").ap()
    w2 = nc.dram_tensor("w2", [DFF, D], F32, kind="ExternalInput").ap()
    bwp = nc.dram_tensor("bwp", [D, DH], F32, kind="ExternalInput").ap()
    cwp = nc.dram_tensor("cwp", [DH, D], F32, kind="ExternalInput").ap()
    ln1g = nc.dram_tensor("ln1g", [128, 8], F32, kind="ExternalInput").ap()
    ln1b = nc.dram_tensor("ln1b", [128, 8], F32, kind="ExternalInput").ap()
    ln2g = nc.dram_tensor("ln2g", [128, 8], F32, kind="ExternalInput").ap()
    ln2b = nc.dram_tensor("ln2b", [128, 8], F32, kind="ExternalInput").ap()
    bqp = nc.dram_tensor("bqp", [128, 8], F32, kind="ExternalInput").ap()
    bkp = nc.dram_tensor("bkp", [128, 8], F32, kind="ExternalInput").ap()
    bvr = nc.dram_tensor("bvr", [1, D], F32, kind="ExternalInput").ap()
    bor = nc.dram_tensor("bor", [1, D], F32, kind="ExternalInput").ap()
    bgr = nc.dram_tensor("bgr", [1, D], F32, kind="ExternalInput").ap()
    b1r = nc.dram_tensor("b1r", [1, DFF], F32, kind="ExternalInput").ap()
    b2r = nc.dram_tensor("b2r", [1, D], F32, kind="ExternalInput").ap()
    a_rep = nc.dram_tensor("a_rep", [DH, 128], F32, kind="ExternalInput").ap()
    wgate = nc.dram_tensor("wgate", [DQ, D], F32, kind="ExternalInput").ap()
    wup = nc.dram_tensor("wup", [DQ, D], F32, kind="ExternalInput").ap()
    wdown = nc.dram_tensor("wdown", [D, DQ], F32, kind="ExternalInput").ap()
    clipx = nc.dram_tensor("clipx", [1, 1], F32, kind="ExternalInput").ap()
    clipd = nc.dram_tensor("clipd", [1, 1], F32, kind="ExternalInput").ap()
    identc = nc.dram_tensor("identc", [128, 128], F32, kind="ExternalInput").ap()
    identcb = nc.dram_tensor("identcb", [128, 128], BF16, kind="ExternalInput").ap()
    maskg = nc.dram_tensor("maskg", [128, 256], F32, kind="ExternalInput").ap()
    maskf = nc.dram_tensor("maskf", [128, 256], F32, kind="ExternalInput").ap()

    out = nc.dram_tensor("out", [NTOK, D], F32, kind="ExternalOutput").ap()

    sp_ao = nc.dram_tensor("sp_ao", [NT, 128, D], F32).ap()
    sp_g = nc.dram_tensor("sp_g", [NT, 128, D], F32).ap()
    sp_h0 = nc.dram_tensor("sp_h0", [NT, 128, D], F32).ap()
    sp_st = nc.dram_tensor("sp_st", [NT, DH, 128], F32).ap()
    sp_x1 = nc.dram_tensor("sp_x1", [NT, 128, D], F32).ap()
    sp_h2hi = nc.dram_tensor("sp_h2hi", [NT, 128, D], BF16).ap()
    sp_h2lo = nc.dram_tensor("sp_h2lo", [NT, 128, D], BF16).ap()
    sp_pt = nc.dram_tensor("sp_pt", [NT, 128, D], F32).ap()
    sp_xm = nc.dram_tensor("sp_xm", [NT, 128, D], F32).ap()
    sp_hp = nc.dram_tensor("sp_hp", [NT, 128, DQ], F32).ap()
    sp_w1h = nc.dram_tensor("sp_w1h", [16, 128, 2048], BF16).ap()
    sp_w1l = nc.dram_tensor("sp_w1l", [16, 128, 2048], BF16).ap()
    sp_w2h = nc.dram_tensor("sp_w2h", [32, 128, D], BF16).ap()
    sp_w2l = nc.dram_tensor("sp_w2l", [32, 128, D], BF16).ap()
    cc1_in = nc.dram_tensor("cc1_in", [1, 1], F32)
    cc1_out = nc.dram_tensor("cc1_out", [1, 1], F32)
    cc2_in = nc.dram_tensor("cc2_in", [1, 1], F32)
    cc2_out = nc.dram_tensor("cc2_out", [1, 1], F32)

    ctx = ExitStack()
    with ctx:
        E = ctx.enter_context

        ldsems = [E(nc.semaphore(f"s_ld{i}")) for i in range(8)]
        gpsems = [E(nc.semaphore(f"s_gq{i}")) for i in range(4)]
        stsems = [E(nc.semaphore(f"s_st{i}")) for i in range(6)]
        s_gp = E(nc.semaphore("s_gp"))
        s_rdy = E(nc.semaphore("s_rdy"))
        s_T = E(nc.semaphore("s_T"))
        s_V = E(nc.semaphore("s_V"))
        s_S = E(nc.semaphore("s_S"))
        s_C = E(nc.semaphore("s_C"))
        ldq, stq = DmaQ(ldsems), DmaQ(stsems)
        gpq = DmaQ(gpsems)
        c_gp = Ct()
        c_T, c_V, c_S, c_C = Ct(), Ct(), Ct(), Ct()
        c_rdy = Ct()

        ident = E(nc.sbuf_tensor("ident", [128, 128], F32))
        identb = E(nc.sbuf_tensor("identb", [128, 128], BF16))
        ones_r = E(nc.sbuf_tensor("ones_r", [1, 128], F32))
        ones_b = E(nc.sbuf_tensor("ones_b", [1, 128], BF16))
        arep_sb = E(nc.sbuf_tensor("arep_sb", [DH, 128], F32))
        lnp = E(nc.sbuf_tensor("lnp", [128, 32], F32))
        bqk = E(nc.sbuf_tensor("bqk", [128, 16], F32))
        clip_sb = E(nc.sbuf_tensor("clip_sb", [1, 2], F32))
        gmax1 = E(nc.sbuf_tensor("gmax1", [1, 1], F32))
        gmax2 = E(nc.sbuf_tensor("gmax2", [1, 1], F32))
        invsx = E(nc.sbuf_tensor("invsx", [128, 1], F32))
        invsd = E(nc.sbuf_tensor("invsd", [128, 1], F32))
        scbc1 = E(nc.sbuf_tensor("scbc1", [128, 1], F32))
        scbc2 = E(nc.sbuf_tensor("scbc2", [128, 1], F32))
        sxg = E(nc.sbuf_tensor("sxg", [1, 1], F32))
        sdg = E(nc.sbuf_tensor("sdg", [1, 1], F32))
        st_sum = E(nc.sbuf_tensor("st_sum", [128, 2], F32))
        st_nmu = E(nc.sbuf_tensor("st_nmu", [128, 2], F32))
        st_var = E(nc.sbuf_tensor("st_var", [128, 2], F32))
        st_rv = E(nc.sbuf_tensor("st_rv", [128, 2], F32))
        st_rstd = E(nc.sbuf_tensor("st_rstd", [128, 2], F32))
        st_nmr = E(nc.sbuf_tensor("st_nmr", [128, 2], F32))
        st_mx = E(nc.sbuf_tensor("st_mx", [128, 2], F32))
        nrs = E(nc.sbuf_tensor("nrs", [128, 1], F32))
        st_mn = E(nc.sbuf_tensor("st_mn", [128, 2], F32))
        rmax1 = E(nc.sbuf_tensor("rmax1", [128, 1], F32))
        rmax2 = E(nc.sbuf_tensor("rmax2", [128, 1], F32))
        at_mx = E(nc.sbuf_tensor("at_mx", [128, 2], F32))
        at_nb = E(nc.sbuf_tensor("at_nb", [128, 8], F32))
        at_sm = E(nc.sbuf_tensor("at_sm", [128, 8], F32))
        at_rs = E(nc.sbuf_tensor("at_rs", [128, 8], F32))
        carry = E(nc.sbuf_tensor("carry", [DH, 1], F32))
        tiny = E(nc.sbuf_tensor("tiny", [1, 8], F32))
        row128 = E(nc.sbuf_tensor("row128", [1, 128], F32))

        pb = [E(nc.psum_tensor(f"pb{i}", [128, 512], F32)) for i in range(8)]

        block = E(nc.Block())

        def barrier(tag):
            tC = c_C.v
            tgp = c_gp.v
            qtoks = [(s, v) for s, v in zip(ldq.sems, ldq.vals) if v > 0] + \
                    [(s, v) for s, v in zip(stq.sems, stq.vals) if v > 0]

            @block.vector
            def _(vector):
                vector.tensor_copy(tiny[0:1, 5:6],
                                   ones_r[0:1, 0:1]).then_inc(s_V, 1)
            c_V.inc()

            @block.scalar
            def _(scalar):
                scalar.copy(tiny[0:1, 6:7], ones_r[0:1, 0:1]).then_inc(s_S, 1)
            c_S.inc()

            bT, bV, bS = c_T.v, c_V.v, c_S.v

            def waits(eng, skip):
                if skip != "T":
                    eng.wait_ge(s_T, bT)
                if skip != "V":
                    eng.wait_ge(s_V, bV)
                if skip != "S":
                    eng.wait_ge(s_S, bS)
                if tC:
                    eng.wait_ge(s_C, tC)
                tokwait(eng, qtoks)
                if tgp:
                    eng.wait_ge(s_gp, tgp)

            @block.tensor
            def _(tensor):
                waits(tensor, "T")

            @block.vector
            def _(vector):
                waits(vector, "V")

            @block.scalar
            def _(scalar):
                waits(scalar, "S")

            @block.sync
            def _(sync):
                waits(sync, None)

            @block.gpsimd
            def _(gpsimd):
                waits(gpsimd, None)

        def ln_emit(src_sb, dst_sb, p):
            pp = slice(p, p + 1)
            I32 = mybir.dt.int32

            @block.vector
            def _(vector):
                vector.drain()
                vector.reduce_sum(st_sum[:, pp], src_sb, axis=AX)
                vector.scalar_tensor_tensor(dst_sb, src_sb, 1.0, src_sb,
                                            ALU.bypass, ALU.mult,
                                            accum_out=st_var[:, pp])
                vector.drain()
                vector.tensor_scalar(st_nmu[:, pp], st_sum[:, pp],
                                     -1.0 / D, None, ALU.mult)
                vector.tensor_scalar(st_rv[:, pp], st_var[:, pp], 1.0 / D, EPS,
                                     ALU.mult, ALU.add)
                vector.drain()
                vector.tensor_mul(nrs[:, 0:1], st_nmu[:, pp], st_nmu[:, pp])
                vector.drain()
                vector.tensor_sub(st_rv[:, pp], st_rv[:, pp], nrs[:, 0:1])
                vector.drain()
                rv_i = st_rv[:, pp].bitcast(I32)
                y_i = st_rstd[:, pp].bitcast(I32)
                vector.tensor_scalar(y_i, rv_i, 1, None, ALU.arith_shift_right)
                vector.drain()
                vector.tensor_scalar(y_i, y_i, 0, None, ALU.bitwise_not)
                vector.drain()
                vector.tensor_scalar(y_i, y_i, 1597463008, None, ALU.add)
                for _i in range(4):
                    vector.drain()
                    vector.tensor_mul(nrs[:, 0:1], st_rstd[:, pp], st_rstd[:, pp])
                    vector.drain()
                    vector.tensor_mul(nrs[:, 0:1], nrs[:, 0:1], st_rv[:, pp])
                    vector.drain()
                    vector.tensor_scalar(nrs[:, 0:1], nrs[:, 0:1], -0.5, 1.5,
                                         ALU.mult, ALU.add)
                    vector.drain()
                    vector.tensor_mul(st_rstd[:, pp], st_rstd[:, pp],
                                      nrs[:, 0:1])
                vector.drain()
                vector.tensor_mul(st_nmr[:, pp], st_nmu[:, pp], st_rstd[:, pp])
                vector.drain()
                vector.tensor_scalar(dst_sb, src_sb, st_rstd[:, pp],
                                     st_nmr[:, pp], ALU.mult,
                                     ALU.add).then_inc(s_V, 1)
            return c_V.inc()

        def tr_group(tensor, bank, srcs, idt=None):
            idt = ident if idt is None else idt
            for i, src in enumerate(srcs):
                tensor.matmul(pb[bank][:, i * 128:(i + 1) * 128],
                              src, idt[:], is_transpose=True,
                              start=(i == 0),
                              stop=(i == len(srcs) - 1)).then_inc(s_T, 1)
                c_T.inc()

        def load_w_chunks(sync, dst_sb, src_ap, nchunk, ck_cols, rows=128):
            return [ldq.dma(sync, dst_sb[:, k * ck_cols:(k + 1) * ck_cols],
                            src_ap[k * rows:(k + 1) * rows, :])
                    for k in range(nchunk)]

        def stage_ready(toks):

            @block.sync
            def _(sync):
                tokwait(sync, toks)
                sync.sem_inc(s_rdy, 1)
            n = c_rdy.inc()

            @block.tensor
            def _(tensor):
                tensor.wait_ge(s_rdy, n)

            @block.vector
            def _(vector):
                vector.wait_ge(s_rdy, n)

            @block.scalar
            def _(scalar):
                scalar.wait_ge(s_rdy, n)

        const_toks = []

        @block.sync
        def _(sync):
            for dst, srcp in ((ident[:], identc), (identb[:], identcb),
                              (arep_sb[:], a_rep), (lnp[:, 0:8], ln1g),
                              (lnp[:, 8:16], ln1b), (lnp[:, 16:24], ln2g),
                              (lnp[:, 24:32], ln2b), (bqk[:, 0:8], bqp),
                              (bqk[:, 8:16], bkp), (clip_sb[:, 0:1], clipx),
                              (clip_sb[:, 1:2], clipd)):
                const_toks.append(ldq.dma(sync, dst, srcp))

        @block.vector
        def _(vector):
            vector.memset(ones_r[:], 1.0).then_inc(s_V, 1)
            vector.memset(ones_b[:], 1.0).then_inc(s_V, 1)
            vector.memset(rmax1[:], 0.0).then_inc(s_V, 1)
            vector.memset(rmax2[:], 0.0).then_inc(s_V, 1)
        c_V.inc(4)
        lab_memset = c_V.v

        s1 = ExitStack()
        wqh = s1.enter_context(nc.sbuf_tensor("wqh", [128, 8 * D], BF16))
        wql = s1.enter_context(nc.sbuf_tensor("wql", [128, 8 * D], BF16))
        wkh = s1.enter_context(nc.sbuf_tensor("wkh", [128, 8 * D], BF16))
        wkl = s1.enter_context(nc.sbuf_tensor("wkl", [128, 8 * D], BF16))
        wvh = s1.enter_context(nc.sbuf_tensor("wvh", [128, 8 * D], BF16))
        wvl = s1.enter_context(nc.sbuf_tensor("wvl", [128, 8 * D], BF16))
        wgh = s1.enter_context(nc.sbuf_tensor("wgh", [128, 8 * D], BF16))
        wgl = s1.enter_context(nc.sbuf_tensor("wgl", [128, 8 * D], BF16))
        bw_sb = s1.enter_context(nc.sbuf_tensor("bw_sb", [128, 8 * DH], F32))
        hth1 = s1.enter_context(nc.sbuf_tensor("h1hi", [128, D], BF16))
        htl1 = s1.enter_context(nc.sbuf_tensor("h1lo", [128, D], BF16))
        brow1h = s1.enter_context(nc.sbuf_tensor("brow1h", [1, D], BF16))
        brow1l = s1.enter_context(nc.sbuf_tensor("brow1l", [1, D], BF16))
        mg_sb = s1.enter_context(nc.sbuf_tensor("mg_sb", [128, 256], F32))
        mf_sb = s1.enter_context(nc.sbuf_tensor("mf_sb", [128, 256], F32))
        x_t = s1.enter_context(nc.sbuf_tensor("x_t", [128, D], F32))
        h0_t = s1.enter_context(nc.sbuf_tensor("h0_t", [128, 2 * D], F32))
        ht_t = s1.enter_context(nc.sbuf_tensor("ht_t", [128, D], F32))
        qt_t = s1.enter_context(nc.sbuf_tensor("qt_t", [128, D], F32))
        kt_t = s1.enter_context(nc.sbuf_tensor("kt_t", [128, 2 * D], F32))
        v_t = s1.enter_context(nc.sbuf_tensor("v_t", [128, 2 * D], F32))
        g_t = s1.enter_context(nc.sbuf_tensor("g_t", [128, D], F32))
        ao_t = s1.enter_context(nc.sbuf_tensor("ao_t", [128, D], F32))
        u_sb = s1.enter_context(nc.sbuf_tensor("u_sb", [DH, 2 * 128], F32))
        stt_sb = s1.enter_context(nc.sbuf_tensor("stt_sb", [DH, 2 * 128], F32))
        pp4 = s1.enter_context(nc.sbuf_tensor("pp4", [128, 1024], F32))
        ptt4 = s1.enter_context(nc.sbuf_tensor("ptt4", [128, 1024], F32))
        qk_tok = s1.enter_context(nc.sbuf_tensor("qk_tok", [128, D], F32))
        hc_t = qk_tok
        bvbc = s1.enter_context(nc.sbuf_tensor("bvbc", [128, D], F32))

        s1_toks = list(const_toks)

        @block.sync
        def _(sync):
            for dst, srcp in ((mg_sb[:], maskg), (mf_sb[:], maskf),
                              (qk_tok[0:1, 0:D], bvr), (kt_t[0:1, 0:D], bgr)):
                s1_toks.append(ldq.dma(sync, dst, srcp))
            s1_toks.extend(load_w_chunks(sync, bw_sb, bwp, 8, DH))
        stage_ready(s1_toks)

        @block.vector
        def _(vector):
            vector.tensor_copy(brow1h[:], kt_t[0:1, 0:D])
            vector.drain()
            vector.tensor_sub(brow1l[:], kt_t[0:1, 0:D],
                              brow1h[:]).then_inc(s_V, 1)
        lab_bg = c_V.inc()

        lab_x_ld = [0] * NT
        lab_h0 = [0] * NT
        lab_hc = [0] * NT
        lab_ht = [0] * NT
        lab_hts = [0] * NT
        lab_tr = [0] * NT
        lab_qkevac = [0] * NT
        lab_qtr = [0] * NT
        lab_ktr = [0] * NT
        lab_vevac = [0] * NT
        lab_gevac = [0] * NT
        lab_gfin = [0] * NT
        NCK = NH // 4
        lab_scrd = [[0] * NCK for _ in range(NT)]
        lab_nb = [[0] * NCK for _ in range(NT)]
        lab_att_av = [[0] * NCK for _ in range(NT)]
        lab_scan = [0] * NT
        lab_qkv_mm = [0] * NT
        lab_att_sc = [[0] * NCK for _ in range(NT)]
        lab_att_pt = [[0] * NCK for _ in range(NT)]
        lab_ptevac = [[0] * NCK for _ in range(NT)]
        lab_exp = [[0] * NCK for _ in range(NT)]
        lab_aoev = [[0] * NCK for _ in range(NT)]
        lab_sp1 = [0] * NT


        def emit_xln(tt):
            pq = tt % 2
            dq = slice(pq * D, (pq + 1) * D)

            @block.sync
            def _(sync, tt=tt):
                if tt >= 1:
                    sync.wait_ge(s_V, lab_h0[tt - 1])
                lab_x_ld[tt] = ldq.dma(sync, x_t[:],
                                       x_in[tt * 128:(tt + 1) * 128, :])

            @block.vector
            def _(vector, tt=tt):
                tokwait(vector, lab_x_ld[tt])
                if tt >= 2:
                    tokwait(vector, lab_sp1[tt - 2])
            lab_h0[tt] = ln_emit(x_t[:], h0_t[:, dq], pq)

            @block.vector
            def _(vector, tt=tt):
                if tt >= 1:
                    vector.wait_ge(s_T, lab_ktr[tt - 1])
            lab_hc[tt] = ln_emit(h0_t[:, dq], hc_t[:], pq)

        def emit_tilehead(tt):
            @block.tensor
            def _(tensor, tt=tt):
                tensor.wait_ge(s_V, lab_hc[tt])
                if tt == 0:
                    tensor.wait_ge(s_V, lab_bvbc)
                if tt >= 1:
                    tensor.wait_ge(s_V, lab_ht[tt - 1])
                    tensor.wait_ge(s_V, lab_scrd[tt - 1][2])
                tr_group(tensor, 0, [hc_t[:, i * 128:(i + 1) * 128]
                                     for i in range(4)])
                tr_group(tensor, 1, [hc_t[:, i * 128:(i + 1) * 128]
                                     for i in range(4, 8)])
            lab_tr[tt] = c_T.v

            @block.vector
            def _(vector, tt=tt):
                vector.wait_ge(s_T, lab_tr[tt])
                if tt >= 1:
                    vector.wait_ge(s_T, lab_qkv_mm[tt - 1])
                for c in range(8):
                    bank, off = (0, c * 128) if c < 4 else (1, (c - 4) * 128)
                    vector.tensor_scalar(ht_t[:, c * 128:(c + 1) * 128],
                                         pb[bank][:, off:off + 128],
                                         lnp[:, c:c + 1], lnp[:, 8 + c:9 + c],
                                         ALU.mult, ALU.add).then_inc(s_V, 1)
                    c_V.inc()
            lab_ht[tt] = c_V.v

            @block.vector
            def _(vector, tt=tt):
                vector.drain()
                vector.tensor_copy(hth1[:], ht_t[:])
                vector.drain()
                vector.tensor_sub(htl1[:], ht_t[:], hth1[:]).then_inc(s_V, 1)
            lab_hts[tt] = c_V.inc()

        @block.tensor
        def _(tensor):
            tensor.wait_ge(s_V, lab_memset)
            for half in range(2):
                tensor.matmul(pb[half][:], ones_r[:],
                              qk_tok[0:1, half * 512:half * 512 + 512],
                              start=True, stop=True).then_inc(s_T, 1)
                c_T.inc()
        lab_bvmm = c_T.v

        @block.vector
        def _(vector):
            vector.wait_ge(s_T, lab_bvmm)
            vector.tensor_copy(bvbc[:, 0:512], pb[0][:])
            vector.tensor_copy(bvbc[:, 512:D], pb[1][:]).then_inc(s_V, 1)
        lab_bvbc = c_V.inc()

        emit_xln(0)
        emit_tilehead(0)

        l_wld1 = [None] * 32
        l_wsp1 = [0] * 32
        for k in range(32):
            wi, kk = divmod(k, 8)
            whbm, hsb, lsb = ((wq, wqh, wql), (wk, wkh, wkl),
                              (wv, wvh, wvl), (wg, wgh, wgl))[wi]
            ssl = slice((k % 2) * D, (k % 2) * D + D)

            @block.sync
            def _(sync, k=k, ssl=ssl, whbm=whbm, kk=kk):
                if k == 0:
                    sync.wait_ge(s_V, lab_bg)
                if k >= 2:
                    sync.wait_ge(s_V, l_wsp1[k - 2])
                l_wld1[k] = ldq.dma(sync, kt_t[:, ssl],
                                    whbm[kk * 128:(kk + 1) * 128, :])

            @block.scalar
            def _(scalar, k=k, ssl=ssl, hsb=hsb, kk=kk):
                tokwait(scalar, [l_wld1[k]])
                scalar.activation(hsb[:, kk * D:(kk + 1) * D], kt_t[:, ssl],
                                  ACT.Copy).then_inc(s_S, 1)
            lab_whi1 = c_S.inc()

            @block.vector
            def _(vector, ssl=ssl, hsb=hsb, lsb=lsb, kk=kk, lab_whi1=lab_whi1):
                vector.wait_ge(s_S, lab_whi1)
                vector.tensor_sub(lsb[:, kk * D:(kk + 1) * D], kt_t[:, ssl],
                                  hsb[:, kk * D:(kk + 1) * D]).then_inc(s_V, 1)
            l_wsp1[k] = c_V.inc()

        for t in range(NT):
            p = t % 2
            first = (t % seq_tiles) == 0
            dslc = slice(p * D, (p + 1) * D)
            uslc = slice(p * 128, (p + 1) * 128)

            if t >= 1:
                emit_tilehead(t)

            @block.tensor
            def _(tensor, t=t, p=p):
                tensor.wait_ge(s_V, lab_hts[t])
                if t == 0:
                    tensor.wait_ge(s_V, l_wsp1[7])
                if t >= 1:
                    tensor.wait_ge(s_V, lab_aoev[t - 1][2])
                for half in range(2):
                    oj = half * 512
                    n = 0
                    for kc in range(8):
                        ksl = slice(kc * 128, (kc + 1) * 128)
                        wsl = slice(kc * D + oj, kc * D + oj + 512)
                        for a_sb, w_sb in ((hth1, wqh), (hth1, wql),
                                           (htl1, wqh)):
                            n += 1
                            mm = tensor.matmul(pb[2 + half][:], a_sb[:, ksl],
                                               w_sb[:, wsl], start=(n == 1),
                                               stop=(n == 24))
                    mm.then_inc(s_T, 1)
                    c_T.inc()
            lab_q_mm = c_T.v

            @block.vector
            def _(vector, t=t, lab_q_mm=lab_q_mm):
                vector.wait_ge(s_T, lab_q_mm)
                if t == 0:
                    vector.wait_ge(s_V, lab_bvbc)
                if t >= 1:
                    vector.wait_ge(s_T, lab_ktr[t - 1])
                vector.tensor_copy(qk_tok[:, 0:512], pb[2][:])
                vector.tensor_copy(qk_tok[:, 512:D], pb[3][:]).then_inc(s_V, 1)
            lab_qev = c_V.inc()

            @block.tensor
            def _(tensor, t=t, lab_qev=lab_qev):
                tensor.wait_ge(s_V, lab_qev)
                tr_group(tensor, 2, [qk_tok[:, i * 128:(i + 1) * 128]
                                     for i in range(4)])
                tr_group(tensor, 3, [qk_tok[:, i * 128:(i + 1) * 128]
                                     for i in range(4, 8)])
            lab_qtr[t] = c_T.v

            @block.vector
            def _(vector, t=t):
                vector.wait_ge(s_T, lab_qtr[t])
                if t >= 1:
                    vector.wait_ge(s_T, lab_att_sc[t - 1][NCK - 1])
                for c in range(8):
                    bank, off = (2, c * 128) if c < 4 else (3, (c - 4) * 128)
                    vector.tensor_scalar(qt_t[:, c * 128:(c + 1) * 128],
                                         pb[bank][:, off:off + 128],
                                         bqk[:, c:c + 1], None,
                                         ALU.add).then_inc(s_V, 1)
                    c_V.inc()
            lab_qtev = c_V.v

            @block.tensor
            def _(tensor, t=t):
                if t == 0:
                    tensor.wait_ge(s_V, l_wsp1[15])
                if t >= 1:
                    tensor.wait_ge(s_V, lab_scrd[t - 1][3])
                    tensor.wait_ge(s_V, lab_ptevac[t - 1][3])
                for half in range(2):
                    oj = half * 512
                    n = 0
                    for kc in range(8):
                        ksl = slice(kc * 128, (kc + 1) * 128)
                        wsl = slice(kc * D + oj, kc * D + oj + 512)
                        for a_sb, w_sb in ((hth1, wkh), (hth1, wkl),
                                           (htl1, wkh)):
                            n += 1
                            mm = tensor.matmul(pb[4 + half][:], a_sb[:, ksl],
                                               w_sb[:, wsl], start=(n == 1),
                                               stop=(n == 24))
                    mm.then_inc(s_T, 1)
                    c_T.inc()
            lab_k_mm = c_T.v

            @block.vector
            def _(vector, t=t, lab_k_mm=lab_k_mm):
                vector.wait_ge(s_T, lab_k_mm)
                vector.wait_ge(s_T, lab_qtr[t])
                vector.tensor_copy(qk_tok[:, 0:512], pb[4][:])
                vector.tensor_copy(qk_tok[:, 512:D],
                                   pb[5][:]).then_inc(s_V, 1)
            lab_kev = c_V.inc()

            @block.tensor
            def _(tensor, t=t, lab_kev=lab_kev):
                tensor.wait_ge(s_V, lab_kev)
                tr_group(tensor, 4, [qk_tok[:, i * 128:(i + 1) * 128]
                                     for i in range(4)])
                tr_group(tensor, 5, [qk_tok[:, i * 128:(i + 1) * 128]
                                     for i in range(4, 8)])
            lab_ktr[t] = c_T.v

            @block.vector
            def _(vector, t=t, p=p):
                vector.wait_ge(s_T, lab_ktr[t])
                for c in range(8):
                    bank, off = (4, c * 128) if c < 4 else (5, (c - 4) * 128)
                    vector.tensor_scalar(kt_t[:, p * D + c * 128: p * D + (c + 1) * 128],
                                         pb[bank][:, off:off + 128],
                                         bqk[:, 8 + c:9 + c], None,
                                         ALU.add).then_inc(s_V, 1)
                    c_V.inc()
            lab_qkevac[t] = c_V.v

            @block.tensor
            def _(tensor, t=t):
                if t == 0:
                    tensor.wait_ge(s_V, l_wsp1[23])
                if t >= 1:
                    tensor.wait_ge(s_V, lab_scrd[t - 1][3])
                    tensor.wait_ge(s_V, lab_aoev[t - 1][3])
                for half in range(2):
                    oj = half * 512
                    n = 0
                    for kc in range(8):
                        ksl = slice(kc * 128, (kc + 1) * 128)
                        wsl = slice(kc * D + oj, kc * D + oj + 512)
                        for a_sb, w_sb in ((hth1, wvh), (hth1, wvl),
                                           (htl1, wvh)):
                            n += 1
                            mm = tensor.matmul(pb[6 + half][:], a_sb[:, ksl],
                                               w_sb[:, wsl], start=(n == 1),
                                               stop=(n == 24))
                    mm.then_inc(s_T, 1)
                    c_T.inc()
            lab_v_mm = c_T.v

            @block.vector
            def _(vector, t=t, p=p, lab_v_mm=lab_v_mm):
                vector.wait_ge(s_T, lab_v_mm)
                if t >= 1:
                    vector.wait_ge(s_T, lab_att_av[t - 1][NCK - 1])
                vector.tensor_tensor(v_t[:, p * D:p * D + 512], pb[6][:],
                                     bvbc[:, 0:512], ALU.add)
                vector.tensor_tensor(v_t[:, p * D + 512:(p + 1) * D], pb[7][:],
                                     bvbc[:, 512:D], ALU.add).then_inc(s_V, 1)
            lab_vevac[t] = c_V.inc()

            @block.tensor
            def _(tensor, t=t, lab_qtev=lab_qtev):
                if t == 0:
                    tensor.wait_ge(s_V, l_wsp1[31])
                tensor.wait_ge(s_V, lab_qtev)
                tensor.wait_ge(s_V, lab_qkevac[t])
                for half in range(2):
                    oj = half * 512
                    for kc in range(8):
                        ksl = slice(kc * 128, (kc + 1) * 128)
                        wsl = slice(kc * D + oj, kc * D + oj + 512)
                        for a_sb, w_sb in ((hth1, wgh), (hth1, wgl),
                                           (htl1, wgh)):
                            tensor.matmul(pb[2 + half][:], a_sb[:, ksl],
                                          w_sb[:, wsl], start=(kc == 0 and
                                                               a_sb is hth1 and
                                                               w_sb is wgh),
                                          stop=False)
                    tensor.matmul(pb[2 + half][:], ones_b[:],
                                  brow1h[0:1, oj:oj + 512],
                                  start=False, stop=False)
                    tensor.matmul(pb[2 + half][:], ones_b[:],
                                  brow1l[0:1, oj:oj + 512],
                                  start=False, stop=True).then_inc(s_T, 1)
                    c_T.inc()
                for kc in range(8):
                    mm = tensor.matmul(pb[4][0:DH, 0:128],
                                       bw_sb[:, kc * DH:(kc + 1) * DH],
                                       ht_t[:, kc * 128:(kc + 1) * 128],
                                       start=(kc == 0), stop=(kc == 7))
                mm.then_inc(s_T, 1)
                c_T.inc()
            lab_g_mm = c_T.v - 1
            lab_u_mm = c_T.v
            lab_qkv_mm[t] = c_T.v

            @block.scalar
            def _(scalar, t=t, lab_g_mm=lab_g_mm):
                scalar.wait_ge(s_T, lab_g_mm)
                if t >= 1:
                    tokwait(scalar, lab_sp1[t - 1])
                scalar.activation(g_t[:, 0:512], pb[2][:],
                                  ACT.Exp, scale=-1.0).then_inc(s_S, 1)
                c_S.inc()
                scalar.activation(g_t[:, 512:D], pb[3][:],
                                  ACT.Exp, scale=-1.0).then_inc(s_S, 1)
                c_S.inc()
            lab_gevac[t] = c_S.v

            @block.vector
            def _(vector, t=t, uslc=uslc, first=first, lab_u_mm=lab_u_mm):
                vector.wait_ge(s_T, lab_u_mm)
                vector.wait_ge(s_S, lab_gevac[t])
                if t >= 2:
                    tokwait(vector, lab_sp1[t - 2])
                vector.drain()
                vector.tensor_scalar(g_t[:], g_t[:], 1.0, None, ALU.add)
                vector.drain()
                vector.reciprocal(g_t[:], g_t[:])
                vector.tensor_copy(u_sb[:, uslc], pb[4][0:DH, 0:128]).then_inc(s_V, 1)
                c_V.inc()
                vector.drain()
                if first:
                    vector.tensor_tensor_scan(
                        stt_sb[:, uslc], arep_sb[:], u_sb[:, uslc], 0.0,
                        ALU.mult, ALU.add)
                else:
                    vector.tensor_tensor_scan(
                        stt_sb[:, uslc], arep_sb[:], u_sb[:, uslc], carry[:, 0:1],
                        ALU.mult, ALU.add)
                vector.drain()
                vector.tensor_copy(carry[:, 0:1],
                                   stt_sb[:, uslc.start + 127:uslc.start + 128]
                                   ).then_inc(s_V, 1)
            lab_scan[t] = c_V.inc()

            def att_sc(c):
                cp = c % 2
                bA, bB = (0, 1) if cp == 0 else (5, 6)

                @block.tensor
                def _(tensor, t=t, p=p, c=c, first=first, bA=bA, bB=bB):
                    tensor.wait_ge(s_V, lab_qkevac[t])
                    if c <= 1:
                        tensor.wait_ge(s_V, lab_vevac[t])
                    if c >= 2:
                        tensor.wait_ge(s_V, lab_scrd[t][c - 2])
                    elif t >= 1:
                        tensor.wait_ge(s_V, lab_scrd[t - 1][c + 2])
                    pk_prev = (t - 1) % 2
                    po = (c % 2) * DH
                    for j in range(4):
                        h = (c // 2) * 8 + 2 * j + (c % 2)
                        hc2 = h // 2
                        bank = bA if j < 2 else bB
                        off = (j % 2) * 256
                        q_ap = qt_t[po:po + DH, hc2 * 128:(hc2 + 1) * 128]
                        kcur = kt_t[po:po + DH,
                                    p * D + hc2 * 128: p * D + (hc2 + 1) * 128]
                        kprev = kcur if first else \
                            kt_t[po:po + DH, pk_prev * D + hc2 * 128:
                                 pk_prev * D + (hc2 + 1) * 128]
                        tensor.matmul(pb[bank][:, off:off + 128], q_ap, kprev,
                                      start=(j % 2 == 0), stop=False
                                      ).then_inc(s_T, 1)
                        c_T.inc()
                        tensor.matmul(pb[bank][:, off + 128:off + 256], q_ap,
                                      kcur, start=False, stop=(j % 2 == 1)
                                      ).then_inc(s_T, 1)
                        c_T.inc()
                lab_att_sc[t][c] = c_T.v

            def att_maskred(c):
                cp = c % 2
                bA, bB = (0, 1) if cp == 0 else (5, 6)

                @block.vector
                def _(vector, t=t, c=c, first=first, bA=bA, bB=bB):
                    vector.wait_ge(s_T, lab_att_sc[t][c])
                    if c >= 1:
                        vector.wait_ge(s_T, lab_att_pt[t][c - 1])
                    elif t >= 1:
                        vector.wait_ge(s_T, lab_att_pt[t - 1][NCK - 1])
                    mask = mf_sb if first else mg_sb
                    for j in range(4):
                        bank = bA if j < 2 else bB
                        off = (j % 2) * 256
                        vector.scalar_tensor_tensor(
                            pp4[:, j * 256:(j + 1) * 256],
                            pb[bank][:, off:off + 256], 0.125, mask[:],
                            ALU.mult, ALU.add).then_inc(s_V, 1)
                        c_V.inc()
                    lab_scrd[t][c] = c_V.v
                    vector.drain()
                    for j in range(4):
                        jj = (c % 2) * 4 + j
                        vector.reduce_max(at_nb[:, jj:jj + 1],
                                          pp4[:, j * 256:(j + 1) * 256],
                                          axis=AX, negate=True).then_inc(s_V, 1)
                        c_V.inc()
                    lab_nb[t][c] = c_V.v

            def att_exp(c):
                @block.scalar
                def _(scalar, t=t, c=c):
                    scalar.wait_ge(s_V, lab_nb[t][c])
                    for j in range(4):
                        jj = (c % 2) * 4 + j
                        scalar.activation(pp4[:, j * 256:(j + 1) * 256],
                                          pp4[:, j * 256:(j + 1) * 256],
                                          ACT.Exp, bias=at_nb[:, jj:jj + 1],
                                          scale=1.0,
                                          accum_out=at_sm[:, jj:jj + 1]
                                          ).then_inc(s_S, 1)
                        c_S.inc()
                lab_exp[t][c] = c_S.v

            def att_pt(c):
                cp = c % 2
                bP0, bP1 = (2, 3) if cp == 0 else (7, 4)

                @block.tensor
                def _(tensor, t=t, c=c, bP0=bP0, bP1=bP1):
                    tensor.wait_ge(s_S, lab_exp[t][c])
                    if c >= 2:
                        tensor.wait_ge(s_V, lab_aoev[t][c - 2])
                    elif t >= 1:
                        tensor.wait_ge(s_V, lab_aoev[t - 1][c + 2])
                    tr_group(tensor, bP0, [pp4[:, i * 128:(i + 1) * 128]
                                           for i in range(4)])
                    tr_group(tensor, bP1, [pp4[:, i * 128:(i + 1) * 128]
                                           for i in range(4, 8)])
                lab_att_pt[t][c] = c_T.v

            def att_ptevac(c):
                cp = c % 2
                bP0, bP1 = (2, 3) if cp == 0 else (7, 4)

                @block.vector
                def _(vector, t=t, c=c, cp=cp, bP0=bP0, bP1=bP1):
                    vector.wait_ge(s_T, lab_att_pt[t][c])
                    vector.tensor_copy(ptt4[:, 0:512], pb[bP0][:])
                    vector.tensor_copy(ptt4[:, 512:1024], pb[bP1][:])
                    for j in range(4):
                        jj = cp * 4 + j
                        vector.reciprocal(at_rs[:, jj:jj + 1],
                                          at_sm[:, jj:jj + 1]).then_inc(s_V, 1)
                        c_V.inc()
                    lab_ptevac[t][c] = c_V.v

            def att_av(c):
                cp = c % 2
                bP0 = 2 if cp == 0 else 7

                @block.tensor
                def _(tensor, t=t, p=p, c=c, first=first, bP0=bP0):
                    tensor.wait_ge(s_V, lab_ptevac[t][c])
                    pk_prev = (t - 1) % 2
                    for j in range(4):
                        h = (c // 2) * 8 + 2 * j + (c % 2)
                        vcur = v_t[:, p * D + h * DH: p * D + (h + 1) * DH]
                        vprev = vcur if first else \
                            v_t[:, pk_prev * D + h * DH:
                                pk_prev * D + (h + 1) * DH]
                        tensor.matmul(pb[bP0][:, j * DH:(j + 1) * DH],
                                      ptt4[:, j * 256:j * 256 + 128], vprev,
                                      start=(j == 0), stop=False)
                        tensor.matmul(pb[bP0][:, j * DH:(j + 1) * DH],
                                      ptt4[:, j * 256 + 128:(j + 1) * 256],
                                      vcur, start=False,
                                      stop=(j == 3)).then_inc(s_T, 1)
                        c_T.inc()
                lab_att_av[t][c] = c_T.v

            def att_aoev(c):
                cp = c % 2
                bP0 = 2 if cp == 0 else 7

                @block.vector
                def _(vector, t=t, c=c, cp=cp, bP0=bP0):
                    vector.wait_ge(s_T, lab_att_av[t][c])
                    if c == 0 and t >= 1:
                        tokwait(vector, lab_sp1[t - 1])
                    for j in range(4):
                        h = (c // 2) * 8 + 2 * j + (c % 2)
                        jj = cp * 4 + j
                        vector.tensor_scalar(ao_t[:, h * DH:(h + 1) * DH],
                                             pb[bP0][:, j * DH:(j + 1) * DH],
                                             at_rs[:, jj:jj + 1], None,
                                             ALU.mult).then_inc(s_V, 1)
                        c_V.inc()
                    lab_aoev[t][c] = c_V.v

            for c in range(NCK):
                att_sc(c)
                if c >= 2:
                    att_av(c - 2)
                if c >= 1:
                    att_pt(c - 1)
                att_maskred(c)
                if c >= 1:
                    att_ptevac(c - 1)
                if c >= 2:
                    att_aoev(c - 2)
                att_exp(c)
                if c == 2 and t + 1 < NT:
                    emit_xln(t + 1)
            att_av(NCK - 2)
            att_pt(NCK - 1)
            att_ptevac(NCK - 1)
            att_aoev(NCK - 2)
            att_av(NCK - 1)
            att_aoev(NCK - 1)

            @block.scalar
            def _(scalar, t=t, p=p, dslc=dslc, uslc=uslc):
                scalar.wait_ge(s_V, lab_aoev[t][NCK - 1])
                scalar.wait_ge(s_V, lab_scan[t])
                scalar.wait_ge(s_S, lab_gevac[t])
                lab_sp1[t] = [
                    stq.dma(scalar, sp_h0[t], h0_t[:, dslc]),
                    stq.dma(scalar, sp_g[t], g_t[:]),
                    stq.dma(scalar, sp_ao[t], ao_t[:]),
                    stq.dma(scalar, sp_st[t], stt_sb[:, uslc]),
                ]

        barrier("S1end")
        s1.close()

        s2 = ExitStack()
        woh = s2.enter_context(nc.sbuf_tensor("woh", [128, 8 * D], BF16))
        wol = s2.enter_context(nc.sbuf_tensor("wol", [128, 8 * D], BF16))
        cw_sb = s2.enter_context(nc.sbuf_tensor("cw_sb", [DH, D], F32))
        brow2 = s2.enter_context(nc.sbuf_tensor("brow2", [1, D], F32))
        ao2_t = s2.enter_context(nc.sbuf_tensor("ao2_t", [128, 2 * D], F32))
        g2_t = s2.enter_context(nc.sbuf_tensor("g2_t", [128, 2 * D], F32))
        h02_t = s2.enter_context(nc.sbuf_tensor("h02_t", [128, 2 * D], F32))
        st2_t = s2.enter_context(nc.sbuf_tensor("st2_t", [DH, 2 * 128], F32))
        aot_t = s2.enter_context(nc.sbuf_tensor("aot_t", [128, D], F32))
        aoth = s2.enter_context(nc.sbuf_tensor("aoth", [128, D], BF16))
        aotl = s2.enter_context(nc.sbuf_tensor("aotl", [128, D], BF16))
        at2_t = s2.enter_context(nc.sbuf_tensor("at2_t", [128, 2 * D], F32))
        ssm2_t = s2.enter_context(nc.sbuf_tensor("ssm2_t", [128, 2 * D], F32))
        x1_t = s2.enter_context(nc.sbuf_tensor("x1_t", [128, 2 * D], F32))
        bobc = s2.enter_context(nc.sbuf_tensor("bobc", [128, D], F32))
        h2c_t = s2.enter_context(nc.sbuf_tensor("h2c_t", [128, D], F32))
        h2t_t = s2.enter_context(nc.sbuf_tensor("h2t_t", [128, 2 * D], F32))
        h2hi2 = s2.enter_context(nc.sbuf_tensor("h2hi2", [128, 2 * D], BF16))
        h2lo2 = s2.enter_context(nc.sbuf_tensor("h2lo2", [128, 2 * D], BF16))

        s2_toks = []

        @block.sync
        def _(sync):
            s2_toks.append(ldq.dma(sync, cw_sb[:], cwp))
            s2_toks.append(ldq.dma(sync, brow2[:], bor))
        stage_ready(s2_toks)

        l_wld2 = [None] * 8
        l_wsp2 = [0] * 8
        for k in range(8):
            ssl = slice((k % 2) * D, (k % 2) * D + D)

            @block.sync
            def _(sync, k=k, ssl=ssl):
                if k >= 2:
                    sync.wait_ge(s_V, l_wsp2[k - 2])
                l_wld2[k] = ldq.dma(sync, at2_t[:, ssl],
                                    wo[k * 128:(k + 1) * 128, :])

            @block.scalar
            def _(scalar, k=k, ssl=ssl):
                tokwait(scalar, [l_wld2[k]])
                scalar.activation(woh[:, k * D:(k + 1) * D], at2_t[:, ssl],
                                  ACT.Copy).then_inc(s_S, 1)
            lab_whi2 = c_S.inc()

            @block.vector
            def _(vector, k=k, ssl=ssl, lab_whi2=lab_whi2):
                vector.wait_ge(s_S, lab_whi2)
                vector.tensor_sub(wol[:, k * D:(k + 1) * D], at2_t[:, ssl],
                                  woh[:, k * D:(k + 1) * D]).then_inc(s_V, 1)
            l_wsp2[k] = c_V.inc()

        @block.tensor
        def _(tensor):
            for half in range(2):
                tensor.matmul(pb[half][:], ones_r[:],
                              brow2[0:1, half * 512:half * 512 + 512],
                              start=True, stop=True).then_inc(s_T, 1)
                c_T.inc()
        lab_bomm = c_T.v

        @block.vector
        def _(vector):
            vector.wait_ge(s_T, lab_bomm)
            vector.tensor_copy(bobc[:, 0:512], pb[0][:])
            vector.tensor_copy(bobc[:, 512:D], pb[1][:]).then_inc(s_V, 1)
        lab_bobc = c_V.inc()

        w_stg = s2.enter_context(nc.sbuf_tensor("w_stg", [128, 2 * 2048], F32))
        whi_stg = s2.enter_context(nc.sbuf_tensor("whi_stg", [128, 2 * 2048], BF16))
        wlo_stg = s2.enter_context(nc.sbuf_tensor("wlo_stg", [128, 2 * 2048], BF16))
        l3ld = [None] * 48
        l3hi = [0] * 48
        l3lo = [0] * 48
        l3st = [None] * 48

        def emit_wslab(k):
            kp = k % 2
            if k < 16:
                halfw, kk = divmod(k, 8)
                wsrc = w1[kk * 128:(kk + 1) * 128,
                          halfw * 2048:(halfw + 1) * 2048]
                cols = 2048
                dsth, dstl = sp_w1h[k], sp_w1l[k]
            else:
                halfw, kk = divmod(k - 16, 16)
                wsrc = w2[halfw * 2048 + kk * 128:
                          halfw * 2048 + (kk + 1) * 128, :]
                cols = 1024
                dsth, dstl = sp_w2h[k - 16], sp_w2l[k - 16]
            ssl = slice(kp * 2048, kp * 2048 + cols)

            @block.gpsimd
            def _(gpsimd, k=k, ssl=ssl, wsrc=wsrc):
                if k >= 2:
                    gpsimd.wait_ge(s_V, l3lo[k - 2])
                l3ld[k] = gpq.dma(gpsimd, w_stg[:, ssl], wsrc)

            @block.scalar
            def _(scalar, k=k, ssl=ssl):
                tokwait(scalar, [l3ld[k]])
                if k >= 2:
                    tokwait(scalar, l3st[k - 2])
                scalar.activation(whi_stg[:, ssl], w_stg[:, ssl],
                                  ACT.Copy).then_inc(s_S, 1)
            l3hi[k] = c_S.inc()

            @block.vector
            def _(vector, k=k, ssl=ssl):
                vector.wait_ge(s_S, l3hi[k])
                vector.tensor_sub(wlo_stg[:, ssl], w_stg[:, ssl],
                                  whi_stg[:, ssl]).then_inc(s_V, 1)
            l3lo[k] = c_V.inc()

            @block.scalar
            def _(scalar, k=k, ssl=ssl, dsth=dsth, dstl=dstl):
                scalar.wait_ge(s_V, l3lo[k])
                l3st[k] = [stq.dma(scalar, dsth, whi_stg[:, ssl]),
                           stq.dma(scalar, dstl, wlo_stg[:, ssl])]

        lab2_ld = [0] * NT
        lab2_aot = [0] * NT
        lab2_aos = [0] * NT
        lab2_tr = [0] * NT
        lab2_mm = [0] * NT
        lab2_att = [0] * NT
        lab2_x1 = [0] * NT
        lab2_h2c = [0] * NT
        lab2_h2tr = [0] * NT
        lab2_h2t = [0] * NT
        lab2_h2s = [0] * NT
        lab2_sp = [0] * NT

        for t in range(NT):
            p = t % 2
            dslc = slice(p * D, (p + 1) * D)
            uslc = slice(p * 128, (p + 1) * 128)

            @block.sync
            def _(sync, t=t, dslc=dslc, uslc=uslc):
                if t >= 2:
                    sync.wait_ge(s_T, lab2_tr[t - 2])
                    sync.wait_ge(s_V, lab2_x1[t - 2])
                lab2_ld[t] = [
                    ldq.dma(sync, ao2_t[:, dslc], sp_ao[t]),
                    ldq.dma(sync, g2_t[:, dslc], sp_g[t]),
                    ldq.dma(sync, h02_t[:, dslc], sp_h0[t]),
                    ldq.dma(sync, st2_t[:, uslc], sp_st[t]),
                ]

            for kslab in (2 * t, 2 * t + 1):
                if kslab < 48:
                    emit_wslab(kslab)

            @block.tensor
            def _(tensor, t=t, p=p):
                tokwait(tensor, lab2_ld[t])
                if t == 0:
                    tensor.wait_ge(s_V, lab_bobc)
                if t >= 1:
                    tensor.wait_ge(s_V, lab2_aos[t - 1])
                tr_group(tensor, 0, [ao2_t[:, p * D + i * 128: p * D + (i + 1) * 128]
                                     for i in range(4)])
                tr_group(tensor, 1, [ao2_t[:, p * D + i * 128: p * D + (i + 1) * 128]
                                     for i in range(4, 8)])
            lab2_tr[t] = c_T.v

            @block.scalar
            def _(scalar, t=t):
                scalar.wait_ge(s_T, lab2_tr[t])
                if t >= 1:
                    scalar.wait_ge(s_T, lab2_mm[t - 1])
                for c in range(8):
                    bank, off = (0, c * 128) if c < 4 else (1, (c - 4) * 128)
                    scalar.activation(aot_t[:, c * 128:(c + 1) * 128],
                                      pb[bank][:, off:off + 128],
                                      ACT.Copy).then_inc(s_S, 1)
                    c_S.inc()
            lab2_aot[t] = c_S.v

            @block.vector
            def _(vector, t=t):
                vector.wait_ge(s_S, lab2_aot[t])
                vector.tensor_copy(aoth[:], aot_t[:])
                vector.drain()
                vector.tensor_sub(aotl[:], aot_t[:],
                                  aoth[:]).then_inc(s_V, 1)
            lab2_aos[t] = c_V.inc()

            @block.tensor
            def _(tensor, t=t, uslc=uslc):
                tensor.wait_ge(s_V, lab2_aos[t])
                if t >= 1:
                    tensor.wait_ge(s_V, lab2_att[t - 1])
                for half in range(2):
                    oj = half * 512
                    n = 0
                    for kc in range(8):
                        ksl = slice(kc * 128, (kc + 1) * 128)
                        wsl = slice(kc * D + oj, kc * D + oj + 512)
                        for a_sb, w_sb in ((aoth, woh), (aoth, wol),
                                           (aotl, woh)):
                            n += 1
                            mm = tensor.matmul(pb[2 + half][:], a_sb[:, ksl],
                                               w_sb[:, wsl], start=(n == 1),
                                               stop=(n == 24))
                    mm.then_inc(s_T, 1)
                    c_T.inc()
                for half in range(2):
                    tensor.matmul(pb[4 + half][:], st2_t[:, uslc],
                                  cw_sb[:, half * 512:(half + 1) * 512],
                                  start=True, stop=True).then_inc(s_T, 1)
                    c_T.inc()
            lab2_mm[t] = c_T.v

            @block.vector
            def _(vector, t=t, p=p):
                vector.wait_ge(s_T, lab2_mm[t])
                if t >= 2:
                    tokwait(vector, lab2_sp[t - 2])
                vector.tensor_tensor(at2_t[:, p * D:p * D + 512], pb[2][:],
                                     bobc[:, 0:512], ALU.add)
                vector.tensor_tensor(at2_t[:, p * D + 512:(p + 1) * D], pb[3][:],
                                     bobc[:, 512:D], ALU.add)
                vector.tensor_copy(ssm2_t[:, p * D:p * D + 512], pb[4][:])
                vector.tensor_copy(ssm2_t[:, p * D + 512:(p + 1) * D],
                                   pb[5][:]).then_inc(s_V, 1)
            lab2_att[t] = c_V.inc()

            @block.vector
            def _(vector, t=t, dslc=dslc):
                vector.drain()
                vector.tensor_mul(at2_t[:, dslc], g2_t[:, dslc], at2_t[:, dslc])
                vector.drain()
                vector.tensor_add(x1_t[:, dslc], h02_t[:, dslc], at2_t[:, dslc])
                vector.drain()
                vector.tensor_add(x1_t[:, dslc], x1_t[:, dslc], ssm2_t[:, dslc])
                vector.drain()
                vector.tensor_mul(ssm2_t[:, dslc], g2_t[:, dslc], ssm2_t[:, dslc])
                vector.drain()
                vector.tensor_sub(x1_t[:, dslc], x1_t[:, dslc],
                                  ssm2_t[:, dslc]).then_inc(s_V, 1)
            lab2_x1[t] = c_V.inc()

            @block.vector
            def _(vector, t=t):
                if t >= 1:
                    vector.wait_ge(s_T, lab2_h2tr[t - 1])
            lab2_h2c[t] = ln_emit(x1_t[:, dslc], h2c_t[:], p)

            @block.tensor
            def _(tensor, t=t):
                tensor.wait_ge(s_V, lab2_h2c[t])
                if t >= 1:
                    tensor.wait_ge(s_V, lab2_h2t[t - 1])
                tr_group(tensor, 6, [h2c_t[:, i * 128:(i + 1) * 128]
                                     for i in range(4)])
                tr_group(tensor, 7, [h2c_t[:, i * 128:(i + 1) * 128]
                                     for i in range(4, 8)])
            lab2_h2tr[t] = c_T.v

            @block.vector
            def _(vector, t=t, p=p):
                vector.wait_ge(s_T, lab2_h2tr[t])
                if t >= 2:
                    tokwait(vector, lab2_sp[t - 2])
                for c in range(8):
                    bank, off = (6, c * 128) if c < 4 else (7, (c - 4) * 128)
                    vector.tensor_scalar(h2t_t[:, p * D + c * 128: p * D + (c + 1) * 128],
                                         pb[bank][:, off:off + 128],
                                         lnp[:, 16 + c:17 + c],
                                         lnp[:, 24 + c:25 + c],
                                         ALU.mult, ALU.add).then_inc(s_V, 1)
                    c_V.inc()
            lab2_h2t[t] = c_V.v

            @block.vector
            def _(vector, t=t, dslc=dslc):
                vector.drain()
                vector.tensor_copy(h2hi2[:, dslc], h2t_t[:, dslc])
                vector.drain()
                vector.tensor_sub(h2lo2[:, dslc], h2t_t[:, dslc],
                                  h2hi2[:, dslc]).then_inc(s_V, 1)
            lab2_h2s[t] = c_V.inc()

            @block.scalar
            def _(scalar, t=t, dslc=dslc):
                scalar.wait_ge(s_V, lab2_h2s[t])
                lab2_sp[t] = [
                    stq.dma(scalar, sp_x1[t], x1_t[:, dslc]),
                    stq.dma(scalar, sp_h2hi[t], h2hi2[:, dslc]),
                    stq.dma(scalar, sp_h2lo[t], h2lo2[:, dslc]),
                ]

        for kslab in range(2 * NT, 48):
            emit_wslab(kslab)

        barrier("S2end")
        s2.close()

        def mlp_half(half_idx):
            hx = str(half_idx)
            jbase = half_idx * 2048
            s3 = ExitStack()
            w1h = s3.enter_context(nc.sbuf_tensor("w1h" + hx, [128, 8 * 2048], BF16))
            w1l = s3.enter_context(nc.sbuf_tensor("w1l" + hx, [128, 8 * 2048], BF16))
            w2h = s3.enter_context(nc.sbuf_tensor("w2h" + hx, [128, 16 * D], BF16))
            w2l = s3.enter_context(nc.sbuf_tensor("w2l" + hx, [128, 16 * D], BF16))
            b1bc = s3.enter_context(nc.sbuf_tensor("b1bc" + hx, [128, 2048], F32))
            h2h3 = s3.enter_context(nc.sbuf_tensor("h2h3" + hx, [128, 2 * D], BF16))
            h2l3 = s3.enter_context(nc.sbuf_tensor("h2l3" + hx, [128, 2 * D], BF16))
            hid_t = s3.enter_context(nc.sbuf_tensor("hid_t" + hx, [128, 2048], F32))
            hth = s3.enter_context(nc.sbuf_tensor("hth" + hx, [128, 2048], BF16))
            htl = s3.enter_context(nc.sbuf_tensor("htl" + hx, [128, 2048], BF16))
            pt3_t = s3.enter_context(nc.sbuf_tensor("pt3_t" + hx, [128, 2 * D], F32))
            if half_idx == 1:
                b2bc = s3.enter_context(nc.sbuf_tensor("b2bc", [128, D], F32))
                x1b_t = s3.enter_context(nc.sbuf_tensor("x1b_t", [128, 2 * D], F32))
                xb_t = s3.enter_context(nc.sbuf_tensor("xb_t", [128, 2 * D], F32))
                xm_t = s3.enter_context(nc.sbuf_tensor("xm_t", [128, D], F32))

            tok_b = []

            @block.sync
            def _(sync):
                tok_b.append(ldq.dma(sync, pt3_t[0:1, 0:2048],
                                     b1r[0:1, jbase:jbase + 2048]))
                if half_idx == 1:
                    tok_b.append(ldq.dma(sync, hid_t[0:1, 0:D], b2r))

            @block.tensor
            def _(tensor):
                tokwait(tensor, tok_b)
                for j in range(4):
                    tensor.matmul(pb[j][:], ones_r[:],
                                  pt3_t[0:1, j * 512:(j + 1) * 512],
                                  start=True, stop=True).then_inc(s_T, 1)
                    c_T.inc()
                if half_idx == 1:
                    for half in range(2):
                        tensor.matmul(pb[4 + half][:], ones_r[:],
                                      hid_t[0:1, half * 512:half * 512 + 512],
                                      start=True, stop=True).then_inc(s_T, 1)
                        c_T.inc()
            bmm = c_T.v

            @block.vector
            def _(vector, bmm=bmm):
                vector.wait_ge(s_T, bmm)
                for j in range(4):
                    vector.tensor_copy(b1bc[:, j * 512:(j + 1) * 512], pb[j][:])
                if half_idx == 1:
                    vector.tensor_copy(b2bc[:, 0:512], pb[4][:])
                    vector.tensor_copy(b2bc[:, 512:D], pb[5][:])
                vector.engine_nop().then_inc(s_V, 1)
            lab_bbc = c_V.inc()

            w1toks = []
            w2toks = []

            @block.sync
            def _(sync):
                for k in range(8):
                    w1toks.append(ldq.dma(sync, w1h[:, k * 2048:(k + 1) * 2048],
                                          sp_w1h[half_idx * 8 + k]))
                    w1toks.append(ldq.dma(sync, w1l[:, k * 2048:(k + 1) * 2048],
                                          sp_w1l[half_idx * 8 + k]))
                for k in range(16):
                    w2toks.append(ldq.dma(sync, w2h[:, k * D:(k + 1) * D],
                                          sp_w2h[half_idx * 16 + k]))
                    w2toks.append(ldq.dma(sync, w2l[:, k * D:(k + 1) * D],
                                          sp_w2l[half_idx * 16 + k]))

            l_ld = [0] * NT
            l_hid_mm = [0] * NT
            l_pre = [0] * NT
            l_gelu = [0] * NT
            l_htr = [0] * NT
            l_hte = [0] * NT
            l_w2 = [0] * NT
            l_ev = [0] * NT
            l_sp = [0] * NT
            l_stats = [0] * NT

            def emit_ld3(tt):
                pq = tt % 2
                dq = slice(pq * D, (pq + 1) * D)

                @block.sync
                def _(sync, tt=tt, dq=dq):
                    if tt >= 2:
                        sync.wait_ge(s_T, l_hid_mm[tt - 2])
                        if half_idx == 1:
                            sync.wait_ge(s_V, l_ev[tt - 2])
                            sync.wait_ge(s_V, l_stats[tt - 2])
                    toks = [ldq.dma(sync, h2h3[:, dq], sp_h2hi[tt]),
                            ldq.dma(sync, h2l3[:, dq], sp_h2lo[tt])]
                    if half_idx == 1:
                        toks.append(ldq.dma(sync, x1b_t[:, dq], sp_x1[tt]))
                        toks.append(ldq.dma(sync, xb_t[:, dq],
                                            x_in[tt * 128:(tt + 1) * 128, :]))
                        if tt <= 1:
                            sync.wait_ge(s_T, bmm)
                        toks.append(ldq.dma(sync, pt3_t[:, dq], sp_pt[tt]))
                    l_ld[tt] = toks

            def hid_mms(tensor, tt, js):
                pq = tt % 2
                for j in js:
                    n = 0
                    for kc in range(8):
                        asl = slice(pq * D + kc * 128, pq * D + (kc + 1) * 128)
                        wsl = slice(kc * 2048 + j * 512,
                                    kc * 2048 + (j + 1) * 512)
                        for a_sb, w_sb in ((h2h3, w1h), (h2h3, w1l),
                                           (h2l3, w1h)):
                            n += 1
                            mm = tensor.matmul(pb[j][:], a_sb[:, asl],
                                               w_sb[:, wsl],
                                               start=(n == 1),
                                               stop=(n == 24))
                    mm.then_inc(s_T, 1)
                    c_T.inc()

            def emit_hid01(tt):
                @block.tensor
                def _(tensor, tt=tt):
                    tokwait(tensor, l_ld[tt])
                    if tt == 0:
                        tensor.wait_ge(s_V, lab_bbc)
                        tokwait(tensor, w1toks)
                    else:
                        tensor.wait_ge(s_V, l_pre[tt - 1])
                    hid_mms(tensor, tt, (0, 1))

            emit_ld3(0)
            emit_hid01(0)

            for t in range(NT):
                p = t % 2
                dslc = slice(p * D, (p + 1) * D)

                @block.tensor
                def _(tensor, t=t):
                    if t >= 1:
                        tensor.wait_ge(s_V, l_ev[t - 1])
                    hid_mms(tensor, t, (2, 3))
                l_hid_mm[t] = c_T.v

                @block.vector
                def _(vector, t=t):
                    vector.wait_ge(s_T, l_hid_mm[t])
                    if t >= 1:
                        vector.wait_ge(s_T, l_htr[t - 1])
                    for j in range(4):
                        mm = vector.tensor_tensor(
                            hid_t[:, j * 512:(j + 1) * 512], pb[j][:],
                            b1bc[:, j * 512:(j + 1) * 512], ALU.add)
                    mm.then_inc(s_V, 1)
                l_pre[t] = c_V.inc()

                @block.scalar
                def _(scalar, t=t):
                    scalar.wait_ge(s_V, l_pre[t])
                    for j in range(4):
                        scalar.activation(hid_t[:, j * 512:(j + 1) * 512],
                                          hid_t[:, j * 512:(j + 1) * 512],
                                          ACT.Gelu).then_inc(s_S, 1)
                        c_S.inc()
                l_gelu[t] = c_S.v

                if t + 1 < NT:
                    emit_ld3(t + 1)
                    emit_hid01(t + 1)

                @block.tensor
                def _(tensor, t=t):
                    tensor.wait_ge(s_S, l_gelu[t])
                    if t >= 1:
                        tensor.wait_ge(s_V, l_hte[t - 1])
                    for b in range(4):
                        tr_group(tensor, 4 + b,
                                 [hid_t[:, (b * 4 + i) * 128:(b * 4 + i + 1) * 128]
                                  for i in range(4)])
                l_htr[t] = c_T.v

                @block.vector
                def _(vector, t=t):
                    vector.wait_ge(s_T, l_htr[t])
                    if t >= 1:
                        vector.wait_ge(s_T, l_w2[t - 1])
                    for b in range(4):
                        vector.tensor_copy(hth[:, b * 512:(b + 1) * 512],
                                           pb[4 + b][:])
                    vector.drain()
                    for b in range(4):
                        vector.tensor_sub(htl[:, b * 512:(b + 1) * 512],
                                          pb[4 + b][:],
                                          hth[:, b * 512:(b + 1) * 512])
                    vector.engine_nop().then_inc(s_V, 1)
                l_hte[t] = c_V.inc()

                @block.tensor
                def _(tensor, t=t):
                    tensor.wait_ge(s_V, l_hte[t])
                    if t == 0:
                        tokwait(tensor, w2toks)
                    for half in range(2):
                        oj = half * 512
                        n = 0
                        for ic in range(16):
                            tsl = slice(ic * 128, (ic + 1) * 128)
                            wsl = slice(ic * D + oj, ic * D + oj + 512)
                            for t_sb, w_sb in ((hth, w2h), (hth, w2l),
                                               (htl, w2h)):
                                n += 1
                                mm = tensor.matmul(pb[2 + half][:],
                                                   t_sb[:, tsl],
                                                   w_sb[:, wsl],
                                                   start=(n == 1),
                                                   stop=(n == 48))
                        mm.then_inc(s_T, 1)
                        c_T.inc()
                l_w2[t] = c_T.v

                if half_idx == 0:
                    @block.vector
                    def _(vector, t=t, p=p, dslc=dslc):
                        vector.wait_ge(s_T, l_w2[t])
                        if t >= 2:
                            tokwait(vector, l_sp[t - 2])
                        vector.tensor_copy(pt3_t[:, p * D:p * D + 512], pb[2][:])
                        vector.tensor_copy(pt3_t[:, p * D + 512:(p + 1) * D],
                                           pb[3][:]).then_inc(s_V, 1)
                    l_ev[t] = c_V.inc()

                    @block.scalar
                    def _(scalar, t=t, dslc=dslc):
                        scalar.wait_ge(s_V, l_ev[t])
                        l_sp[t] = stq.dma(scalar, sp_pt[t], pt3_t[:, dslc])
                else:
                    @block.vector
                    def _(vector, t=t, p=p, dslc=dslc):
                        vector.wait_ge(s_T, l_w2[t])
                        if t >= 1:
                            tokwait(vector, l_sp[t - 1])
                        for half in range(2):
                            oj = half * 512
                            osl = slice(oj, oj + 512)
                            vector.tensor_add(xm_t[:, osl], pb[2 + half][:],
                                              b2bc[:, osl])
                            vector.drain()
                            vector.tensor_add(xm_t[:, osl], xm_t[:, osl],
                                              pt3_t[:, p * D + oj:p * D + oj + 512])
                            vector.drain()
                            vector.tensor_add(xm_t[:, osl], xm_t[:, osl],
                                              x1b_t[:, p * D + oj:p * D + oj + 512])
                            vector.drain()
                            vector.tensor_add(xm_t[:, osl], xm_t[:, osl],
                                              xb_t[:, p * D + oj:p * D + oj + 512])
                            vector.drain()
                        vector.engine_nop().then_inc(s_V, 1)
                    l_ev[t] = c_V.inc()

                    @block.vector
                    def _(vector, t=t, p=p, dslc=dslc):
                        pp = slice(p, p + 1)
                        vector.drain()
                        vector.reduce_sum(st_sum[:, pp], xm_t[:], axis=AX)
                        vector.scalar_tensor_tensor(x1b_t[:, dslc], xm_t[:],
                                                    1.0, xm_t[:], ALU.bypass,
                                                    ALU.mult,
                                                    accum_out=st_var[:, pp])
                        vector.reduce_max(st_mx[:, pp], xm_t[:], axis=AX)
                        vector.tensor_reduce(st_mn[:, pp], xm_t[:],
                                             axis=AX, op=ALU.min)
                        vector.drain()
                        vector.tensor_scalar(st_nmu[:, pp], st_sum[:, pp],
                                             -1.0 / D, None, ALU.mult)
                        vector.tensor_scalar(st_rv[:, pp], st_var[:, pp],
                                             1.0 / D, EPS, ALU.mult, ALU.add)
                        vector.drain()
                        vector.tensor_mul(nrs[:, 0:1], st_nmu[:, pp],
                                          st_nmu[:, pp])
                        vector.drain()
                        vector.tensor_sub(st_rv[:, pp], st_rv[:, pp],
                                          nrs[:, 0:1])
                        vector.tensor_add(st_mx[:, pp], st_mx[:, pp],
                                          st_nmu[:, pp])
                        vector.tensor_add(st_mn[:, pp], st_mn[:, pp],
                                          st_nmu[:, pp])
                        vector.drain()
                        vector.reciprocal(st_rv[:, pp], st_rv[:, pp])
                        vector.tensor_scalar(st_mn[:, pp], st_mn[:, pp], -1.0,
                                             None, ALU.mult)
                        vector.drain()
                        vector.tensor_max(st_mx[:, pp], st_mx[:, pp],
                                          st_mn[:, pp])
                        vector.drain()
                        vector.tensor_mul(st_mx[:, pp], st_mx[:, pp],
                                          st_mx[:, pp])
                        vector.drain()
                        vector.tensor_mul(st_mx[:, pp], st_mx[:, pp],
                                          st_rv[:, pp])
                        vector.drain()
                        vector.tensor_max(rmax1[:, 0:1], rmax1[:, 0:1],
                                          st_mx[:, pp]).then_inc(s_V, 1)
                    l_stats[t] = c_V.inc()

                    @block.scalar
                    def _(scalar, t=t):
                        scalar.wait_ge(s_V, l_ev[t])
                        l_sp[t] = stq.dma(scalar, sp_xm[t], xm_t[:])

            barrier(f"S3{half_idx}end")
            s3.close()

        mlp_half(0)
        mlp_half(1)

        def gs_start(rmax_sb, cc_in_t, cc_out_t, gmax_sb, bank):
            @block.tensor
            def _(tensor):
                tensor.transpose(pb[bank][0:1, 0:128], rmax_sb[:, 0:1],
                                 ident[:]).then_inc(s_T, 1)
            tt = c_T.inc()

            @block.vector
            def _(vector):
                vector.wait_ge(s_T, tt)
                vector.reduce_max(tiny[0:1, 0:1], pb[bank][0:1, 0:128],
                                  axis=AX).then_inc(s_V, 1)
            tv = c_V.inc()

            g0 = c_gp.v
            cc0 = c_C.v

            @block.gpsimd
            def _(gpsimd):
                gpsimd.wait_ge(s_V, tv)
                if g0:
                    gpsimd.wait_ge(s_gp, g0)
                gpsimd.dma_start(out=cc_in_t.ap(),
                                 in_=tiny[0:1, 0:1]).then_inc(s_gp, 16)
                gpsimd.wait_ge(s_gp, g0 + 16)
                gpsimd.collective_compute(
                    "AllReduce", ALU.max,
                    replica_groups=[list(range(cc_cores))],
                    ins=[cc_in_t.ap()],
                    outs=[cc_out_t.ap()],
                ).then_inc(s_C, 1)
                gpsimd.wait_ge(s_C, cc0 + 1)
                gpsimd.dma_start(out=gmax_sb[:],
                                 in_=cc_out_t.ap()).then_inc(s_gp, 16)
            c_gp.inc(32)
            c_C.inc(1)
            return c_gp.v

        def gs_finish(g_done, gmax_sb, clip_idx, scale_g, inv_bc, sc_bc,
                      bank, sqrt_first=False):
            @block.scalar
            def _(scalar):
                scalar.wait_ge(s_gp, g_done)
                if sqrt_first:
                    scalar.activation(gmax_sb[:], gmax_sb[:], ACT.Ln)
                    scalar.drain()
                    scalar.activation(gmax_sb[:], gmax_sb[:], ACT.Exp, scale=0.5)
                    scalar.drain()
                scalar.activation(tiny[0:1, 1:2],
                                  clip_sb[0:1, clip_idx:clip_idx + 1],
                                  ACT.Sigmoid).then_inc(s_S, 1)
            ts_ = c_S.inc()

            @block.vector
            def _(vector):
                vector.wait_ge(s_S, ts_)
                vector.drain()
                vector.tensor_mul(tiny[0:1, 2:3], tiny[0:1, 1:2], gmax_sb[:])
                vector.drain()
                vector.tensor_scalar(scale_g[:], tiny[0:1, 2:3], INV127, None,
                                     ALU.mult)
                vector.drain()
                vector.reciprocal(tiny[0:1, 3:4], scale_g[:]).then_inc(s_V, 1)
            tv2 = c_V.inc()

            @block.tensor
            def _(tensor):
                tensor.wait_ge(s_V, tv2)
                tensor.matmul(pb[bank][:, 0:1], ones_r[:], tiny[0:1, 3:4],
                              start=True, stop=True).then_inc(s_T, 1)
            tt2 = c_T.inc()

            @block.vector
            def _(vector):
                vector.wait_ge(s_T, tt2)
                vector.tensor_copy(inv_bc[:], pb[bank][:, 0:1]).then_inc(s_V, 1)
            tv3 = c_V.inc()

            @block.tensor
            def _(tensor):
                tensor.wait_ge(s_V, tv3)
                tensor.matmul(pb[bank][:, 0:1], ones_r[:], scale_g[0:1, 0:1],
                              start=True, stop=True).then_inc(s_T, 1)
            tt3 = c_T.inc()

            @block.vector
            def _(vector):
                vector.wait_ge(s_T, tt3)
                vector.tensor_copy(sc_bc[:], pb[bank][:, 0:1]).then_inc(s_V, 1)
            return c_V.inc()

        gs1_done = gs_start(rmax1, cc1_in, cc1_out, gmax1, bank=2)

        s4 = ExitStack()
        qwg_sb = s4.enter_context(nc.sbuf_tensor("qwg_sb", [128, 8 * DQ], BF16))
        qwu_sb = s4.enter_context(nc.sbuf_tensor("qwu_sb", [128, 8 * DQ], BF16))
        wsl_t = s4.enter_context(nc.sbuf_tensor("wsl_t", [128, 2 * D], F32))
        wql_t = s4.enter_context(nc.sbuf_tensor("wql_t", [128, 2 * D], F32))
        sgbc = s4.enter_context(nc.sbuf_tensor("sgbc", [128, DQ], F32))
        subc = s4.enter_context(nc.sbuf_tensor("subc", [128, DQ], F32))
        xm4_t = s4.enter_context(nc.sbuf_tensor("xm4_t", [128, 2 * D], F32))
        h14_t = s4.enter_context(nc.sbuf_tensor("h14_t", [128, 2 * D], F32))
        xq_t = s4.enter_context(nc.sbuf_tensor("xq_t", [128, D], F32))
        xqt_t = s4.enter_context(nc.sbuf_tensor("xqt_t", [128, 2 * D], BF16))
        gr_t = s4.enter_context(nc.sbuf_tensor("gr_t", [128, 512], F32))
        sg_t = s4.enter_context(nc.sbuf_tensor("sg_t", [128, 512], F32))
        up_t = s4.enter_context(nc.sbuf_tensor("up_t", [128, 512], F32))
        hp_t = s4.enter_context(nc.sbuf_tensor("hp_t", [128, 2 * DQ], F32))
        rmw = s4.enter_context(nc.sbuf_tensor("rmw", [128, 4], F32))

        l4_wld = [[0] * NSLAB_G for _ in range(2)]
        l4_q = [[0] * NSLAB_G for _ in range(2)]
        l4_tr = [[0] * NSLAB_G for _ in range(2)]
        l4_ev = [[0] * NSLAB_G for _ in range(2)]
        l4_row = [[0] * NSLAB_G for _ in range(2)]
        l4_bcv = [[0] * NSLAB_G for _ in range(2)]

        for wi, (wsrc, qdst, sdst) in enumerate(
                ((wgate, qwg_sb, sgbc), (wup, qwu_sb, subc))):
            for s in range(NSLAB_G):
                sp4 = s % 2
                dslc = slice(sp4 * D, (sp4 + 1) * D)
                idx = wi * NSLAB_G + s

                @block.sync
                def _(sync, wi=wi, s=s, dslc=dslc, wsrc=wsrc, idx=idx):
                    if idx == 0:
                        sync.wait_ge(s_rdy, c_rdy.v)
                    if idx >= 2:
                        pi, ps = divmod(idx - 2, NSLAB_G)
                        sync.wait_ge(s_V, l4_q[pi][ps])
                    l4_wld[wi][s] = ldq.dma(sync, wsl_t[:, dslc],
                                            wsrc[s * 128:(s + 1) * 128, :])

                @block.vector
                def _(vector, wi=wi, s=s, dslc=dslc, idx=idx):
                    tokwait(vector, l4_wld[wi][s])
                    if idx >= 2:
                        pi, ps = divmod(idx - 2, NSLAB_G)
                        vector.wait_ge(s_T, l4_tr[pi][ps])
                    vector.reduce_max(rmw[:, 0:1], wsl_t[:, dslc], axis=AX,
                                      apply_absolute_value=True)
                    vector.drain()
                    vector.tensor_scalar(rmw[:, 1:2], rmw[:, 0:1], INV127,
                                         None, ALU.mult)
                    vector.drain()
                    vector.reciprocal(rmw[:, 2:3], rmw[:, 1:2])
                    vector.drain()
                    vector.tensor_scalar(wql_t[:, dslc], wsl_t[:, dslc],
                                         rmw[:, 2:3], MAGIC, ALU.mult, ALU.add)
                    vector.drain()
                    vector.tensor_scalar(wql_t[:, dslc], wql_t[:, dslc],
                                         MAGIC, None,
                                         ALU.subtract).then_inc(s_V, 1)
                l4_q[wi][s] = c_V.inc()

                @block.tensor
                def _(tensor, wi=wi, s=s, sp4=sp4, idx=idx):
                    tensor.wait_ge(s_V, l4_q[wi][s])
                    if idx >= 1:
                        pi, ps = divmod(idx - 1, NSLAB_G)
                        tensor.wait_ge(s_S, l4_ev[pi][ps])
                        tensor.wait_ge(s_V, l4_bcv[pi][ps])
                    tensor.transpose(pb[7][0:1, 0:128], rmw[:, 1:2],
                                     ident[:]).then_inc(s_T, 1)
                    c_T.inc()
                    tr_group(tensor, 0, [wql_t[:, sp4 * D + i * 128:
                                                sp4 * D + (i + 1) * 128]
                                         for i in range(4)])
                    tr_group(tensor, 1, [wql_t[:, sp4 * D + i * 128:
                                                sp4 * D + (i + 1) * 128]
                                         for i in range(4, 8)])
                l4_tr[wi][s] = c_T.v

                @block.vector
                def _(vector, wi=wi, s=s):
                    vector.wait_ge(s_T, l4_tr[wi][s])
                    vector.tensor_copy(row128[0:1, :],
                                       pb[7][0:1, 0:128]).then_inc(s_V, 1)
                l4_row[wi][s] = c_V.inc()

                @block.tensor
                def _(tensor, wi=wi, s=s):
                    tensor.wait_ge(s_V, l4_row[wi][s])
                    tensor.matmul(pb[6][:, 0:128], ones_r[:], row128[0:1, :],
                                  start=True, stop=True).then_inc(s_T, 1)
                bcm = c_T.inc()

                @block.vector
                def _(vector, wi=wi, s=s, sdst=sdst, bcm=bcm):
                    vector.wait_ge(s_T, bcm)
                    vector.tensor_copy(sdst[:, s * 128:(s + 1) * 128],
                                       pb[6][:, 0:128]).then_inc(s_V, 1)
                l4_bcv[wi][s] = c_V.inc()

                @block.scalar
                def _(scalar, wi=wi, s=s, qdst=qdst):
                    scalar.wait_ge(s_T, l4_tr[wi][s])
                    for c in range(8):
                        bank, off = (0, c * 128) if c < 4 else (1, (c - 4) * 128)
                        scalar.activation(
                            qdst[:, c * DQ + s * 128: c * DQ + (s + 1) * 128],
                            pb[bank][:, off:off + 128],
                            ACT.Copy).then_inc(s_S, 1)
                        c_S.inc()
                l4_ev[wi][s] = c_S.v

        fin1 = gs_finish(gs1_done, gmax1, 0, sxg, invsx, scbc1, bank=6,
                         sqrt_first=True)

        @block.vector
        def _(vector):
            vector.wait_ge(s_V, fin1)
            vector.wait_ge(s_V, l4_bcv[1][NSLAB_G - 1])
            vector.drain()
            vector.tensor_scalar(sgbc[:], sgbc[:], scbc1[:, 0:1], None,
                                 ALU.mult)
            vector.drain()
            vector.tensor_scalar(subc[:], subc[:], scbc1[:, 0:1], None,
                                 ALU.mult).then_inc(s_V, 1)
        l4_scaled = c_V.inc()

        l4_ld = [0] * NT
        l4_h1 = [0] * NT
        l4_xq = [0] * NT
        l4_xtr = [0] * NT
        l4_xte = [0] * NT
        l4_mm = [[0] * 6 for _ in range(NT)]
        l4_sig = [[0] * 6 for _ in range(NT)]
        l4_sil = [[0] * 6 for _ in range(NT)]
        l4_sp = [0] * NT

        for t in range(NT):
            p = t % 2
            dslc = slice(p * D, (p + 1) * D)

            @block.sync
            def _(sync, t=t, dslc=dslc):
                if t == 0:
                    sync.wait_ge(s_V, l4_scaled)
                    sync.wait_ge(s_S, l4_ev[1][NSLAB_G - 1])
                if t >= 2:
                    sync.wait_ge(s_V, l4_h1[t - 2])
                lab = ldq.dma(sync, xm4_t[:, dslc], sp_xm[t])
                l4_ld[t] = lab

            @block.vector
            def _(vector, t=t):
                tokwait(vector, l4_ld[t])
                if t >= 2:
                    vector.wait_ge(s_V, l4_xq[t - 2])
            l4_h1[t] = ln_emit(xm4_t[:, dslc], h14_t[:, dslc], p)

            @block.vector
            def _(vector, t=t, dslc=dslc):
                if t >= 1:
                    vector.wait_ge(s_T, l4_xtr[t - 1])
                vector.drain()
                vector.tensor_scalar(xq_t[:], h14_t[:, dslc],
                                     invsx[:, 0:1], MAGIC, ALU.mult, ALU.add)
                vector.drain()
                vector.tensor_scalar(xq_t[:], xq_t[:],
                                     MAGIC - 128.0, MAGIC + 127.0,
                                     ALU.max, ALU.min)
                vector.drain()
                vector.tensor_scalar(xq_t[:], xq_t[:], MAGIC, None,
                                     ALU.subtract).then_inc(s_V, 1)
            l4_xq[t] = c_V.inc()

            @block.tensor
            def _(tensor, t=t):
                tensor.wait_ge(s_V, l4_xq[t])
                if t >= 1:
                    tensor.wait_ge(s_S, l4_xte[t - 1])
                tr_group(tensor, 6, [xq_t[:, i * 128:(i + 1) * 128]
                                     for i in range(4)])
                tr_group(tensor, 7, [xq_t[:, i * 128:(i + 1) * 128]
                                     for i in range(4, 8)])
            l4_xtr[t] = c_T.v

            @block.scalar
            def _(scalar, t=t, p=p):
                scalar.wait_ge(s_T, l4_xtr[t])
                if t >= 1:
                    scalar.wait_ge(s_T, l4_mm[t - 1][5])
                for c in range(8):
                    bank, off = (6, c * 128) if c < 4 else (7, (c - 4) * 128)
                    scalar.activation(
                        xqt_t[:, p * D + c * 128: p * D + (c + 1) * 128],
                        pb[bank][:, off:off + 128], ACT.Copy).then_inc(s_S, 1)
                    c_S.inc()
            l4_xte[t] = c_S.v

            for j in range(6):
                w = min(512, DQ - j * 512)
                gb = (j % 2) * 2
                ub = gb + 1

                @block.tensor
                def _(tensor, t=t, p=p, j=j, w=w, gb=gb, ub=ub):
                    tensor.wait_ge(s_S, l4_xte[t])
                    if j >= 2:
                        tensor.wait_ge(s_V, l4_sil[t][j - 2])
                    elif t >= 1:
                        tensor.wait_ge(s_V, l4_sil[t - 1][4 + j])
                    for (qsb, bank) in ((qwg_sb, gb), (qwu_sb, ub)):
                        for kc in range(8):
                            mm = tensor.matmul(
                                pb[bank][:, 0:w],
                                xqt_t[:, p * D + kc * 128: p * D + (kc + 1) * 128],
                                qsb[:, kc * DQ + j * 512: kc * DQ + j * 512 + w],
                                start=(kc == 0), stop=(kc == 7))
                        mm.then_inc(s_T, 1)
                        c_T.inc()
                l4_mm[t][j] = c_T.v

                @block.vector
                def _(vector, t=t, j=j, w=w, gb=gb, ub=ub):
                    jsl = slice(j * 512, j * 512 + w)
                    vector.wait_ge(s_T, l4_mm[t][j])
                    if t >= 2 and j == 0:
                        tokwait(vector, l4_sp[t - 2])
                    vector.tensor_mul(gr_t[:, 0:w], pb[gb][:, 0:w], sgbc[:, jsl])
                    vector.tensor_mul(up_t[:, 0:w], pb[ub][:, 0:w],
                                      subc[:, jsl]).then_inc(s_V, 1)
                vsc = c_V.inc()

                @block.scalar
                def _(scalar, t=t, j=j, w=w, vsc=vsc):
                    scalar.wait_ge(s_V, vsc)
                    scalar.activation(sg_t[:, 0:w], gr_t[:, 0:w],
                                      ACT.Silu).then_inc(s_S, 1)
                l4_sig[t][j] = c_S.inc()

                @block.vector
                def _(vector, t=t, p=p, j=j, w=w):
                    hsl = slice(p * DQ + j * 512, p * DQ + j * 512 + w)
                    vector.wait_ge(s_S, l4_sig[t][j])
                    vector.drain()
                    vector.tensor_mul(hp_t[:, hsl], sg_t[:, 0:w], up_t[:, 0:w])
                    vector.drain()
                    vector.reduce_max(rmw[:, 3:4], hp_t[:, hsl], axis=AX,
                                      apply_absolute_value=True)
                    vector.drain()
                    vector.tensor_max(rmax2[:, 0:1], rmax2[:, 0:1],
                                      rmw[:, 3:4]).then_inc(s_V, 1)
                l4_sil[t][j] = c_V.inc()

            @block.scalar
            def _(scalar, t=t, p=p):
                scalar.wait_ge(s_V, l4_sil[t][5])
                l4_sp[t] = stq.dma(scalar, sp_hp[t],
                                   hp_t[:, p * DQ:(p + 1) * DQ])

        barrier("S4end")
        s4.close()

        gs2_done = gs_start(rmax2, cc2_in, cc2_out, gmax2, bank=2)

        s5 = ExitStack()
        qwd_sb = s5.enter_context(nc.sbuf_tensor("qwd_sb", [128, NSLAB_G * D], BF16))
        wsl5 = s5.enter_context(nc.sbuf_tensor("wsl5", [128, 2 * DQ], F32))
        wql5 = s5.enter_context(nc.sbuf_tensor("wql5", [128, 2 * DQ], F32))
        sdbc = s5.enter_context(nc.sbuf_tensor("sdbc", [128, D], F32))
        hp5_t = s5.enter_context(nc.sbuf_tensor("hp5_t", [128, 2 * DQ], F32))
        hq5_t = s5.enter_context(nc.sbuf_tensor("hq5_t", [128, 2 * DQ], F32))
        hqt_t = s5.enter_context(nc.sbuf_tensor("hqt_t", [128, 2 * DQ], BF16))
        xm5_t = s5.enter_context(nc.sbuf_tensor("xm5_t", [128, 2 * D], F32))
        o5_t = s5.enter_context(nc.sbuf_tensor("o5_t", [128, 2 * D], F32))
        rmw5 = s5.enter_context(nc.sbuf_tensor("rmw5", [128, 4], F32))

        l5_wld = [0] * 8
        l5_q = [0] * 8
        l5_tr = [0] * 8
        l5_ev = [0] * 8
        l5_row = [0] * 8
        l5_bcv = [0] * 8
        for s in range(8):
            sp5 = s % 2
            qslc = slice(sp5 * DQ, (sp5 + 1) * DQ)

            @block.sync
            def _(sync, s=s, qslc=qslc):
                if s == 0:
                    sync.wait_ge(s_rdy, c_rdy.v)
                if s >= 2:
                    sync.wait_ge(s_V, l5_q[s - 2])
                l5_wld[s] = ldq.dma(sync, wsl5[:, qslc],
                                    wdown[s * 128:(s + 1) * 128, :])

            @block.vector
            def _(vector, s=s, qslc=qslc):
                tokwait(vector, l5_wld[s])
                if s >= 2:
                    vector.wait_ge(s_T, l5_tr[s - 2])
                vector.reduce_max(rmw5[:, 0:1], wsl5[:, qslc], axis=AX,
                                  apply_absolute_value=True)
                vector.drain()
                vector.tensor_scalar(rmw5[:, 1:2], rmw5[:, 0:1], INV127,
                                     None, ALU.mult)
                vector.drain()
                vector.reciprocal(rmw5[:, 2:3], rmw5[:, 1:2])
                vector.drain()
                vector.tensor_scalar(wql5[:, qslc], wsl5[:, qslc],
                                     rmw5[:, 2:3], MAGIC, ALU.mult, ALU.add)
                vector.drain()
                vector.tensor_scalar(wql5[:, qslc], wql5[:, qslc], MAGIC,
                                     None, ALU.subtract).then_inc(s_V, 1)
            l5_q[s] = c_V.inc()

            @block.tensor
            def _(tensor, s=s, sp5=sp5):
                tensor.wait_ge(s_V, l5_q[s])
                if s >= 1:
                    tensor.wait_ge(s_S, l5_ev[s - 1])
                    tensor.wait_ge(s_V, l5_bcv[s - 1])
                tensor.transpose(pb[7][0:1, 0:128], rmw5[:, 1:2],
                                 ident[:]).then_inc(s_T, 1)
                c_T.inc()
                done = 0
                bank_iter = 0
                while done < NSLAB_G:
                    nn_ = min(4, NSLAB_G - done)
                    tr_group(tensor, bank_iter % 6,
                             [wql5[:, sp5 * DQ + (done + i) * 128:
                                   sp5 * DQ + (done + i + 1) * 128]
                              for i in range(nn_)])
                    done += nn_
                    bank_iter += 1
            l5_tr[s] = c_T.v

            @block.vector
            def _(vector, s=s):
                vector.wait_ge(s_T, l5_tr[s])
                vector.tensor_copy(row128[0:1, :],
                                   pb[7][0:1, 0:128]).then_inc(s_V, 1)
            l5_row[s] = c_V.inc()

            @block.tensor
            def _(tensor, s=s):
                tensor.wait_ge(s_V, l5_row[s])
                tensor.matmul(pb[6][:, 0:128], ones_r[:], row128[0:1, :],
                              start=True, stop=True).then_inc(s_T, 1)
            bcm = c_T.inc()

            @block.vector
            def _(vector, s=s, bcm=bcm):
                vector.wait_ge(s_T, bcm)
                vector.tensor_copy(sdbc[:, s * 128:(s + 1) * 128],
                                   pb[6][:, 0:128]).then_inc(s_V, 1)
            l5_bcv[s] = c_V.inc()

            @block.scalar
            def _(scalar, s=s):
                scalar.wait_ge(s_T, l5_tr[s])
                done = 0
                bank_iter = 0
                while done < NSLAB_G:
                    nn_ = min(4, NSLAB_G - done)
                    for i in range(nn_):
                        ic = done + i
                        scalar.activation(
                            qwd_sb[:, ic * D + s * 128: ic * D + (s + 1) * 128],
                            pb[bank_iter % 6][:, i * 128:(i + 1) * 128],
                            ACT.Copy).then_inc(s_S, 1)
                        c_S.inc()
                    done += nn_
                    bank_iter += 1
            l5_ev[s] = c_S.v

        fin2 = gs_finish(gs2_done, gmax2, 1, sdg, invsd, scbc2, bank=6)

        @block.vector
        def _(vector):
            vector.wait_ge(s_V, fin2)
            vector.wait_ge(s_V, l5_bcv[7])
            vector.drain()
            vector.tensor_scalar(sdbc[:], sdbc[:], scbc2[:, 0:1], None,
                                 ALU.mult).then_inc(s_V, 1)
        l5_scaled = c_V.inc()

        l5_ld = [0] * NT
        l5_hq = [0] * NT
        l5_xtr = [0] * NT
        l5_xte = [0] * NT
        l5_mm = [0] * NT
        l5_out = [0] * NT
        l5_sp = [0] * NT

        for t in range(NT):
            p = t % 2
            dslc = slice(p * D, (p + 1) * D)
            qslc = slice(p * DQ, (p + 1) * DQ)

            def emit_ldq5(tt):
                pq = tt % 2
                dq = slice(pq * D, (pq + 1) * D)
                qq = slice(pq * DQ, (pq + 1) * DQ)

                @block.sync
                def _(sync, tt=tt, dq=dq, qq=qq):
                    if tt == 0:
                        sync.wait_ge(s_V, l5_scaled)
                        sync.wait_ge(s_S, l5_ev[7])
                    if tt >= 2:
                        sync.wait_ge(s_V, l5_hq[tt - 2])
                        sync.wait_ge(s_V, l5_out[tt - 2])
                    l5_ld[tt] = [
                        ldq.dma(sync, hp5_t[:, qq], sp_hp[tt]),
                        ldq.dma(sync, xm5_t[:, dq], sp_xm[tt]),
                    ]

                @block.vector
                def _(vector, tt=tt, qq=qq):
                    tokwait(vector, l5_ld[tt])
                    if tt >= 2:
                        vector.wait_ge(s_T, l5_xtr[tt - 2])
                    vector.tensor_scalar(hq5_t[:, qq], hp5_t[:, qq],
                                         invsd[:, 0:1], MAGIC,
                                         ALU.mult, ALU.add)
                    vector.drain()
                    vector.tensor_scalar(hq5_t[:, qq], hq5_t[:, qq],
                                         MAGIC - 128.0, MAGIC + 127.0,
                                         ALU.max, ALU.min)
                    vector.drain()
                    vector.tensor_scalar(hq5_t[:, qq], hq5_t[:, qq], MAGIC,
                                         None, ALU.subtract).then_inc(s_V, 1)
                l5_hq[tt] = c_V.inc()

            if t == 0:
                emit_ldq5(0)

            @block.tensor
            def _(tensor, t=t, p=p):
                tensor.wait_ge(s_V, l5_hq[t])
                if t >= 1:
                    tensor.wait_ge(s_S, l5_xte[t - 1])
                done = 0
                bank_iter = 0
                while done < NSLAB_G:
                    nn_ = min(4, NSLAB_G - done)
                    tr_group(tensor, bank_iter % 6,
                             [hq5_t[:, p * DQ + (done + i) * 128:
                                    p * DQ + (done + i + 1) * 128]
                              for i in range(nn_)])
                    done += nn_
                    bank_iter += 1
            l5_xtr[t] = c_T.v

            if t + 1 < NT:
                emit_ldq5(t + 1)

            @block.scalar
            def _(scalar, t=t, p=p):
                scalar.wait_ge(s_T, l5_xtr[t])
                if t >= 1:
                    scalar.wait_ge(s_T, l5_mm[t - 1])
                done = 0
                bank_iter = 0
                while done < NSLAB_G:
                    nn_ = min(4, NSLAB_G - done)
                    for i in range(nn_):
                        ic = done + i
                        scalar.activation(
                            hqt_t[:, p * DQ + ic * 128: p * DQ + (ic + 1) * 128],
                            pb[bank_iter % 6][:, i * 128:(i + 1) * 128],
                            ACT.Copy).then_inc(s_S, 1)
                        c_S.inc()
                    done += nn_
                    bank_iter += 1
            l5_xte[t] = c_S.v

            @block.tensor
            def _(tensor, t=t, p=p):
                tensor.wait_ge(s_S, l5_xte[t])
                if t >= 1:
                    tensor.wait_ge(s_V, l5_out[t - 1])
                for half in range(2):
                    oj = half * 512
                    for ic in range(NSLAB_G):
                        mm = tensor.matmul(
                            pb[6 + half][:],
                            hqt_t[:, p * DQ + ic * 128: p * DQ + (ic + 1) * 128],
                            qwd_sb[:, ic * D + oj: ic * D + oj + 512],
                            start=(ic == 0), stop=(ic == NSLAB_G - 1))
                    mm.then_inc(s_T, 1)
                    c_T.inc()
            l5_mm[t] = c_T.v

            @block.vector
            def _(vector, t=t, p=p):
                vector.wait_ge(s_T, l5_mm[t])
                if t >= 2:
                    tokwait(vector, l5_sp[t - 2])
                for half in range(2):
                    oj = half * 512
                    sl = slice(p * D + oj, p * D + oj + 512)
                    vector.tensor_mul(o5_t[:, sl], pb[6 + half][:],
                                      sdbc[:, oj:oj + 512])
                    vector.drain()
                    vector.tensor_add(o5_t[:, sl], o5_t[:, sl], xm5_t[:, sl])
                    vector.drain()
                vector.engine_nop().then_inc(s_V, 1)
            l5_out[t] = c_V.inc()

            @block.scalar
            def _(scalar, t=t, dslc=dslc):
                scalar.wait_ge(s_V, l5_out[t])
                l5_sp[t] = stq.dma(scalar, out[t * 128:(t + 1) * 128, :],
                                   o5_t[:, dslc])

        fin_toks = [(s, v) for s, v in zip(stq.sems, stq.vals) if v > 0]

        @block.vector
        def _(vector):
            tokwait(vector, fin_toks)

        s5.close()

    return nc


def _chunk_vec(v):
    return np.ascontiguousarray(v.reshape(8, 128).T)


def _eye_bf16():
    import ml_dtypes
    return np.eye(128).astype(ml_dtypes.bfloat16)


def _make_masks():
    q = np.arange(128)[:, None]
    k = np.arange(256)[None, :]
    allow_gen = np.where(k < 128, k >= q + 1, (k - 128) <= q)
    allow_first = np.where(k < 128, False, (k - 128) <= q)
    mg = np.where(allow_gen, 0.0, NEGINF).astype(np.float32)
    mf = np.where(allow_first, 0.0, NEGINF).astype(np.float32)
    return mg, mf


def kernel(L0_attn_x, L0_attn_ln1_g, L0_attn_ln1_b, L0_attn_WQ, L0_attn_bQ,
           L0_attn_WK, L0_attn_bK, L0_attn_WV, L0_attn_bV, L0_attn_WO,
           L0_attn_bO, L0_attn_Wg, L0_attn_bg, L0_attn_A, L0_attn_Bw,
           L0_attn_Cw, L0_attn_ln2_g, L0_attn_ln2_b, L0_attn_W1, L0_attn_b1,
           L0_attn_W2, L0_attn_b2, L0_ffn_clip_x, L0_ffn_clip_down,
           L0_ffn_w_gate, L0_ffn_w_up, L0_ffn_w_down):
    x = np.ascontiguousarray(np.asarray(L0_attn_x, dtype=np.float32))
    B, T, d = x.shape
    n_cores = 8
    bs = B // n_cores

    mg, mf = _make_masks()
    common = {
        "wq": np.ascontiguousarray(L0_attn_WQ, np.float32),
        "wk": np.ascontiguousarray(L0_attn_WK, np.float32),
        "wv": np.ascontiguousarray(L0_attn_WV, np.float32),
        "wo": np.ascontiguousarray(L0_attn_WO, np.float32),
        "wg": np.ascontiguousarray(L0_attn_Wg, np.float32),
        "w1": np.ascontiguousarray(L0_attn_W1, np.float32),
        "w2": np.ascontiguousarray(L0_attn_W2, np.float32),
        "bwp": np.ascontiguousarray(L0_attn_Bw, np.float32),
        "cwp": np.ascontiguousarray(L0_attn_Cw, np.float32),
        "ln1g": _chunk_vec(np.asarray(L0_attn_ln1_g, np.float32)),
        "ln1b": _chunk_vec(np.asarray(L0_attn_ln1_b, np.float32)),
        "ln2g": _chunk_vec(np.asarray(L0_attn_ln2_g, np.float32)),
        "ln2b": _chunk_vec(np.asarray(L0_attn_ln2_b, np.float32)),
        "bqp": _chunk_vec(np.asarray(L0_attn_bQ, np.float32)),
        "bkp": _chunk_vec(np.asarray(L0_attn_bK, np.float32)),
        "bvr": np.asarray(L0_attn_bV, np.float32).reshape(1, -1),
        "bor": np.asarray(L0_attn_bO, np.float32).reshape(1, -1),
        "bgr": np.asarray(L0_attn_bg, np.float32).reshape(1, -1),
        "b1r": np.asarray(L0_attn_b1, np.float32).reshape(1, -1),
        "b2r": np.asarray(L0_attn_b2, np.float32).reshape(1, -1),
        "a_rep": np.ascontiguousarray(
            np.broadcast_to(np.asarray(L0_attn_A, np.float32)[:, None],
                            (DH, 128))),
        "wgate": np.ascontiguousarray(L0_ffn_w_gate, np.float32),
        "wup": np.ascontiguousarray(L0_ffn_w_up, np.float32),
        "wdown": np.ascontiguousarray(L0_ffn_w_down, np.float32),
        "clipx": np.asarray(L0_ffn_clip_x, np.float32).reshape(1, 1),
        "clipd": np.asarray(L0_ffn_clip_down, np.float32).reshape(1, 1),
        "identc": np.eye(128, dtype=np.float32),
        "identcb": _eye_bf16(),
        "maskg": mg,
        "maskf": mf,
    }

    nc = build_program(n_tiles=(bs * T) // 128, seq_tiles=T // 128,
                       cc_cores=n_cores)
    in_maps = []
    for c in range(n_cores):
        m = dict(common)
        m["x"] = np.ascontiguousarray(
            x[c * bs:(c + 1) * bs].reshape(bs * T, d))
        in_maps.append(m)
    res = run_bass_kernel_spmd(nc, in_maps, core_ids=list(range(n_cores)))
    outs = [res.results[c]["out"].reshape(bs, T, d) for c in range(n_cores)]
    return np.concatenate(outs, axis=0)
